# version 07
# baseline (speedup 1.0000x reference)
import numpy as np
import ml_dtypes
from contextlib import ExitStack

import concourse.bass as bass
import concourse.mybir as mybir
from concourse.bass_utils import run_bass_kernel_spmd

F32 = mybir.dt.float32
BF16 = mybir.dt.bfloat16
AF = mybir.ActivationFunctionType
ALU = mybir.AluOpType
X = mybir.AxisListType.X

T = 2048
D = 1024
HN = 16
HS = 64
DR = 32
HD = HS + DR
DFF = 3072
NT = T // 128
ND = D // 128
NI = DFF // 128
NQ = HN * HD

MAGIC = float(np.float32(12582912.0))
INV7 = float(np.float32(1.0) / np.float32(7.0))
INV32767 = float(np.float32(1.0) / np.float32(32767.0))
INV1024 = float(np.float32(1.0) / np.float32(1024.0))
ISQ96 = float(np.float32(1.0) / np.float32(np.sqrt(np.float32(96.0))))
EPS_LN = 1e-5
EPS_Q = 1e-12

PT_OFF = [0] * (NT + 1)
for _ki in range(NT):
    PT_OFF[_ki + 1] = PT_OFF[_ki] + (T - 128 * _ki)
PT_TOT = PT_OFF[NT]


class K:
    def __init__(self, block, sems):
        self.block = block
        self.sems = sems
        self.cnt = {k: 0 for k in sems}

    def s(self, name):
        return self.sems[name]

    def inc(self, instr, name, n=1):
        self.cnt[name] += n
        return instr.then_inc(self.sems[name], n)

    def dma(self, eng, out, in_, name, **kw):
        i = eng.dma_start(out=out, in_=in_, **kw)
        self.cnt[name] += 16
        i.then_inc(self.sems[name], 16)
        return i

    def w(self, eng, *names):
        for name in names:
            eng.wait_ge(self.sems[name], self.cnt[name])

    def gv(self, vector):
        C = self

        class GV:
            def wait_ge(self, *a, **k):
                return vector.wait_ge(*a, **k)

            def __getattr__(self, name):
                fn = getattr(vector, name)

                def wrapped(*a, **k):
                    vector.wait_ge(C.s('vg'), C.cnt['vg'])
                    i = fn(*a, **k)
                    C.inc(i, 'vg')
                    return i
                return wrapped
        return GV()

    def gs(self, scalar):
        C = self

        class GS:
            def wait_ge(self, *a, **k):
                return scalar.wait_ge(*a, **k)

            def __getattr__(self, name):
                fn = getattr(scalar, name)

                def wrapped(*a, **k):
                    scalar.wait_ge(C.s('sc'), C.cnt['sc'])
                    i = fn(*a, **k)
                    C.inc(i, 'sc')
                    return i
                return wrapped
        return GS()


def build_nc(dbg=False):
    nc = bass.Bass("TRN2", target_bir_lowering=False, debug=False,
                   detect_race_conditions=False)

    def _dz(name, shape, dt):
        if dbg:
            return nc.dram_tensor(name, shape, dt, kind="ExternalOutput").ap()
        return nc.dram_tensor(name, shape, dt).ap()

    x_in = nc.dram_tensor("x", [T, D], F32, kind="ExternalInput").ap()
    wdqT = nc.dram_tensor("wdqT", [4096, D], F32, kind="ExternalInput").ap()
    wcatq = nc.dram_tensor("wcatq", [4096, NQ], F32, kind="ExternalInput").ap()
    wdkv = nc.dram_tensor("wdkv", [D, 256], F32, kind="ExternalInput").ap()
    wuk = nc.dram_tensor("wuk", [256, D], F32, kind="ExternalInput").ap()
    wuv = nc.dram_tensor("wuv", [256, D], F32, kind="ExternalInput").ap()
    wkr = nc.dram_tensor("wkr", [D, DR], F32, kind="ExternalInput").ap()
    wo = nc.dram_tensor("wo", [D, D], F32, kind="ExternalInput").ap()
    wg = nc.dram_tensor("wg", [DFF, D], F32, kind="ExternalInput").ap()
    wu = nc.dram_tensor("wu", [DFF, D], F32, kind="ExternalInput").ap()
    wd = nc.dram_tensor("wd", [D, DFF], F32, kind="ExternalInput").ap()
    ropeT_in = nc.dram_tensor("ropeT", [DR, T], F32, kind="ExternalInput").ap()
    ltmask = nc.dram_tensor("ltmask", [128, 128], F32, kind="ExternalInput").ap()
    onescol = nc.dram_tensor("onescol", [128, 32], F32, kind="ExternalInput").ap()
    onesf = nc.dram_tensor("onesf", [128, 64], F32, kind="ExternalInput").ap()
    magicv = nc.dram_tensor("magicv", [128, 2], F32, kind="ExternalInput").ap()
    eye_in = nc.dram_tensor("eye", [128, 128], F32, kind="ExternalInput").ap()
    out = nc.dram_tensor("out", [T, D], F32, kind="ExternalOutput").ap()

    z_x1 = _dz("z_x1", [T, D], F32)
    z_x2 = _dz("z_x2", [T, D], F32)
    z_wgT = _dz("z_wgT", [NI, D, 128], BF16)
    z_wuT = _dz("z_wuT", [NI, D, 128], BF16)
    z_wdT = _dz("z_wdT", [NI, 128, D], BF16)
    z_upraw = _dz("z_upraw", [DFF, T], F32)
    z_graw = _dz("z_graw", [DFF, T], F32)
    z_qu = _dz("z_qu", [DFF, T], mybir.dt.int16)
    z_g2 = _dz("z_g2", [DFF, T], F32)
    z_u = _dz("z_u", [DFF, T], F32)
    z_qT = _dz("z_qT", [NQ, T], BF16)
    z_kT = _dz("z_kT", [NQ, T], BF16)
    z_v = _dz("z_v", [T, D], BF16)
    d_a8 = _dz("d_a8", [128, 8 * T], BF16) if dbg else None
    d_a8f = _dz("d_a8f", [128, 8 * T], BF16) if dbg else None
    d_stat = _dz("d_stat", [128, 64], F32) if dbg else None
    cc_in = nc.dram_tensor("cc_in", [1, 2], F32).ap()
    cc_out = nc.dram_tensor("cc_out", [1, 2], F32).ap()
    z_sc = nc.dram_tensor("z_sc", [1, 16], F32).ap()

    with ExitStack() as ctx:
        def sb(name, shape, dt):
            return ctx.enter_context(nc.sbuf_tensor(name, shape, dt))

        c_rope = sb("c_rope", [128, T], F32)
        c_lt = sb("c_lt", [128, 128], BF16)
        c_eye = sb("c_eye", [128, 128], BF16)
        c_ones32 = sb("c_ones32", [128, 32], BF16)
        c_onesf = sb("c_onesf", [128, 64], F32)
        c_magic = sb("c_magic", [128, 2], F32)
        sval = sb("sval", [1, 16], F32)
        sscale = sb("sscale", [128, 16], F32)
        streeg = sb("streeg", [128, 4], F32)
        s32a = sb("s32a", [128, 32], F32)
        s32b = sb("s32b", [128, 32], F32)
        sacc = sb("sacc", [128, 4], F32)
        sstat = sb("sstat", [128, 4 * NT], F32)
        spad = sb("spad", [128, 16], F32)
        af = sb("af", [128, 16384], F32)
        a8 = sb("a8", [128, 8 * T], BF16)
        wa = sb("wa", [128, 24576], BF16)
        stg = sb("stg", [128, 4096], F32)
        sbq = sb("sbq", [128, 2048], BF16)
        sqh = [sb(f"sqh{i}", [HD, T], BF16) for i in range(2)]
        skh = [sb(f"skh{i}", [HD, T], BF16) for i in range(2)]
        svh = [sb(f"svh{i}", [128, HN * HS], BF16) for i in range(2)]
        skr = sb("skr", [DR, T], BF16)
        sd1 = sb("sd1", [128, 2048], F32)

        pb = [ctx.enter_context(nc.psum_tensor(f"pb{i}", [128, 512], F32))
              for i in range(8)]
        pt6 = pb[6][:, :].bitcast(BF16)
        pt7 = pb[7][:, :].bitcast(BF16)
        pTa = af[:, 0:8704].bitcast(BF16)
        weff = af[:, 0:6144].bitcast(BF16)

        sems = {}
        for name in ["ds", "dg", "tc", "sc", "cc", "vg"]:
            sems[name] = ctx.enter_context(nc.semaphore(name))
        lo = min(s.num for s in sems.values())
        hi = max(s.num for s in sems.values()) + 1
        nc.gpsimd.sem_clear(range(lo, hi))
        nc.all_engine_barrier()
        block = ctx.enter_context(nc.Block())
        C = K(block, sems)

        @block.sync
        def _(sync):
            C.dma(sync, c_rope[0:32, :], ropeT_in, 'ds')
            C.dma(sync, c_rope[64:96, :], ropeT_in, 'ds')
            C.dma(sync, c_onesf[:, :], onesf, 'ds')
            C.dma(sync, c_magic[:, :], magicv, 'ds')

        @block.gpsimd
        def _(gpsimd):
            C.dma(gpsimd, c_lt[:, :], ltmask, 'dg')
            C.dma(gpsimd, c_eye[:, :], eye_in, 'dg')
            C.dma(gpsimd, c_ones32[:, :], onescol, 'dg')
        dg_consts = C.cnt['dg']

        def tree_max(vector, src_col, dst_col):
            vector.tensor_copy(out=streeg[:64, 0:1], in_=sacc[64:128, src_col:src_col + 1])
            vector.tensor_tensor(out=streeg[:64, 1:2], in0=sacc[:64, src_col:src_col + 1],
                                 in1=streeg[:64, 0:1], op=ALU.max)
            vector.tensor_copy(out=streeg[:32, 2:3], in_=streeg[32:64, 1:2])
            vector.tensor_tensor(out=s32a[:32, 0:1], in0=streeg[:32, 1:2],
                                 in1=streeg[:32, 2:3], op=ALU.max)
            vector.transpose(out=s32b[:32, :32], in_=s32a[:32, :32])
            return vector.reduce_max(out=sval[0:1, dst_col:dst_col + 1],
                                     in_=s32b[0:1, :32], axis=X)

        def emit_allreduce():
            vg_ready = C.cnt['vg']

            @block.gpsimd
            def _(gpsimd):
                gpsimd.wait_ge(C.s('vg'), vg_ready)
                C.dma(gpsimd, cc_in, sval[0:1, 12:14], 'dg')
                C.w(gpsimd, 'dg')
                i = gpsimd.collective_compute(
                    "AllReduce", ALU.max,
                    replica_groups=[list(range(8))],
                    ins=[cc_in.opt()],
                    outs=[cc_out.opt()],
                )
                C.inc(i, 'cc')
                gpsimd.wait_ge(C.s('cc'), C.cnt['cc'])
                C.dma(gpsimd, sval[0:1, 10:12], cc_out, 'dg')

        def emit_scales(build, ncols):
            dg_ready = C.cnt['dg']

            @block.vector
            def _(vector):
                vector = C.gv(vector)
                vector.wait_ge(C.s('dg'), dg_ready)
                build(vector)
            vg_here = C.cnt['vg']

            @block.sync
            def _(sync):
                sync.wait_ge(C.s('vg'), vg_here)
                C.dma(sync, z_sc[0:1, 0:ncols], sval[0:1, 0:ncols], 'ds')
                C.w(sync, 'ds')
                C.dma(sync, sscale[:, 0:ncols],
                      z_sc[0:1, 0:ncols].broadcast_to([128, ncols]), 'ds')
                C.dma(sync, spad[:], ltmask[:, 0:16], 'ds')

        def dequant_core(vector, buf, tq, bm, scl, rcp, outv, nblk):
            vector.reduce_max(out=bm.rearrange("p (a b) -> p a b", b=1),
                              in_=buf.rearrange("p (a b) -> p a b", b=32),
                              axis=X, apply_absolute_value=True)
            vector.tensor_scalar(out=scl, in0=bm, scalar1=INV7, scalar2=EPS_Q,
                                 op0=ALU.mult, op1=ALU.max)
            vector.reciprocal(out=rcp, in_=scl)
            vector.tensor_tensor(out=tq.rearrange("p (a b) -> p a b", b=32),
                                 in0=buf.rearrange("p (a b) -> p a b", b=32),
                                 in1=rcp[:, :, None].broadcast_to([128, nblk, 32]),
                                 op=ALU.mult)
            vector.tensor_scalar(out=tq, in0=tq, scalar1=MAGIC, scalar2=MAGIC,
                                 op0=ALU.add, op1=ALU.subtract)
            return vector.tensor_tensor(out=outv.rearrange("p (a b) -> p a b", b=32),
                                        in0=tq.rearrange("p (a b) -> p a b", b=32),
                                        in1=scl[:, :, None].broadcast_to([128, nblk, 32]),
                                        op=ALU.mult)

        def emit_dequant_gateup(src, dstT):
            vg_after = []
            tc_after = []
            sc_after = []
            ds_ld = []
            ds_st = []
            for i in range(NI):
                p = i % 2
                buf = af[:, p * 1024:(p + 1) * 1024]
                tq = af[:, 4096 + p * 1024:4096 + (p + 1) * 1024]
                bm = af[:, 8192 + p * 128:8192 + p * 128 + 32]
                scl = af[:, 8448 + p * 128:8448 + p * 128 + 32]
                rcp = af[:, 8704 + p * 128:8704 + p * 128 + 32]
                obf = sbq[:, p * 1024:(p + 1) * 1024]
                ptv = pt6 if p == 0 else pt7

                @block.sync
                def _(sync):
                    if i >= 2:
                        sync.wait_ge(C.s('vg'), vg_after[i - 2])
                    else:
                        C.w(sync, 'vg')
                    C.dma(sync, buf, src[i * 128:(i + 1) * 128, :], 'ds')
                ds_ld.append(C.cnt['ds'])

                @block.vector
                def _(vector):
                    vector = C.gv(vector)
                    vector.wait_ge(C.s('ds'), ds_ld[i])
                    if i >= 2:
                        vector.wait_ge(C.s('tc'), tc_after[i - 2])
                        vector.wait_ge(C.s('dg'), ds_st[i - 2])
                    else:
                        C.w(vector, 'tc', 'dg')
                    dequant_core(vector, buf, tq, bm, scl, rcp, obf, 32)
                vg_after.append(C.cnt['vg'])

                @block.tensor
                def _(tensor):
                    tensor.wait_ge(C.s('vg'), vg_after[i])
                    if i == 0:
                        tensor.wait_ge(C.s('dg'), dg_consts)
                    if i >= 2:
                        tensor.wait_ge(C.s('sc'), sc_after[i - 2])
                    for j in range(ND):
                        i2 = tensor.transpose(out=ptv[:, j * 128:(j + 1) * 128],
                                              in_=obf[:, j * 128:(j + 1) * 128],
                                              identity=c_eye[:, :])
                    C.inc(i2, 'tc')
                tc_after.append(C.cnt['tc'])

                @block.scalar
                def _(scalar):
                    scalar = C.gs(scalar)
                    scalar.wait_ge(C.s('tc'), tc_after[i])
                    scalar.copy(out=obf[:, :], in_=ptv[:, :])
                sc_after.append(C.cnt['sc'])

                @block.gpsimd
                def _(gpsimd):
                    gpsimd.wait_ge(C.s('sc'), sc_after[i])
                    C.dma(gpsimd, dstT[i].rearrange("(a p) c -> p a c", p=128),
                          obf.rearrange("p (a c) -> p a c", a=ND), 'dg')
                ds_st.append(C.cnt['dg'])

        def emit_dequant_down(src, dstT):
            for r in range(ND):
                buf = af[:, 0:3072]
                tq = af[:, 4096:7168]
                bm = af[:, 8192:8288]
                scl = af[:, 8448:8544]
                rcp = af[:, 8704:8800]
                obf = af[:, 10240:11776].bitcast(BF16)

                @block.sync
                def _(sync):
                    C.w(sync, 'vg', 'tc')
                    C.dma(sync, buf, src[r * 128:(r + 1) * 128, :], 'ds')
                ds_h = C.cnt['ds']

                @block.vector
                def _(vector):
                    vector = C.gv(vector)
                    vector.wait_ge(C.s('ds'), ds_h)
                    dequant_core(vector, buf, tq, bm, scl, rcp, obf, 96)
                vg_h = C.cnt['vg']

                for rnd in range(3):
                    ptv = pt6 if rnd % 2 == 0 else pt7
                    slot = sbq[:, (rnd % 2) * 1024:(rnd % 2 + 1) * 1024]

                    @block.tensor
                    def _(tensor):
                        if rnd == 0:
                            tensor.wait_ge(C.s('vg'), vg_h)
                            tensor.wait_ge(C.s('dg'), dg_consts)
                        tensor.wait_ge(C.s('dg'), C.cnt['dg'])
                        tensor.wait_ge(C.s('sc'), C.cnt['sc'])
                        for jj in range(8):
                            i2 = tensor.transpose(
                                out=ptv[:, jj * 128:(jj + 1) * 128],
                                in_=obf[:, (rnd * 8 + jj) * 128:(rnd * 8 + jj + 1) * 128],
                                identity=c_eye[:, :])
                        C.inc(i2, 'tc')
                    tc_h = C.cnt['tc']

                    @block.scalar
                    def _(scalar):
                        scalar = C.gs(scalar)
                        scalar.wait_ge(C.s('tc'), tc_h)
                        scalar.copy(out=slot, in_=ptv[:, :])
                    sc_h = C.cnt['sc']

                    @block.gpsimd
                    def _(gpsimd):
                        gpsimd.wait_ge(C.s('sc'), sc_h)
                        C.dma(gpsimd,
                              dstT[rnd * 8:(rnd + 1) * 8, :, r * 128:(r + 1) * 128]
                              .rearrange("a p c -> p a c"),
                              slot.rearrange("p (a c) -> p a c", a=8), 'dg')

        emit_dequant_gateup(wu, z_wuT)
        emit_dequant_gateup(wg, z_wgT)
        emit_dequant_down(wd, z_wdT)

        def emit_ln(src):
            ds_after = []
            for t in range(NT):
                xt = af[:, t * 1024:(t + 1) * 1024]

                @block.sync
                def _(sync):
                    if t == 0:
                        C.w(sync, 'vg', 'tc', 'sc', 'ds', 'dg')
                    C.dma(sync, xt, src[t * 128:(t + 1) * 128, :], 'ds')
                ds_after.append(C.cnt['ds'])

            @block.sync
            def _(sync):
                C.dma(sync, spad[:], ltmask[:, 0:16], 'ds')

            for t in range(NT):
                xt = af[:, t * 1024:(t + 1) * 1024]

                @block.vector
                def _(vector):
                    vector = C.gv(vector)
                    vector.wait_ge(C.s('ds'), ds_after[t] + 16)
                    vector.reduce_sum(out=sstat[:, t:t + 1], in_=xt, axis=X)
                    vector.tensor_single_scalar(out=sstat[:, NT + t:NT + t + 1],
                                                in_=sstat[:, t:t + 1],
                                                scalar=-INV1024, op=ALU.mult)
                vg_t = C.cnt['vg']

                @block.scalar
                def _(scalar):
                    scalar = C.gs(scalar)
                    scalar.wait_ge(C.s('vg'), vg_t)
                    scalar.activation(out=stg[:, (t % 2) * 1024:(t % 2 + 1) * 1024],
                                      in_=xt, func=AF.Square,
                                      bias=sstat[:, NT + t:NT + t + 1], scale=1.0,
                                      accum_out=sstat[:, 2 * NT + t:2 * NT + t + 1])
            sc_sq = C.cnt['sc']

            @block.vector
            def _(vector):
                vector = C.gv(vector)
                vector.wait_ge(C.s('sc'), sc_sq)
                vector.tensor_scalar(out=sstat[:, 2 * NT:3 * NT],
                                     in0=sstat[:, 2 * NT:3 * NT],
                                     scalar1=INV1024, scalar2=EPS_LN,
                                     op0=ALU.mult, op1=ALU.add)
            vg_var = C.cnt['vg']

            @block.scalar
            def _(scalar):
                scalar = C.gs(scalar)
                scalar.wait_ge(C.s('vg'), vg_var)
                scalar.sqrt(out=sstat[:, 2 * NT:3 * NT], in_=sstat[:, 2 * NT:3 * NT])
            sc_sqrt = C.cnt['sc']

            @block.vector
            def _(vector):
                vector = C.gv(vector)
                vector.wait_ge(C.s('sc'), sc_sqrt)
                vector.reciprocal(out=sstat[:, 2 * NT:3 * NT], in_=sstat[:, 2 * NT:3 * NT])
                vector.tensor_tensor(out=sstat[:, 3 * NT:4 * NT],
                                     in0=sstat[:, NT:2 * NT],
                                     in1=sstat[:, 2 * NT:3 * NT], op=ALU.mult)
            vg_rstd = C.cnt['vg']

            for t in range(NT):
                xt = af[:, t * 1024:(t + 1) * 1024]

                @block.scalar
                def _(scalar):
                    scalar = C.gs(scalar)
                    if t == 0:
                        scalar.wait_ge(C.s('vg'), vg_rstd)
                    scalar.activation(out=xt, in_=xt, func=AF.Identity,
                                      bias=sstat[:, 3 * NT + t:3 * NT + t + 1],
                                      scale=sstat[:, 2 * NT + t:2 * NT + t + 1])
                sc_t = C.cnt['sc']

                @block.vector
                def _(vector):
                    vector = C.gv(vector)
                    vector.wait_ge(C.s('sc'), sc_t)
                    vector.reduce_max(out=sacc[:, 1:2], in_=xt, axis=X,
                                      apply_absolute_value=True)
                    if t == 0:
                        vector.tensor_copy(out=sacc[:, 0:1], in_=sacc[:, 1:2])
                    else:
                        vector.tensor_tensor(out=sacc[:, 0:1], in0=sacc[:, 0:1],
                                             in1=sacc[:, 1:2], op=ALU.max)

        def emit_tiles_to_a8(quant):
            vg_evac = []
            tc_tr = []
            for t in range(NT):
                p = t % 2
                xt = af[:, t * 1024:(t + 1) * 1024]
                slot = sbq[:, p * 1024:(p + 1) * 1024]
                ptv = pt6 if p == 0 else pt7

                @block.vector
                def _(vector):
                    vector = C.gv(vector)
                    if t == 0:
                        C.w(vector, 'ds')
                    if t >= 2:
                        vector.wait_ge(C.s('tc'), tc_tr[t - 2])
                    if quant:
                        vector.tensor_scalar(out=xt, in0=xt, scalar1=sscale[:, 0:1],
                                             scalar2=MAGIC, op0=ALU.mult, op1=ALU.add)
                        vector.tensor_scalar(out=slot, in0=xt, scalar1=MAGIC,
                                             scalar2=sscale[:, 1:2],
                                             op0=ALU.subtract, op1=ALU.mult)
                    else:
                        vector.tensor_copy(out=slot, in_=xt)
                vg_q = C.cnt['vg']

                @block.tensor
                def _(tensor):
                    tensor.wait_ge(C.s('vg'), vg_q)
                    if t == 0:
                        tensor.wait_ge(C.s('dg'), dg_consts)
                    if t >= 2:
                        tensor.wait_ge(C.s('vg'), vg_evac[t - 2])
                    for j in range(ND):
                        i2 = tensor.transpose(out=ptv[:, j * 128:(j + 1) * 128],
                                              in_=slot[:, j * 128:(j + 1) * 128],
                                              identity=c_eye[:, :])
                    C.inc(i2, 'tc')
                tc_tr.append(C.cnt['tc'])

                @block.vector
                def _(vector):
                    vector = C.gv(vector)
                    vector.wait_ge(C.s('tc'), tc_tr[t])
                    vector.tensor_copy(
                        out=a8.rearrange("p (j c) -> p j c", j=ND)[:, :, t * 128:(t + 1) * 128],
                        in_=ptv.rearrange("p (j c) -> p j c", j=ND))
                vg_evac.append(C.cnt['vg'])

        def emit_ffn(src, dst, tag=1):
            emit_ln(src)

            @block.vector
            def _(vector):
                vector = C.gv(vector)
                tree_max(vector, 0, 12)
                vector.tensor_copy(out=sval[0:1, 13:14], in_=sval[0:1, 12:13])

            emit_allreduce()

            def scales_x(vector):
                vector.tensor_scalar(out=sval[0:1, 1:2], in0=sval[0:1, 10:11],
                                     scalar1=INV32767, scalar2=EPS_Q,
                                     op0=ALU.mult, op1=ALU.max)
                vector.reciprocal(out=sval[0:1, 0:1], in_=sval[0:1, 1:2])
            emit_scales(scales_x, 2)

            emit_tiles_to_a8(quant=True)

            if dbg and tag == 2:
                vg_now2 = C.cnt['vg']

                @block.sync
                def _(sync):
                    sync.wait_ge(C.s('vg'), vg_now2)
                    C.dma(sync, d_a8f, a8[:, :], 'ds')

            def gateup(wT, dst_raw, acc_col):
                vg_evac = []
                tc_mm = []
                ds_ld = []
                ds_st = []
                tc0 = C.cnt['tc']
                dg0 = C.cnt['dg']
                for i in range(NI):
                    p = i % 2
                    wtile = wa[:, p * 1024:(p + 1) * 1024]
                    banks = pb[0:4] if p == 0 else pb[4:8]
                    stgh = stg[:, 0:2048] if p == 0 else stg[:, 2048:4096]

                    @block.sync
                    def _(sync):
                        if i < 2:
                            sync.wait_ge(C.s('dg'), dg0)
                            sync.wait_ge(C.s('tc'), tc0)
                        if i >= 2:
                            sync.wait_ge(C.s('tc'), tc_mm[i - 2])
                        C.dma(sync, wtile.rearrange("p (a c) -> p a c", a=ND),
                              wT[i].rearrange("(a p) c -> p a c", p=128), 'ds')
                    ds_ld.append(C.cnt['ds'])

                    @block.tensor
                    def _(tensor):
                        tensor.wait_ge(C.s('ds'), ds_ld[i] + 16)
                        if i == 0:
                            tensor.wait_ge(C.s('vg'), C.cnt['vg'])
                        if i >= 2:
                            tensor.wait_ge(C.s('vg'), vg_evac[i - 2])
                        for kk in range(ND):
                            for n in range(4):
                                i2 = tensor.matmul(
                                    out=banks[n][:, :],
                                    lhsT=wtile.rearrange("p (a c) -> p a c", a=ND)[:, kk, :],
                                    rhs=a8.rearrange("p (j c) -> p j c", j=ND)[:, kk, n * 512:(n + 1) * 512],
                                    start=(kk == 0), stop=(kk == ND - 1))
                        C.inc(i2, 'tc')
                    tc_mm.append(C.cnt['tc'])

                    @block.vector
                    def _(vector):
                        vector = C.gv(vector)
                        vector.wait_ge(C.s('tc'), tc_mm[i])
                        if i >= 2:
                            vector.wait_ge(C.s('dg'), ds_st[i - 2])
                        else:
                            C.w(vector, 'dg')
                        for n in range(4):
                            vector.tensor_copy(out=stgh[:, n * 512:(n + 1) * 512],
                                               in_=banks[n][:, :])
                        vector.reduce_max(out=sacc[:, 1:2], in_=stgh,
                                          axis=X, apply_absolute_value=True)
                        if i == 0:
                            vector.tensor_copy(out=sacc[:, acc_col:acc_col + 1],
                                               in_=sacc[:, 1:2])
                        else:
                            vector.tensor_tensor(out=sacc[:, acc_col:acc_col + 1],
                                                 in0=sacc[:, acc_col:acc_col + 1],
                                                 in1=sacc[:, 1:2], op=ALU.max)
                    vg_evac.append(C.cnt['vg'])

                    @block.gpsimd
                    def _(gpsimd):
                        gpsimd.wait_ge(C.s('vg'), vg_evac[i])
                        C.dma(gpsimd, dst_raw[i * 128:(i + 1) * 128, :], stgh, 'dg')
                    ds_st.append(C.cnt['dg'])

                @block.sync
                def _(sync):
                    C.dma(sync, spad[:], ltmask[:, 0:16], 'ds')

            gateup(z_wuT, z_upraw, 2)
            gateup(z_wgT, z_graw, 3)

            @block.vector
            def _(vector):
                vector = C.gv(vector)
                tree_max(vector, 2, 12)
                tree_max(vector, 3, 13)

            emit_allreduce()

            def scales_ug(vector):
                vector.tensor_scalar(out=sval[0:1, 1:2], in0=sval[0:1, 10:11],
                                     scalar1=INV32767, scalar2=EPS_Q,
                                     op0=ALU.mult, op1=ALU.max)
                vector.tensor_scalar(out=sval[0:1, 3:4], in0=sval[0:1, 11:12],
                                     scalar1=INV32767, scalar2=EPS_Q,
                                     op0=ALU.mult, op1=ALU.max)
                vector.reciprocal(out=sval[0:1, 0:1], in_=sval[0:1, 1:2])
                vector.reciprocal(out=sval[0:1, 2:3], in_=sval[0:1, 3:4])
                vector.tensor_single_scalar(out=sval[0:1, 4:5], in_=sval[0:1, 3:4],
                                            scalar=float(2.0 ** -16), op=ALU.mult)
            emit_scales(scales_ug, 5)

            NI2 = NI // 2
            vg_b = []
            sc_b = []
            ds_b = []
            dg_b = []
            for i in range(NI2):
                p = i % 2
                BU = af[:, p * 4096:(p + 1) * 4096]
                BG = af[:, 8192 + p * 4096:8192 + (p + 1) * 4096]
                TG = a8[:, p * 8192:(p + 1) * 8192].bitcast(F32)
                SSG = wa[:, p * 8192:(p + 1) * 8192].bitcast(F32)
                T1 = stg[:, 0:4096]

                @block.sync
                def _(sync):
                    if i == 0:
                        C.w(sync, 'vg', 'tc', 'sc', 'dg')
                    if i >= 2:
                        sync.wait_ge(C.s('dg'), dg_b[i - 2])
                    C.dma(sync, BU.rearrange("p (a c) -> p a c", a=2),
                          z_upraw[i * 256:(i + 1) * 256, :].rearrange("(a p) c -> p a c", p=128), 'ds')
                    C.dma(sync, BG.rearrange("p (a c) -> p a c", a=2),
                          z_graw[i * 256:(i + 1) * 256, :].rearrange("(a p) c -> p a c", p=128), 'ds')
                ds_b.append(C.cnt['ds'])
                ds_bm = ds_b[i] + (32 if i < NI2 - 1 else 16)

                @block.scalar
                def _(scalar):
                    scalar = C.gs(scalar)
                    scalar.wait_ge(C.s('ds'), ds_bm)
                    if i >= 2:
                        scalar.wait_ge(C.s('vg'), vg_b[i - 2])
                    scalar.activation(out=T1, in_=BU, func=AF.Identity,
                                      bias=c_magic[:, 0:1], scale=sscale[:, 0:1])
                    scalar.activation(out=BU, in_=T1, func=AF.Identity,
                                      bias=c_magic[:, 1:2], scale=1.0)
                    scalar.activation(out=T1, in_=BG, func=AF.Identity,
                                      bias=c_magic[:, 0:1], scale=sscale[:, 2:3])
                    scalar.activation(out=TG, in_=T1, func=AF.Identity,
                                      bias=c_magic[:, 1:2], scale=1.0)
                    scalar.activation(out=SSG, in_=TG, func=AF.Sigmoid,
                                      bias=0.0, scale=sscale[:, 3:4])
                sc_b.append(C.cnt['sc'])

                @block.vector
                def _(vector):
                    vector = C.gv(vector)
                    vector.wait_ge(C.s('sc'), sc_b[i])
                    vector.tensor_scalar(out=SSG, in0=SSG, scalar1=65536.0,
                                         scalar2=MAGIC, op0=ALU.mult, op1=ALU.add)
                    vector.tensor_scalar(out=SSG, in0=SSG, scalar1=MAGIC,
                                         scalar2=65535.0, op0=ALU.subtract, op1=ALU.min)
                    vector.scalar_tensor_tensor(out=BG, in0=TG, scalar=sscale[:, 4:5],
                                                in1=SSG, op0=ALU.mult, op1=ALU.mult)
                    vector.reduce_max(out=sacc[:, 1:2], in_=BG, axis=X,
                                      apply_absolute_value=True)
                    if i == 0:
                        vector.tensor_copy(out=sacc[:, 2:3], in_=sacc[:, 1:2])
                    else:
                        vector.tensor_tensor(out=sacc[:, 2:3], in0=sacc[:, 2:3],
                                             in1=sacc[:, 1:2], op=ALU.max)
                vg_b.append(C.cnt['vg'])

                @block.gpsimd
                def _(gpsimd):
                    gpsimd.wait_ge(C.s('sc'), sc_b[i])
                    gpsimd.wait_ge(C.s('vg'), vg_b[i])
                    C.dma(gpsimd, z_qu[i * 256:(i + 1) * 256, :].rearrange("(a p) c -> p a c", p=128),
                          BU.rearrange("p (a c) -> p a c", a=2), 'dg')
                    C.dma(gpsimd, z_g2[i * 256:(i + 1) * 256, :].rearrange("(a p) c -> p a c", p=128),
                          BG.rearrange("p (a c) -> p a c", a=2), 'dg')
                dg_b.append(C.cnt['dg'])

            @block.sync
            def _(sync):
                C.dma(sync, spad[:], ltmask[:, 0:16], 'ds')

            @block.vector
            def _(vector):
                vector = C.gv(vector)
                tree_max(vector, 2, 12)
                vector.tensor_copy(out=sval[0:1, 13:14], in_=sval[0:1, 12:13])

            emit_allreduce()

            def scales_g2(vector):
                vector.tensor_scalar(out=sval[0:1, 6:7], in0=sval[0:1, 10:11],
                                     scalar1=INV32767, scalar2=EPS_Q,
                                     op0=ALU.mult, op1=ALU.max)
                vector.reciprocal(out=sval[0:1, 5:6], in_=sval[0:1, 6:7])
            emit_scales(scales_g2, 7)

            vg_c = []
            sc_c = []
            ds_c = []
            dg_c = []
            for i in range(NI2):
                p = i % 2
                BG = af[:, p * 4096:(p + 1) * 4096]
                BU = af[:, 8192 + p * 4096:8192 + (p + 1) * 4096]
                UO = a8[:, p * 8192:(p + 1) * 8192].bitcast(F32)
                QUI = wa[:, p * 4096:(p + 1) * 4096].bitcast(mybir.dt.int16)
                T1 = stg[:, 0:4096]

                @block.sync
                def _(sync):
                    if i == 0:
                        C.w(sync, 'vg', 'sc', 'dg')
                    if i >= 2:
                        sync.wait_ge(C.s('vg'), vg_c[i - 2])
                    C.dma(sync, BG.rearrange("p (a c) -> p a c", a=2),
                          z_g2[i * 256:(i + 1) * 256, :].rearrange("(a p) c -> p a c", p=128), 'ds')
                    C.dma(sync, QUI.rearrange("p (a c) -> p a c", a=2),
                          z_qu[i * 256:(i + 1) * 256, :].rearrange("(a p) c -> p a c", p=128), 'ds')
                ds_c.append(C.cnt['ds'])
                ds_cm = ds_c[i] + (32 if i < NI2 - 1 else 16)

                @block.scalar
                def _(scalar):
                    scalar = C.gs(scalar)
                    scalar.wait_ge(C.s('ds'), ds_cm)
                    scalar.activation(out=T1, in_=BG, func=AF.Identity,
                                      bias=c_magic[:, 0:1], scale=sscale[:, 5:6])
                    scalar.activation(out=BG, in_=T1, func=AF.Identity,
                                      bias=c_magic[:, 1:2], scale=1.0)
                    scalar.mul(out=BG, in_=BG, mul=sscale[:, 6:7])
                sc_c.append(C.cnt['sc'])

                @block.vector
                def _(vector):
                    vector = C.gv(vector)
                    vector.wait_ge(C.s('ds'), ds_cm)
                    if i >= 2:
                        vector.wait_ge(C.s('dg'), dg_c[i - 2])
                    vector.tensor_copy(out=BU, in_=QUI)
                    vector.tensor_single_scalar(out=BU, in_=BU,
                                                scalar=sscale[:, 1:2], op=ALU.mult)
                    vector.wait_ge(C.s('sc'), sc_c[i])
                    vector.tensor_tensor(out=UO, in0=BG, in1=BU, op=ALU.mult)
                    vector.reduce_max(out=sacc[:, 1:2], in_=UO, axis=X,
                                      apply_absolute_value=True)
                    if i == 0:
                        vector.tensor_copy(out=sacc[:, 2:3], in_=sacc[:, 1:2])
                    else:
                        vector.tensor_tensor(out=sacc[:, 2:3], in0=sacc[:, 2:3],
                                             in1=sacc[:, 1:2], op=ALU.max)
                vg_c.append(C.cnt['vg'])

                @block.gpsimd
                def _(gpsimd):
                    gpsimd.wait_ge(C.s('vg'), vg_c[i])
                    C.dma(gpsimd, z_u[i * 256:(i + 1) * 256, :].rearrange("(a p) c -> p a c", p=128),
                          UO.rearrange("p (a c) -> p a c", a=2), 'dg')
                dg_c.append(C.cnt['dg'])

            @block.sync
            def _(sync):
                C.dma(sync, spad[:], ltmask[:, 0:16], 'ds')

            @block.vector
            def _(vector):
                vector = C.gv(vector)
                tree_max(vector, 2, 12)
                vector.tensor_copy(out=sval[0:1, 13:14], in_=sval[0:1, 12:13])

            emit_allreduce()

            def scales_o(vector):
                vector.tensor_scalar(out=sval[0:1, 8:9], in0=sval[0:1, 10:11],
                                     scalar1=INV32767, scalar2=EPS_Q,
                                     op0=ALU.mult, op1=ALU.max)
                vector.reciprocal(out=sval[0:1, 7:8], in_=sval[0:1, 8:9])
                vector.tensor_single_scalar(out=sval[0:1, 8:9], in_=sval[0:1, 8:9],
                                            scalar=0.5, op=ALU.mult)
            emit_scales(scales_o, 9)

            @block.sync
            def _(sync):
                C.w(sync, 'tc', 'vg', 'dg')
                for i in range(NI):
                    C.dma(sync, wa.rearrange("p (a c) -> p a c", a=NI)[:, i, :],
                          z_wdT[i], 'ds')
            ds_wd = C.cnt['ds']

            vg_dn = []
            vg_oq = []
            tc_dn = []
            ds_dn = []
            ds_os = []
            for g in range(8):
                p = g % 2
                banks = pb[0:4] if p == 0 else pb[4:8]
                oarea = a8[:, p * 6144:p * 6144 + 6144]
                xres = stg[:, p * 2048:(p + 1) * 2048]
                uarea = af[:, 4096 + p * 6144:4096 + (p + 1) * 6144]
                outt = af[:, p * 2048:(p + 1) * 2048]

                @block.sync
                def _(sync):
                    if g == 0:
                        C.w(sync, 'dg', 'vg', 'sc')
                    if g >= 2:
                        sync.wait_ge(C.s('vg'), vg_oq[g - 2])
                        sync.wait_ge(C.s('vg'), vg_dn[g - 2])
                        sync.wait_ge(C.s('dg'), ds_os[g - 2])
                    C.dma(sync, uarea.rearrange("p (a c) -> p a c", a=NI),
                          z_u[:, g * 256:(g + 1) * 256].rearrange("(a p) c -> p a c", p=128),
                          'ds')
                    C.dma(sync, xres[:, 0:1024], src[g * 256:g * 256 + 128, :], 'ds')
                    C.dma(sync, xres[:, 1024:2048], src[g * 256 + 128:(g + 1) * 256, :], 'ds')
                ds_dn.append(C.cnt['ds'])
                ds_dm = ds_dn[g] + (48 if g < 7 else 16)

                @block.vector
                def _(vector):
                    vector = C.gv(vector)
                    vector.wait_ge(C.s('ds'), ds_dm)
                    if g >= 2:
                        vector.wait_ge(C.s('tc'), tc_dn[g - 2])
                    vector.tensor_scalar(out=uarea, in0=uarea, scalar1=sscale[:, 7:8],
                                         scalar2=MAGIC, op0=ALU.mult, op1=ALU.add)
                    vector.tensor_scalar(out=oarea, in0=uarea, scalar1=MAGIC,
                                         scalar2=sscale[:, 8:9],
                                         op0=ALU.subtract, op1=ALU.mult)
                vg_oq.append(C.cnt['vg'])

                @block.tensor
                def _(tensor):
                    tensor.wait_ge(C.s('vg'), vg_oq[g])
                    tensor.wait_ge(C.s('ds'), ds_wd)
                    if g >= 2:
                        tensor.wait_ge(C.s('vg'), vg_dn[g - 2])
                    for i in range(NI):
                        for tt in range(2):
                            for n2 in range(2):
                                i2 = tensor.matmul(
                                    out=banks[tt * 2 + n2][:, :],
                                    lhsT=oarea.rearrange("p (a c) -> p a c", a=NI)[:, i, tt * 128:(tt + 1) * 128],
                                    rhs=wa.rearrange("p (a c) -> p a c", a=NI)[:, i, n2 * 512:(n2 + 1) * 512],
                                    start=(i == 0), stop=(i == NI - 1))
                    C.inc(i2, 'tc')
                tc_dn.append(C.cnt['tc'])

                @block.vector
                def _(vector):
                    vector = C.gv(vector)
                    vector.wait_ge(C.s('tc'), tc_dn[g])
                    for tt in range(2):
                        for n2 in range(2):
                            vector.scalar_tensor_tensor(
                                out=outt[:, tt * 1024 + n2 * 512:tt * 1024 + (n2 + 1) * 512],
                                in0=banks[tt * 2 + n2][:, :], scalar=1.0,
                                in1=xres[:, tt * 1024 + n2 * 512:tt * 1024 + (n2 + 1) * 512],
                                op0=ALU.mult, op1=ALU.add)
                vg_dn.append(C.cnt['vg'])

                @block.gpsimd
                def _(gpsimd):
                    gpsimd.wait_ge(C.s('vg'), vg_dn[g])
                    C.dma(gpsimd, dst[g * 256:g * 256 + 128, :], outt[:, 0:1024], 'dg')
                    C.dma(gpsimd, dst[g * 256 + 128:(g + 1) * 256, :], outt[:, 1024:2048], 'dg')
                ds_os.append(C.cnt['dg'])

            @block.sync
            def _(sync):
                C.dma(sync, spad[:], ltmask[:, 0:16], 'ds')

        def emit_attention():
            emit_ln(z_x1)
            emit_tiles_to_a8(quant=False)

            wa_dkv = wa[:, 0:2048]
            wa_uk = wa[:, 2048:4096]
            wa_uv = wa[:, 4096:6144]
            wa_kr = wa[:, 6144:6400]
            wa_ckv = wa[:, 6400:10496]
            wa_wo = wa[:, 10496:18688]
            wa_st = wa[:, 18688:21760]

            @block.gpsimd
            def _(gpsimd):
                C.w(gpsimd, 'tc', 'vg')
                C.dma(gpsimd, wa_dkv.rearrange("p (a c) -> p a c", a=ND),
                      wdkv.rearrange("(a p) c -> p a c", p=128), 'dg')
                C.dma(gpsimd, wa_uk.rearrange("p (a c) -> p a c", a=2),
                      wuk.rearrange("(a p) c -> p a c", p=128), 'dg')
                C.dma(gpsimd, wa_uv.rearrange("p (a c) -> p a c", a=2),
                      wuv.rearrange("(a p) c -> p a c", p=128), 'dg')
                C.dma(gpsimd, wa_kr.rearrange("p (a c) -> p a c", a=ND),
                      wkr.rearrange("(a p) c -> p a c", p=128), 'dg')
                C.dma(gpsimd, wa_wo.rearrange("p (a c) -> p a c", a=ND),
                      wo.rearrange("(a p) c -> p a c", p=128), 'dg')
            dg_w = C.cnt['dg']

            for m2 in range(2):
                @block.tensor
                def _(tensor):
                    tensor.wait_ge(C.s('dg'), dg_w)
                    tensor.wait_ge(C.s('vg'), C.cnt['vg'])
                    for kk in range(ND):
                        for n in range(4):
                            i2 = tensor.matmul(
                                out=pb[n][:, :],
                                lhsT=wa_dkv.rearrange("p (a c) -> p a c", a=ND)[:, kk, m2 * 128:(m2 + 1) * 128],
                                rhs=a8.rearrange("p (j c) -> p j c", j=ND)[:, kk, n * 512:(n + 1) * 512],
                                start=(kk == 0), stop=(kk == ND - 1))
                    C.inc(i2, 'tc')
                tc_h = C.cnt['tc']

                @block.vector
                def _(vector):
                    vector = C.gv(vector)
                    vector.wait_ge(C.s('tc'), tc_h)
                    for n in range(4):
                        vector.tensor_copy(
                            out=wa_ckv[:, m2 * 2048 + n * 512:m2 * 2048 + (n + 1) * 512],
                            in_=pb[n][:, :])

            for n in range(4):
                @block.tensor
                def _(tensor):
                    tensor.wait_ge(C.s('vg'), C.cnt['vg'])
                    for kk in range(ND):
                        i2 = tensor.matmul(
                            out=pb[0][0:32, :],
                            lhsT=wa_kr.rearrange("p (a c) -> p a c", a=ND)[:, kk, :],
                            rhs=a8.rearrange("p (j c) -> p j c", j=ND)[:, kk, n * 512:(n + 1) * 512],
                            start=(kk == 0), stop=(kk == ND - 1))
                    C.inc(i2, 'tc')
                tc_h = C.cnt['tc']

                @block.vector
                def _(vector):
                    vector = C.gv(vector)
                    vector.wait_ge(C.s('tc'), tc_h)
                    vector.tensor_tensor(out=skr[0:32, n * 512:(n + 1) * 512],
                                         in0=pb[0][0:32, :],
                                         in1=c_rope[0:32, n * 512:(n + 1) * 512],
                                         op=ALU.mult)

            vg_v = []
            tc_v = []
            ds_v = []
            for t in range(NT):
                p = t % 2
                slot = sbq[:, p * 1024:(p + 1) * 1024]

                @block.tensor
                def _(tensor):
                    if t >= 2:
                        tensor.wait_ge(C.s('vg'), vg_v[t - 2])
                    for k2 in range(2):
                        for n2 in range(2):
                            i2 = tensor.matmul(
                                out=pb[4 + p * 2 + n2][:, :],
                                lhsT=wa_ckv[:, k2 * 2048 + t * 128:k2 * 2048 + (t + 1) * 128],
                                rhs=wa_uv[:, k2 * 1024 + n2 * 512:k2 * 1024 + (n2 + 1) * 512],
                                start=(k2 == 0), stop=(k2 == 1))
                    C.inc(i2, 'tc')
                tc_v.append(C.cnt['tc'])

                @block.vector
                def _(vector):
                    vector = C.gv(vector)
                    vector.wait_ge(C.s('tc'), tc_v[t])
                    if t >= 2:
                        vector.wait_ge(C.s('ds'), ds_v[t - 2])
                    vector.tensor_copy(out=slot[:, 0:512], in_=pb[4 + p * 2][:, :])
                    vector.tensor_copy(out=slot[:, 512:1024], in_=pb[4 + p * 2 + 1][:, :])
                vg_v.append(C.cnt['vg'])

                @block.sync
                def _(sync):
                    sync.wait_ge(C.s('vg'), vg_v[t])
                    C.dma(sync, z_v[t * 128:(t + 1) * 128, :], slot, 'ds')
                ds_v.append(C.cnt['ds'])

            vg_k = []
            ds_k = []
            for h in range(HN):
                p = h % 2
                ktile = skh[p]

                @block.vector
                def _(vector):
                    vector = C.gv(vector)
                    if h >= 2:
                        vector.wait_ge(C.s('ds'), ds_k[h - 2])
                    vector.tensor_copy(out=ktile[64:96, :], in_=skr[0:32, :])
                vg_kr = C.cnt['vg']

                @block.tensor
                def _(tensor):
                    tensor.wait_ge(C.s('vg'), C.cnt['vg'] if h == 0 else vg_k[h - 1])
                    for n in range(4):
                        for k2 in range(2):
                            i2 = tensor.matmul(
                                out=pb[n][0:64, :],
                                lhsT=wa_uk[:, k2 * 1024 + h * 64:k2 * 1024 + (h + 1) * 64],
                                rhs=wa_ckv[:, k2 * 2048 + n * 512:k2 * 2048 + (n + 1) * 512],
                                start=(k2 == 0), stop=(k2 == 1))
                        if n == 3:
                            C.inc(i2, 'tc')
                tc_k = C.cnt['tc']

                @block.vector
                def _(vector):
                    vector = C.gv(vector)
                    vector.wait_ge(C.s('tc'), tc_k)
                    for n in range(4):
                        vector.tensor_copy(out=ktile[0:64, n * 512:(n + 1) * 512],
                                           in_=pb[n][0:64, :])
                vg_k.append(C.cnt['vg'])

                @block.sync
                def _(sync):
                    sync.wait_ge(C.s('vg'), vg_k[h])
                    C.dma(sync, z_kT[h * HD:(h + 1) * HD, :], ktile[0:HD, :], 'ds')
                ds_k.append(C.cnt['ds'])

            @block.sync
            def _(sync):
                C.w(sync, 'vg', 'tc', 'sc', 'ds')

            dg_we = []
            tc_we = []
            for n3 in range(3):
                for kp in range(32):
                    p = kp % 2
                    lslot = wa_st[:, p * 1024:(p + 1) * 1024]
                    rslot = wa_st[:, 2048 + p * 512:2048 + (p + 1) * 512]

                    @block.gpsimd
                    def _(gpsimd):
                        if len(tc_we) >= 2:
                            gpsimd.wait_ge(C.s('tc'), tc_we[-2])
                        C.dma(gpsimd, lslot.rearrange("p (a c) -> p a c", a=ND),
                              wdqT[kp * 128:(kp + 1) * 128, :].rearrange("p (a c) -> p a c", a=ND),
                              'dg')
                        C.dma(gpsimd, rslot,
                              wcatq[kp * 128:(kp + 1) * 128, n3 * 512:(n3 + 1) * 512],
                              'dg')
                    dg_we.append(C.cnt['dg'])

                    @block.tensor
                    def _(tensor):
                        tensor.wait_ge(C.s('dg'), dg_we[-1])
                        if kp == 0:
                            tensor.wait_ge(C.s('vg'), C.cnt['vg'])
                        for m in range(8):
                            i2 = tensor.matmul(
                                out=pb[m][:, :],
                                lhsT=lslot.rearrange("p (a c) -> p a c", a=ND)[:, m, :],
                                rhs=rslot,
                                start=(kp == 0), stop=(kp == 31))
                        C.inc(i2, 'tc')
                    tc_we.append(C.cnt['tc'])

                @block.vector
                def _(vector):
                    vector = C.gv(vector)
                    vector.wait_ge(C.s('tc'), tc_we[-1])
                    for m in range(8):
                        vector.tensor_copy(
                            out=weff[:, m * NQ + n3 * 512:m * NQ + (n3 + 1) * 512],
                            in_=pb[m][:, :])

            vg_q = []
            ds_q = []
            for half in range(2):
                for n in range(4):
                    @block.tensor
                    def _(tensor):
                        tensor.wait_ge(C.s('vg'), C.cnt['vg'])
                        for kk in range(ND):
                            for h8 in range(8):
                                i2 = tensor.matmul(
                                    out=pb[h8][0:HD, :],
                                    lhsT=weff[:, kk * NQ + (half * 8 + h8) * HD:kk * NQ + (half * 8 + h8 + 1) * HD],
                                    rhs=a8.rearrange("p (j c) -> p j c", j=ND)[:, kk, n * 512:(n + 1) * 512],
                                    start=(kk == 0), stop=(kk == ND - 1))
                        C.inc(i2, 'tc')
                    tc_q = C.cnt['tc']

                    for h8 in range(8):
                        p2 = h8 % 2
                        slot = sbq[:, p2 * 1024:p2 * 1024 + 512]

                        @block.vector
                        def _(vector):
                            vector = C.gv(vector)
                            vector.wait_ge(C.s('tc'), tc_q)
                            if len(ds_q) >= 2:
                                vector.wait_ge(C.s('ds'), ds_q[-2])
                            vector.tensor_copy(out=slot[0:64, :], in_=pb[h8][0:64, :])
                            vector.tensor_tensor(out=slot[64:96, :],
                                                 in0=pb[h8][64:96, :],
                                                 in1=c_rope[64:96, n * 512:(n + 1) * 512],
                                                 op=ALU.mult)
                        vg_q.append(C.cnt['vg'])

                        @block.sync
                        def _(sync):
                            sync.wait_ge(C.s('vg'), vg_q[-1])
                            C.dma(sync, z_qT[(half * 8 + h8) * HD:(half * 8 + h8 + 1) * HD,
                                             n * 512:(n + 1) * 512],
                                  slot[0:HD, :], 'ds')
                        ds_q.append(C.cnt['ds'])

            ds_hd = []
            tc_end = []
            vg_hd = []
            for h in range(HN):
                hp = h % 2
                ybase = 0 if hp == 0 else 64
                dbase = 96 if hp == 0 else 32

                @block.sync
                def _(sync):
                    if h == 0:
                        C.w(sync, 'vg')
                    if h >= 2:
                        sync.wait_ge(C.s('tc'), tc_end[h - 2])
                    C.dma(sync, sqh[hp][0:HD, :], z_qT[h * HD:(h + 1) * HD, :], 'ds')
                    C.dma(sync, skh[hp][0:HD, :], z_kT[h * HD:(h + 1) * HD, :], 'ds')
                    C.dma(sync, svh[hp].rearrange("p (a c) -> p a c", a=NT),
                          z_v[:, h * HS:(h + 1) * HS].rearrange("(a p) c -> p a c", p=128),
                          'ds')
                ds_hd.append(C.cnt['ds'])

                steps = []
                for ki in range(NT):
                    nq = T - 128 * ki
                    for qc in range((nq + 511) // 512):
                        w_ = min(512, nq - qc * 512)
                        steps.append((ki, qc, w_))
                tc0 = C.cnt['tc']
                sc0 = C.cnt['sc']

                @block.tensor
                def _(tensor):
                    tensor.wait_ge(C.s('ds'), ds_hd[h])
                    for si, (ki, qc, w_) in enumerate(steps):
                        if si >= 2:
                            tensor.wait_ge(C.s('sc'), sc0 + si - 1)
                        else:
                            tensor.wait_ge(C.s('sc'), sc0)
                        i2 = tensor.matmul(
                            out=pb[si % 2][:, 0:w_],
                            lhsT=skh[hp][0:HD, ki * 128:(ki + 1) * 128],
                            rhs=sqh[hp][0:HD, 128 * ki + qc * 512:128 * ki + qc * 512 + w_],
                            start=True, stop=True)
                        C.inc(i2, 'tc')

                @block.scalar
                def _(scalar):
                    for si, (ki, qc, w_) in enumerate(steps):
                        scalar.wait_ge(C.s('tc'), tc0 + si + 1)
                        i2 = scalar.activation(
                            out=pTa[:, PT_OFF[ki] + qc * 512:PT_OFF[ki] + qc * 512 + w_],
                            in_=pb[si % 2][:, 0:w_],
                            func=AF.Exp, bias=0.0, scale=ISQ96)
                        C.inc(i2, 'sc')
                sc_exp = C.cnt['sc']

                @block.vector
                def _(vector):
                    vector = C.gv(vector)
                    vector.wait_ge(C.s('sc'), sc_exp)
                    for ki in range(NT):
                        vector.tensor_tensor(
                            out=pTa[:, PT_OFF[ki]:PT_OFF[ki] + 128],
                            in0=pTa[:, PT_OFF[ki]:PT_OFF[ki] + 128],
                            in1=c_lt[:, :], op=ALU.mult)
                vg_mask = C.cnt['vg']

                @block.tensor
                def _(tensor):
                    tensor.wait_ge(C.s('vg'), vg_mask)
                    for n in range(4):
                        for ki in range(4 * n + 4):
                            if 128 * ki <= 512 * n:
                                rhs = pTa[:, PT_OFF[ki] + 512 * n - 128 * ki:
                                          PT_OFF[ki] + 512 * n - 128 * ki + 512]
                                c0, c1 = 0, 512
                                st_ = (ki == 0)
                            else:
                                off = 128 * ki - 512 * n
                                rhs = pTa[:, PT_OFF[ki]:PT_OFF[ki] + 512 - off]
                                c0, c1 = off, 512
                                st_ = False
                            tensor.matmul(out=pb[2 + n][ybase:ybase + 64, c0:c1],
                                          lhsT=svh[hp][:, ki * 64:(ki + 1) * 64],
                                          rhs=rhs, start=st_, stop=(ki == 4 * n + 3))
                            i2 = tensor.matmul(out=pb[2 + n][dbase:dbase + 32, c0:c1],
                                               lhsT=c_ones32[:, :],
                                               rhs=rhs, start=st_, stop=(ki == 4 * n + 3),
                                               tile_position=(0, dbase))
                    C.inc(i2, 'tc')
                tc_pv = C.cnt['tc']

                @block.vector
                def _(vector):
                    vector = C.gv(vector)
                    vector.wait_ge(C.s('tc'), tc_pv)
                    for n in range(4):
                        vector.tensor_copy(out=sd1[32 * n:32 * n + 1, 0:512],
                                           in_=pb[2 + n][dbase:dbase + 1, :])
                    vector.reciprocal(out=sd1[:, 512:1024], in_=sd1[:, 0:512])
                vg_rd = C.cnt['vg']

                for n in range(4):
                    @block.tensor
                    def _(tensor):
                        tensor.wait_ge(C.s('vg'), vg_rd if n == 0 else C.cnt['vg'])
                        i2 = tensor.matmul(out=pb[6][ybase:ybase + 64, :],
                                           lhsT=c_onesf[32 * n:32 * n + 1, :],
                                           rhs=sd1[32 * n:32 * n + 1, 512:1024],
                                           start=True, stop=True,
                                           tile_position=(32 * n, ybase))
                        C.inc(i2, 'tc')
                    tc_bc = C.cnt['tc']

                    @block.vector
                    def _(vector):
                        vector = C.gv(vector)
                        vector.wait_ge(C.s('tc'), tc_bc)
                        vector.tensor_copy(out=sd1[ybase:ybase + 64, 1024:1536],
                                           in_=pb[6][ybase:ybase + 64, :])
                        vector.tensor_tensor(
                            out=a8.rearrange("p (j c) -> p j c", j=ND)[:, h // 2, n * 512:(n + 1) * 512][ybase:ybase + 64, :],
                            in0=pb[2 + n][ybase:ybase + 64, :],
                            in1=sd1[ybase:ybase + 64, 1024:1536],
                            op=ALU.mult)
                vg_hd.append(C.cnt['vg'])
                tc_end.append(C.cnt['tc'])

            if dbg:
                vg_now = C.cnt['vg']

                @block.sync
                def _(sync):
                    sync.wait_ge(C.s('vg'), vg_now)
                    C.dma(sync, d_a8, a8[:, :], 'ds')

            tc_wo = []
            vg_wo = []
            ds_wo = []
            ds_xo = []
            for t in range(NT):
                p = t % 2
                banks = pb[0:2] if p == 0 else pb[2:4]
                xres = stg[:, p * 1024:(p + 1) * 1024]
                outt = sd1[:, p * 1024:(p + 1) * 1024]

                @block.sync
                def _(sync):
                    if t == 0:
                        C.w(sync, 'vg', 'dg')
                    if t >= 2:
                        sync.wait_ge(C.s('vg'), vg_wo[t - 2])
                        sync.wait_ge(C.s('dg'), ds_xo[t - 2])
                    C.dma(sync, xres, z_x1[t * 128:(t + 1) * 128, :], 'ds')
                ds_wo.append(C.cnt['ds'])

                @block.tensor
                def _(tensor):
                    if t == 0:
                        tensor.wait_ge(C.s('vg'), C.cnt['vg'])
                        tensor.wait_ge(C.s('dg'), dg_w)
                    if t >= 2:
                        tensor.wait_ge(C.s('vg'), vg_wo[t - 2])
                    for kk in range(ND):
                        for n2 in range(2):
                            i2 = tensor.matmul(
                                out=banks[n2][:, :],
                                lhsT=a8.rearrange("p (j c) -> p j c", j=ND)[:, kk, t * 128:(t + 1) * 128],
                                rhs=wa_wo[:, kk * 1024 + n2 * 512:kk * 1024 + (n2 + 1) * 512],
                                start=(kk == 0), stop=(kk == ND - 1))
                    C.inc(i2, 'tc')
                tc_wo.append(C.cnt['tc'])

                @block.vector
                def _(vector):
                    vector = C.gv(vector)
                    vector.wait_ge(C.s('tc'), tc_wo[t])
                    vector.wait_ge(C.s('ds'), ds_wo[t])
                    for n2 in range(2):
                        vector.scalar_tensor_tensor(
                            out=outt[:, n2 * 512:(n2 + 1) * 512],
                            in0=banks[n2][:, :], scalar=1.0,
                            in1=xres[:, n2 * 512:(n2 + 1) * 512],
                            op0=ALU.mult, op1=ALU.add)
                vg_wo.append(C.cnt['vg'])

                @block.gpsimd
                def _(gpsimd):
                    gpsimd.wait_ge(C.s('vg'), vg_wo[t])
                    C.dma(gpsimd, z_x2[t * 128:(t + 1) * 128, :], outt, 'dg')
                ds_xo.append(C.cnt['dg'])

        emit_ffn(x_in, z_x1)
        emit_attention()
        emit_ffn(z_x2, out, tag=2)

        if dbg:
            @block.sync
            def _(sync):
                C.w(sync, 'vg', 'sc')
                C.dma(sync, d_stat, sstat[:, 0:64], 'ds')

        @block.sync
        def _(sync):
            C.w(sync, 'ds', 'dg')

    return nc


_NC_CACHE = {}


def kernel(L0_attn_x, L0_attn_W_DQ, L0_attn_W_UQ, L0_attn_W_DKV, L0_attn_W_UK,
           L0_attn_W_UV, L0_attn_W_QR, L0_attn_W_KR, L0_attn_W_O,
           L0_ffn_w_gate, L0_ffn_w_up, L0_ffn_w_down):
    f = np.float32
    x = np.ascontiguousarray(np.asarray(L0_attn_x, dtype=f))
    wdqT = np.ascontiguousarray(np.asarray(L0_attn_W_DQ, dtype=f).T)
    wuq = np.asarray(L0_attn_W_UQ, dtype=f)
    wqr = np.asarray(L0_attn_W_QR, dtype=f)
    wcatq = np.ascontiguousarray(
        np.concatenate([wuq.reshape(4096, HN, HS), wqr.reshape(4096, HN, DR)],
                       axis=2).reshape(4096, NQ))
    wdkv = np.ascontiguousarray(np.asarray(L0_attn_W_DKV, dtype=f))
    wuk = np.ascontiguousarray(np.asarray(L0_attn_W_UK, dtype=f))
    wuv = np.ascontiguousarray(np.asarray(L0_attn_W_UV, dtype=f))
    wkr = np.ascontiguousarray(np.asarray(L0_attn_W_KR, dtype=f))
    wo = np.ascontiguousarray(np.asarray(L0_attn_W_O, dtype=f))
    wg = np.ascontiguousarray(np.asarray(L0_ffn_w_gate, dtype=f))
    wu = np.ascontiguousarray(np.asarray(L0_ffn_w_up, dtype=f))
    wd = np.ascontiguousarray(np.asarray(L0_ffn_w_down, dtype=f))

    t = np.arange(T, dtype=f)
    inv = (f(1.0) / (f(10000.0) ** (np.arange(0, DR, 2, dtype=f) / f(DR)))).astype(f)
    freqs = (t[:, None] * inv[None, :]).astype(f)
    emb = np.concatenate([freqs, freqs], axis=-1)
    rope = (np.cos(emb) + np.sin(emb)).astype(f)
    ropeT = np.ascontiguousarray(rope.T)

    ltm = np.zeros((128, 128), dtype=f)
    for kk in range(128):
        ltm[kk, kk:] = 1.0
    onescol = np.zeros((128, 32), dtype=f)
    onescol[:, 0] = 1.0
    onesf = np.ones((128, 64), dtype=f)
    magicv = np.zeros((128, 2), dtype=f); magicv[:, 0] = f(12582912.0); magicv[:, 1] = f(-12582912.0)
    eye = np.eye(128, dtype=f)

    if "nc" not in _NC_CACHE:
        _NC_CACHE["nc"] = build_nc()
    nc = _NC_CACHE["nc"]

    shared = dict(wdqT=wdqT, wcatq=wcatq, wdkv=wdkv, wuk=wuk, wuv=wuv, wkr=wkr,
                  wo=wo, wg=wg, wu=wu, wd=wd, ropeT=ropeT, ltmask=ltm,
                  onescol=onescol, onesf=onesf, magicv=magicv, eye=eye)
    in_maps = [dict(shared, x=np.ascontiguousarray(x[c])) for c in range(8)]
    res = run_bass_kernel_spmd(nc, in_maps, core_ids=list(range(8)))
    return np.stack([np.asarray(res.results[c]["out"]) for c in range(8)], axis=0)


# version 08
# speedup vs baseline: 1.1724x; 1.1724x over previous
import numpy as np
import ml_dtypes
from contextlib import ExitStack

import concourse.bass as bass
import concourse.mybir as mybir
from concourse.bass_utils import run_bass_kernel_spmd

F32 = mybir.dt.float32
BF16 = mybir.dt.bfloat16
AF = mybir.ActivationFunctionType
ALU = mybir.AluOpType
X = mybir.AxisListType.X

T = 2048
D = 1024
HN = 16
HS = 64
DR = 32
HD = HS + DR
DFF = 3072
NT = T // 128
ND = D // 128
NI = DFF // 128
NQ = HN * HD

MAGIC = float(np.float32(12582912.0))
INV7 = float(np.float32(1.0) / np.float32(7.0))
INV32767 = float(np.float32(1.0) / np.float32(32767.0))
INV1024 = float(np.float32(1.0) / np.float32(1024.0))
ISQ96 = float(np.float32(1.0) / np.float32(np.sqrt(np.float32(96.0))))
EPS_LN = 1e-5
EPS_Q = 1e-12

PT_OFF = [0] * (NT + 1)
for _ki in range(NT):
    PT_OFF[_ki + 1] = PT_OFF[_ki] + (T - 128 * _ki)
PT_TOT = PT_OFF[NT]


class K:
    def __init__(self, block, sems):
        self.block = block
        self.sems = sems
        self.cnt = {k: 0 for k in sems}

    def s(self, name):
        return self.sems[name]

    def inc(self, instr, name, n=1):
        self.cnt[name] += n
        return instr.then_inc(self.sems[name], n)

    def dma(self, eng, out, in_, name, **kw):
        i = eng.dma_start(out=out, in_=in_, **kw)
        self.cnt[name] += 16
        i.then_inc(self.sems[name], 16)
        return i

    def w(self, eng, *names):
        for name in names:
            eng.wait_ge(self.sems[name], self.cnt[name])

    def gv(self, vector):
        C = self

        class GV:
            def wait_ge(self, *a, **k):
                return vector.wait_ge(*a, **k)

            def __getattr__(self, name):
                fn = getattr(vector, name)

                def wrapped(*a, **k):
                    vector.wait_ge(C.s('vg'), C.cnt['vg'])
                    i = fn(*a, **k)
                    C.inc(i, 'vg')
                    return i
                return wrapped
        return GV()

    def gs(self, scalar):
        C = self

        class GS:
            def wait_ge(self, *a, **k):
                return scalar.wait_ge(*a, **k)

            def __getattr__(self, name):
                fn = getattr(scalar, name)

                def wrapped(*a, **k):
                    scalar.wait_ge(C.s('sc'), C.cnt['sc'])
                    i = fn(*a, **k)
                    C.inc(i, 'sc')
                    return i
                return wrapped
        return GS()


def build_nc(dbg=False):
    nc = bass.Bass("TRN2", target_bir_lowering=False, debug=False,
                   detect_race_conditions=False)

    def _dz(name, shape, dt):
        if dbg:
            return nc.dram_tensor(name, shape, dt, kind="ExternalOutput").ap()
        return nc.dram_tensor(name, shape, dt).ap()

    x_in = nc.dram_tensor("x", [T, D], F32, kind="ExternalInput").ap()
    wdqT = nc.dram_tensor("wdqT", [4096, D], F32, kind="ExternalInput").ap()
    wcatq = nc.dram_tensor("wcatq", [4096, NQ], F32, kind="ExternalInput").ap()
    wdkv = nc.dram_tensor("wdkv", [D, 256], F32, kind="ExternalInput").ap()
    wuk = nc.dram_tensor("wuk", [256, D], F32, kind="ExternalInput").ap()
    wuv = nc.dram_tensor("wuv", [256, D], F32, kind="ExternalInput").ap()
    wkr = nc.dram_tensor("wkr", [D, DR], F32, kind="ExternalInput").ap()
    wo = nc.dram_tensor("wo", [D, D], F32, kind="ExternalInput").ap()
    wg = nc.dram_tensor("wg", [DFF, D], F32, kind="ExternalInput").ap()
    wu = nc.dram_tensor("wu", [DFF, D], F32, kind="ExternalInput").ap()
    wd = nc.dram_tensor("wd", [D, DFF], F32, kind="ExternalInput").ap()
    ropeT_in = nc.dram_tensor("ropeT", [DR, T], F32, kind="ExternalInput").ap()
    ltmask = nc.dram_tensor("ltmask", [128, 128], F32, kind="ExternalInput").ap()
    onescol = nc.dram_tensor("onescol", [128, 32], F32, kind="ExternalInput").ap()
    onesf = nc.dram_tensor("onesf", [128, 64], F32, kind="ExternalInput").ap()
    magicv = nc.dram_tensor("magicv", [128, 2], F32, kind="ExternalInput").ap()
    eye_in = nc.dram_tensor("eye", [128, 128], F32, kind="ExternalInput").ap()
    out = nc.dram_tensor("out", [T, D], F32, kind="ExternalOutput").ap()

    z_x1 = _dz("z_x1", [T, D], F32)
    z_x2 = _dz("z_x2", [T, D], F32)
    z_wgT = _dz("z_wgT", [NI, D, 128], BF16)
    z_wuT = _dz("z_wuT", [NI, D, 128], BF16)
    z_wdT = _dz("z_wdT", [NI, 128, D], BF16)
    z_upraw = _dz("z_upraw", [DFF, T], F32)
    z_graw = _dz("z_graw", [DFF, T], F32)
    z_qu = _dz("z_qu", [DFF, T], mybir.dt.int16)
    z_g2 = _dz("z_g2", [DFF, T], F32)
    z_u = _dz("z_u", [DFF, T], F32)
    z_qT = _dz("z_qT", [NQ, T], BF16)
    z_kT = _dz("z_kT", [NQ, T], BF16)
    z_v = _dz("z_v", [T, D], BF16)
    d_a8 = _dz("d_a8", [128, 8 * T], BF16) if dbg else None
    d_a8f = _dz("d_a8f", [128, 8 * T], BF16) if dbg else None
    d_stat = _dz("d_stat", [128, 64], F32) if dbg else None
    cc_in = nc.dram_tensor("cc_in", [1, 2], F32).ap()
    cc_out = nc.dram_tensor("cc_out", [1, 2], F32).ap()
    z_sc = nc.dram_tensor("z_sc", [1, 16], F32).ap()

    with ExitStack() as ctx:
        def sb(name, shape, dt):
            return ctx.enter_context(nc.sbuf_tensor(name, shape, dt))

        c_rope = sb("c_rope", [128, T], F32)
        c_lt = sb("c_lt", [128, 128], BF16)
        c_eye = sb("c_eye", [128, 128], BF16)
        c_ones32 = sb("c_ones32", [128, 32], BF16)
        c_onesf = sb("c_onesf", [128, 64], F32)
        c_magic = sb("c_magic", [128, 2], F32)
        sval = sb("sval", [1, 16], F32)
        sscale = sb("sscale", [128, 16], F32)
        streeg = sb("streeg", [128, 4], F32)
        s32a = sb("s32a", [128, 32], F32)
        s32b = sb("s32b", [128, 32], F32)
        sacc = sb("sacc", [128, 4], F32)
        sstat = sb("sstat", [128, 4 * NT], F32)
        spad = sb("spad", [128, 16], F32)
        af = sb("af", [128, 16384], F32)
        a8 = sb("a8", [128, 8 * T], BF16)
        wa = sb("wa", [128, 24576], BF16)
        stg = sb("stg", [128, 4096], F32)
        sbq = sb("sbq", [128, 2048], BF16)
        sqh = [sb(f"sqh{i}", [HD, T], BF16) for i in range(2)]
        skh = [sb(f"skh{i}", [HD, T], BF16) for i in range(2)]
        svh = [sb(f"svh{i}", [128, HN * HS], BF16) for i in range(2)]
        skr = sb("skr", [DR, T], BF16)
        sd1 = sb("sd1", [128, 2048], F32)

        pb = [ctx.enter_context(nc.psum_tensor(f"pb{i}", [128, 512], F32))
              for i in range(8)]
        pt6 = pb[6][:, :].bitcast(BF16)
        pt7 = pb[7][:, :].bitcast(BF16)
        pTa = af[:, 0:8704].bitcast(BF16)
        weff = af[:, 0:6144].bitcast(BF16)

        sems = {}
        for name in ["ds", "dg", "tc", "sc", "cc", "vg"]:
            sems[name] = ctx.enter_context(nc.semaphore(name))
        lo = min(s.num for s in sems.values())
        hi = max(s.num for s in sems.values()) + 1
        nc.gpsimd.sem_clear(range(lo, hi))
        nc.all_engine_barrier()
        block = ctx.enter_context(nc.Block())
        C = K(block, sems)

        @block.sync
        def _(sync):
            C.dma(sync, c_rope[0:32, :], ropeT_in, 'ds')
            C.dma(sync, c_rope[64:96, :], ropeT_in, 'ds')
            C.dma(sync, c_onesf[:, :], onesf, 'ds')
            C.dma(sync, c_magic[:, :], magicv, 'ds')

        @block.gpsimd
        def _(gpsimd):
            C.dma(gpsimd, c_lt[:, :], ltmask, 'dg')
            C.dma(gpsimd, c_eye[:, :], eye_in, 'dg')
            C.dma(gpsimd, c_ones32[:, :], onescol, 'dg')
        dg_consts = C.cnt['dg']

        def tree_max(vector, src_col, dst_col):
            vector.tensor_copy(out=streeg[:64, 0:1], in_=sacc[64:128, src_col:src_col + 1])
            vector.tensor_tensor(out=streeg[:64, 1:2], in0=sacc[:64, src_col:src_col + 1],
                                 in1=streeg[:64, 0:1], op=ALU.max)
            vector.tensor_copy(out=streeg[:32, 2:3], in_=streeg[32:64, 1:2])
            vector.tensor_tensor(out=s32a[:32, 0:1], in0=streeg[:32, 1:2],
                                 in1=streeg[:32, 2:3], op=ALU.max)
            vector.transpose(out=s32b[:32, :32], in_=s32a[:32, :32])
            return vector.reduce_max(out=sval[0:1, dst_col:dst_col + 1],
                                     in_=s32b[0:1, :32], axis=X)

        def emit_allreduce():
            vg_ready = C.cnt['vg']

            @block.gpsimd
            def _(gpsimd):
                gpsimd.wait_ge(C.s('vg'), vg_ready)
                C.dma(gpsimd, cc_in, sval[0:1, 12:14], 'dg')
                C.w(gpsimd, 'dg')
                i = gpsimd.collective_compute(
                    "AllReduce", ALU.max,
                    replica_groups=[list(range(8))],
                    ins=[cc_in.opt()],
                    outs=[cc_out.opt()],
                )
                C.inc(i, 'cc')
                gpsimd.wait_ge(C.s('cc'), C.cnt['cc'])
                C.dma(gpsimd, sval[0:1, 10:12], cc_out, 'dg')

        def emit_scales(build, ncols):
            dg_ready = C.cnt['dg']

            @block.vector
            def _(vector):
                vector = C.gv(vector)
                vector.wait_ge(C.s('dg'), dg_ready)
                build(vector)
            vg_here = C.cnt['vg']

            @block.sync
            def _(sync):
                sync.wait_ge(C.s('vg'), vg_here)
                C.dma(sync, z_sc[0:1, 0:ncols], sval[0:1, 0:ncols], 'ds')
                C.w(sync, 'ds')
                C.dma(sync, sscale[:, 0:ncols],
                      z_sc[0:1, 0:ncols].broadcast_to([128, ncols]), 'ds')
                C.dma(sync, spad[:], ltmask[:, 0:16], 'ds')

        def dequant_core(vector, buf, tq, bm, scl, rcp, outv, nblk):
            vector.reduce_max(out=bm.rearrange("p (a b) -> p a b", b=1),
                              in_=buf.rearrange("p (a b) -> p a b", b=32),
                              axis=X, apply_absolute_value=True)
            vector.tensor_scalar(out=scl, in0=bm, scalar1=INV7, scalar2=EPS_Q,
                                 op0=ALU.mult, op1=ALU.max)
            vector.reciprocal(out=rcp, in_=scl)
            vector.tensor_tensor(out=tq.rearrange("p (a b) -> p a b", b=32),
                                 in0=buf.rearrange("p (a b) -> p a b", b=32),
                                 in1=rcp[:, :, None].broadcast_to([128, nblk, 32]),
                                 op=ALU.mult)
            vector.tensor_scalar(out=tq, in0=tq, scalar1=MAGIC, scalar2=MAGIC,
                                 op0=ALU.add, op1=ALU.subtract)
            return vector.tensor_tensor(out=outv.rearrange("p (a b) -> p a b", b=32),
                                        in0=tq.rearrange("p (a b) -> p a b", b=32),
                                        in1=scl[:, :, None].broadcast_to([128, nblk, 32]),
                                        op=ALU.mult)

        def emit_dequant_gateup(src, dstT):
            vg_after = []
            tc_after = []
            sc_after = []
            ds_ld = []
            ds_st = []
            for i in range(NI):
                p = i % 2
                buf = af[:, p * 1024:(p + 1) * 1024]
                tq = af[:, 4096 + p * 1024:4096 + (p + 1) * 1024]
                bm = af[:, 8192 + p * 128:8192 + p * 128 + 32]
                scl = af[:, 8448 + p * 128:8448 + p * 128 + 32]
                rcp = af[:, 8704 + p * 128:8704 + p * 128 + 32]
                obf = sbq[:, p * 1024:(p + 1) * 1024]
                ptv = pt6 if p == 0 else pt7

                @block.sync
                def _(sync):
                    if i >= 2:
                        sync.wait_ge(C.s('vg'), vg_after[i - 2])
                    else:
                        C.w(sync, 'vg')
                    C.dma(sync, buf, src[i * 128:(i + 1) * 128, :], 'ds')
                ds_ld.append(C.cnt['ds'])

                @block.vector
                def _(vector):
                    vector = C.gv(vector)
                    vector.wait_ge(C.s('ds'), ds_ld[i])
                    if i >= 2:
                        vector.wait_ge(C.s('tc'), tc_after[i - 2])
                        vector.wait_ge(C.s('dg'), ds_st[i - 2])
                    else:
                        C.w(vector, 'tc', 'dg')
                    dequant_core(vector, buf, tq, bm, scl, rcp, obf, 32)
                vg_after.append(C.cnt['vg'])

                @block.tensor
                def _(tensor):
                    tensor.wait_ge(C.s('vg'), vg_after[i])
                    if i == 0:
                        tensor.wait_ge(C.s('dg'), dg_consts)
                    if i >= 2:
                        tensor.wait_ge(C.s('sc'), sc_after[i - 2])
                    for j in range(ND):
                        i2 = tensor.transpose(out=ptv[:, j * 128:(j + 1) * 128],
                                              in_=obf[:, j * 128:(j + 1) * 128],
                                              identity=c_eye[:, :])
                    C.inc(i2, 'tc')
                tc_after.append(C.cnt['tc'])

                @block.scalar
                def _(scalar):
                    scalar = C.gs(scalar)
                    scalar.wait_ge(C.s('tc'), tc_after[i])
                    scalar.copy(out=obf[:, :], in_=ptv[:, :])
                sc_after.append(C.cnt['sc'])

                @block.gpsimd
                def _(gpsimd):
                    gpsimd.wait_ge(C.s('sc'), sc_after[i])
                    C.dma(gpsimd, dstT[i].rearrange("(a p) c -> p a c", p=128),
                          obf.rearrange("p (a c) -> p a c", a=ND), 'dg')
                ds_st.append(C.cnt['dg'])

        def emit_dequant_down(src, dstT):
            for r in range(ND):
                buf = af[:, 0:3072]
                tq = af[:, 4096:7168]
                bm = af[:, 8192:8288]
                scl = af[:, 8448:8544]
                rcp = af[:, 8704:8800]
                obf = af[:, 10240:11776].bitcast(BF16)

                @block.sync
                def _(sync):
                    C.w(sync, 'vg', 'tc')
                    C.dma(sync, buf, src[r * 128:(r + 1) * 128, :], 'ds')
                ds_h = C.cnt['ds']

                @block.vector
                def _(vector):
                    vector = C.gv(vector)
                    vector.wait_ge(C.s('ds'), ds_h)
                    dequant_core(vector, buf, tq, bm, scl, rcp, obf, 96)
                vg_h = C.cnt['vg']

                for rnd in range(3):
                    ptv = pt6 if rnd % 2 == 0 else pt7
                    slot = sbq[:, (rnd % 2) * 1024:(rnd % 2 + 1) * 1024]

                    @block.tensor
                    def _(tensor):
                        if rnd == 0:
                            tensor.wait_ge(C.s('vg'), vg_h)
                            tensor.wait_ge(C.s('dg'), dg_consts)
                        tensor.wait_ge(C.s('dg'), C.cnt['dg'])
                        tensor.wait_ge(C.s('sc'), C.cnt['sc'])
                        for jj in range(8):
                            i2 = tensor.transpose(
                                out=ptv[:, jj * 128:(jj + 1) * 128],
                                in_=obf[:, (rnd * 8 + jj) * 128:(rnd * 8 + jj + 1) * 128],
                                identity=c_eye[:, :])
                        C.inc(i2, 'tc')
                    tc_h = C.cnt['tc']

                    @block.scalar
                    def _(scalar):
                        scalar = C.gs(scalar)
                        scalar.wait_ge(C.s('tc'), tc_h)
                        scalar.copy(out=slot, in_=ptv[:, :])
                    sc_h = C.cnt['sc']

                    @block.gpsimd
                    def _(gpsimd):
                        gpsimd.wait_ge(C.s('sc'), sc_h)
                        C.dma(gpsimd,
                              dstT[rnd * 8:(rnd + 1) * 8, :, r * 128:(r + 1) * 128]
                              .rearrange("a p c -> p a c"),
                              slot.rearrange("p (a c) -> p a c", a=8), 'dg')

        emit_dequant_gateup(wu, z_wuT)
        emit_dequant_gateup(wg, z_wgT)
        emit_dequant_down(wd, z_wdT)

        def emit_ln(src):
            ds_after = []
            for t in range(NT):
                xt = af[:, t * 1024:(t + 1) * 1024]

                @block.sync
                def _(sync):
                    if t == 0:
                        C.w(sync, 'vg', 'tc', 'sc', 'ds', 'dg')
                    C.dma(sync, xt, src[t * 128:(t + 1) * 128, :], 'ds')
                ds_after.append(C.cnt['ds'])

            @block.sync
            def _(sync):
                C.dma(sync, spad[:], ltmask[:, 0:16], 'ds')

            for t in range(NT):
                xt = af[:, t * 1024:(t + 1) * 1024]

                @block.vector
                def _(vector):
                    vector = C.gv(vector)
                    vector.wait_ge(C.s('ds'), ds_after[t] + 16)
                    vector.reduce_sum(out=sstat[:, t:t + 1], in_=xt, axis=X)
                    vector.tensor_single_scalar(out=sstat[:, NT + t:NT + t + 1],
                                                in_=sstat[:, t:t + 1],
                                                scalar=-INV1024, op=ALU.mult)
                vg_t = C.cnt['vg']

                @block.scalar
                def _(scalar):
                    scalar = C.gs(scalar)
                    scalar.wait_ge(C.s('vg'), vg_t)
                    scalar.activation(out=stg[:, (t % 2) * 1024:(t % 2 + 1) * 1024],
                                      in_=xt, func=AF.Square,
                                      bias=sstat[:, NT + t:NT + t + 1], scale=1.0,
                                      accum_out=sstat[:, 2 * NT + t:2 * NT + t + 1])
            sc_sq = C.cnt['sc']

            @block.vector
            def _(vector):
                vector = C.gv(vector)
                vector.wait_ge(C.s('sc'), sc_sq)
                vector.tensor_scalar(out=sstat[:, 2 * NT:3 * NT],
                                     in0=sstat[:, 2 * NT:3 * NT],
                                     scalar1=INV1024, scalar2=EPS_LN,
                                     op0=ALU.mult, op1=ALU.add)
            vg_var = C.cnt['vg']

            @block.scalar
            def _(scalar):
                scalar = C.gs(scalar)
                scalar.wait_ge(C.s('vg'), vg_var)
                scalar.sqrt(out=sstat[:, 2 * NT:3 * NT], in_=sstat[:, 2 * NT:3 * NT])
            sc_sqrt = C.cnt['sc']

            @block.vector
            def _(vector):
                vector = C.gv(vector)
                vector.wait_ge(C.s('sc'), sc_sqrt)
                vector.reciprocal(out=sstat[:, 2 * NT:3 * NT], in_=sstat[:, 2 * NT:3 * NT])
                vector.tensor_tensor(out=sstat[:, 3 * NT:4 * NT],
                                     in0=sstat[:, NT:2 * NT],
                                     in1=sstat[:, 2 * NT:3 * NT], op=ALU.mult)
            vg_rstd = C.cnt['vg']

            for t in range(NT):
                xt = af[:, t * 1024:(t + 1) * 1024]

                @block.scalar
                def _(scalar):
                    scalar = C.gs(scalar)
                    if t == 0:
                        scalar.wait_ge(C.s('vg'), vg_rstd)
                    scalar.activation(out=xt, in_=xt, func=AF.Identity,
                                      bias=sstat[:, 3 * NT + t:3 * NT + t + 1],
                                      scale=sstat[:, 2 * NT + t:2 * NT + t + 1])
                sc_t = C.cnt['sc']

                @block.vector
                def _(vector):
                    vector = C.gv(vector)
                    vector.wait_ge(C.s('sc'), sc_t)
                    vector.reduce_max(out=sacc[:, 1:2], in_=xt, axis=X,
                                      apply_absolute_value=True)
                    if t == 0:
                        vector.tensor_copy(out=sacc[:, 0:1], in_=sacc[:, 1:2])
                    else:
                        vector.tensor_tensor(out=sacc[:, 0:1], in0=sacc[:, 0:1],
                                             in1=sacc[:, 1:2], op=ALU.max)

        def emit_tiles_to_a8(quant):
            vg_evac = []
            tc_tr = []
            for t in range(NT):
                p = t % 2
                xt = af[:, t * 1024:(t + 1) * 1024]
                slot = sbq[:, p * 1024:(p + 1) * 1024]
                ptv = pt6 if p == 0 else pt7

                @block.vector
                def _(vector):
                    vector = C.gv(vector)
                    if t == 0:
                        C.w(vector, 'ds')
                    if t >= 2:
                        vector.wait_ge(C.s('tc'), tc_tr[t - 2])
                    if quant:
                        vector.tensor_scalar(out=xt, in0=xt, scalar1=sscale[:, 0:1],
                                             scalar2=MAGIC, op0=ALU.mult, op1=ALU.add)
                        vector.tensor_scalar(out=slot, in0=xt, scalar1=MAGIC,
                                             scalar2=sscale[:, 1:2],
                                             op0=ALU.subtract, op1=ALU.mult)
                    else:
                        vector.tensor_copy(out=slot, in_=xt)
                vg_q = C.cnt['vg']

                @block.tensor
                def _(tensor):
                    tensor.wait_ge(C.s('vg'), vg_q)
                    if t == 0:
                        tensor.wait_ge(C.s('dg'), dg_consts)
                    if t >= 2:
                        tensor.wait_ge(C.s('vg'), vg_evac[t - 2])
                    for j in range(ND):
                        i2 = tensor.transpose(out=ptv[:, j * 128:(j + 1) * 128],
                                              in_=slot[:, j * 128:(j + 1) * 128],
                                              identity=c_eye[:, :])
                    C.inc(i2, 'tc')
                tc_tr.append(C.cnt['tc'])

                @block.vector
                def _(vector):
                    vector = C.gv(vector)
                    vector.wait_ge(C.s('tc'), tc_tr[t])
                    vector.tensor_copy(
                        out=a8.rearrange("p (j c) -> p j c", j=ND)[:, :, t * 128:(t + 1) * 128],
                        in_=ptv.rearrange("p (j c) -> p j c", j=ND))
                vg_evac.append(C.cnt['vg'])

        def emit_ffn(src, dst, tag=1):
            emit_ln(src)

            @block.vector
            def _(vector):
                vector = C.gv(vector)
                tree_max(vector, 0, 12)
                vector.tensor_copy(out=sval[0:1, 13:14], in_=sval[0:1, 12:13])

            emit_allreduce()

            def scales_x(vector):
                vector.tensor_scalar(out=sval[0:1, 1:2], in0=sval[0:1, 10:11],
                                     scalar1=INV32767, scalar2=EPS_Q,
                                     op0=ALU.mult, op1=ALU.max)
                vector.reciprocal(out=sval[0:1, 0:1], in_=sval[0:1, 1:2])
            emit_scales(scales_x, 2)

            emit_tiles_to_a8(quant=True)

            if dbg and tag == 2:
                vg_now2 = C.cnt['vg']

                @block.sync
                def _(sync):
                    sync.wait_ge(C.s('vg'), vg_now2)
                    C.dma(sync, d_a8f, a8[:, :], 'ds')

            def gateup(wT, dst_raw, acc_col):
                vg_evac = []
                tc_mm = []
                ds_ld = []
                ds_st = []
                tc0 = C.cnt['tc']
                dg0 = C.cnt['dg']
                for i in range(NI):
                    p = i % 2
                    wtile = wa[:, p * 1024:(p + 1) * 1024]
                    banks = pb[0:4] if p == 0 else pb[4:8]
                    stgh = stg[:, 0:2048] if p == 0 else stg[:, 2048:4096]

                    @block.sync
                    def _(sync):
                        if i < 2:
                            sync.wait_ge(C.s('dg'), dg0)
                            sync.wait_ge(C.s('tc'), tc0)
                        if i >= 2:
                            sync.wait_ge(C.s('tc'), tc_mm[i - 2])
                        C.dma(sync, wtile.rearrange("p (a c) -> p a c", a=ND),
                              wT[i].rearrange("(a p) c -> p a c", p=128), 'ds')
                    ds_ld.append(C.cnt['ds'])

                    @block.tensor
                    def _(tensor):
                        tensor.wait_ge(C.s('ds'), ds_ld[i] + 16)
                        if i == 0:
                            tensor.wait_ge(C.s('vg'), C.cnt['vg'])
                        if i >= 2:
                            tensor.wait_ge(C.s('vg'), vg_evac[i - 2])
                        for kk in range(ND):
                            for n in range(4):
                                i2 = tensor.matmul(
                                    out=banks[n][:, :],
                                    lhsT=wtile.rearrange("p (a c) -> p a c", a=ND)[:, kk, :],
                                    rhs=a8.rearrange("p (j c) -> p j c", j=ND)[:, kk, n * 512:(n + 1) * 512],
                                    start=(kk == 0), stop=(kk == ND - 1))
                        C.inc(i2, 'tc')
                    tc_mm.append(C.cnt['tc'])

                    @block.vector
                    def _(vector):
                        vector = C.gv(vector)
                        vector.wait_ge(C.s('tc'), tc_mm[i])
                        if i >= 2:
                            vector.wait_ge(C.s('dg'), ds_st[i - 2])
                        else:
                            C.w(vector, 'dg')
                        for n in range(4):
                            vector.tensor_copy(out=stgh[:, n * 512:(n + 1) * 512],
                                               in_=banks[n][:, :])
                        vector.reduce_max(out=sacc[:, 1:2], in_=stgh,
                                          axis=X, apply_absolute_value=True)
                        if i == 0:
                            vector.tensor_copy(out=sacc[:, acc_col:acc_col + 1],
                                               in_=sacc[:, 1:2])
                        else:
                            vector.tensor_tensor(out=sacc[:, acc_col:acc_col + 1],
                                                 in0=sacc[:, acc_col:acc_col + 1],
                                                 in1=sacc[:, 1:2], op=ALU.max)
                    vg_evac.append(C.cnt['vg'])

                    @block.gpsimd
                    def _(gpsimd):
                        gpsimd.wait_ge(C.s('vg'), vg_evac[i])
                        C.dma(gpsimd, dst_raw[i * 128:(i + 1) * 128, :], stgh, 'dg')
                    ds_st.append(C.cnt['dg'])

                @block.sync
                def _(sync):
                    C.dma(sync, spad[:], ltmask[:, 0:16], 'ds')

            gateup(z_wuT, z_upraw, 2)
            gateup(z_wgT, z_graw, 3)

            @block.vector
            def _(vector):
                vector = C.gv(vector)
                tree_max(vector, 2, 12)
                tree_max(vector, 3, 13)

            emit_allreduce()

            def scales_ug(vector):
                vector.tensor_scalar(out=sval[0:1, 1:2], in0=sval[0:1, 10:11],
                                     scalar1=INV32767, scalar2=EPS_Q,
                                     op0=ALU.mult, op1=ALU.max)
                vector.tensor_scalar(out=sval[0:1, 3:4], in0=sval[0:1, 11:12],
                                     scalar1=INV32767, scalar2=EPS_Q,
                                     op0=ALU.mult, op1=ALU.max)
                vector.reciprocal(out=sval[0:1, 0:1], in_=sval[0:1, 1:2])
                vector.reciprocal(out=sval[0:1, 2:3], in_=sval[0:1, 3:4])
                vector.tensor_single_scalar(out=sval[0:1, 4:5], in_=sval[0:1, 3:4],
                                            scalar=float(2.0 ** -16), op=ALU.mult)
            emit_scales(scales_ug, 5)

            def a8f(k):
                return a8[:, k * 4096:(k + 1) * 4096].bitcast(F32)

            def waf(k):
                return wa[:, k * 4096:(k + 1) * 4096].bitcast(F32)
            vg_b = []
            sc_b = []
            ds_b = []
            dg_b = []
            for i in range(NI):
                s4 = i % 4
                BU = af[:, s4 * 2048:(s4 + 1) * 2048]
                BG = af[:, (4 + s4) * 2048:(5 + s4) * 2048]
                TG = a8f(s4)
                SSG = waf(s4)
                T1 = stg[:, 0:2048]

                @block.sync
                def _(sync):
                    if i == 0:
                        C.w(sync, 'vg', 'tc', 'sc', 'dg')
                    if i >= 4:
                        sync.wait_ge(C.s('dg'), dg_b[i - 4])
                    C.dma(sync, BU, z_upraw[i * 128:(i + 1) * 128, :], 'ds')
                    C.dma(sync, BG, z_graw[i * 128:(i + 1) * 128, :], 'ds')
                ds_b.append(C.cnt['ds'])
                ds_bm = ds_b[i] + (32 if i < NI - 1 else 16)

                @block.scalar
                def _(scalar):
                    scalar = C.gs(scalar)
                    scalar.wait_ge(C.s('ds'), ds_bm)
                    if i >= 4:
                        scalar.wait_ge(C.s('vg'), vg_b[i - 4])
                    scalar.activation(out=T1, in_=BU, func=AF.Identity,
                                      bias=c_magic[:, 0:1], scale=sscale[:, 0:1])
                    scalar.activation(out=BU, in_=T1, func=AF.Identity,
                                      bias=c_magic[:, 1:2], scale=1.0)
                    scalar.activation(out=T1, in_=BG, func=AF.Identity,
                                      bias=c_magic[:, 0:1], scale=sscale[:, 2:3])
                    scalar.activation(out=TG, in_=T1, func=AF.Identity,
                                      bias=c_magic[:, 1:2], scale=1.0)
                    scalar.activation(out=SSG, in_=TG, func=AF.Sigmoid,
                                      bias=0.0, scale=sscale[:, 3:4])
                sc_b.append(C.cnt['sc'])

                @block.vector
                def _(vector):
                    vector = C.gv(vector)
                    vector.wait_ge(C.s('sc'), sc_b[i])
                    vector.tensor_scalar(out=SSG, in0=SSG, scalar1=65536.0,
                                         scalar2=MAGIC, op0=ALU.mult, op1=ALU.add)
                    vector.tensor_scalar(out=SSG, in0=SSG, scalar1=MAGIC,
                                         scalar2=65535.0, op0=ALU.subtract, op1=ALU.min)
                    vector.scalar_tensor_tensor(out=BG, in0=TG, scalar=sscale[:, 4:5],
                                                in1=SSG, op0=ALU.mult, op1=ALU.mult)
                    vector.reduce_max(out=sacc[:, 1:2], in_=BG, axis=X,
                                      apply_absolute_value=True)
                    if i == 0:
                        vector.tensor_copy(out=sacc[:, 2:3], in_=sacc[:, 1:2])
                    else:
                        vector.tensor_tensor(out=sacc[:, 2:3], in0=sacc[:, 2:3],
                                             in1=sacc[:, 1:2], op=ALU.max)
                vg_b.append(C.cnt['vg'])

                @block.gpsimd
                def _(gpsimd):
                    gpsimd.wait_ge(C.s('sc'), sc_b[i])
                    gpsimd.wait_ge(C.s('vg'), vg_b[i])
                    C.dma(gpsimd, z_qu[i * 128:(i + 1) * 128, :], BU, 'dg')
                    C.dma(gpsimd, z_g2[i * 128:(i + 1) * 128, :], BG, 'dg')
                dg_b.append(C.cnt['dg'])

            @block.sync
            def _(sync):
                C.dma(sync, spad[:], ltmask[:, 0:16], 'ds')

            @block.vector
            def _(vector):
                vector = C.gv(vector)
                tree_max(vector, 2, 12)
                vector.tensor_copy(out=sval[0:1, 13:14], in_=sval[0:1, 12:13])

            emit_allreduce()

            def scales_g2(vector):
                vector.tensor_scalar(out=sval[0:1, 6:7], in0=sval[0:1, 10:11],
                                     scalar1=INV32767, scalar2=EPS_Q,
                                     op0=ALU.mult, op1=ALU.max)
                vector.reciprocal(out=sval[0:1, 5:6], in_=sval[0:1, 6:7])
            emit_scales(scales_g2, 7)

            vg_c = []
            sc_c = []
            ds_c = []
            dg_c = []
            for i in range(NI):
                s4 = i % 4
                BG = af[:, s4 * 2048:(s4 + 1) * 2048]
                BU = af[:, (4 + s4) * 2048:(5 + s4) * 2048]
                UO = a8f(s4)
                QUI = wa[:, s4 * 2048:(s4 + 1) * 2048].bitcast(mybir.dt.int16)
                T1 = stg[:, 0:2048]

                @block.sync
                def _(sync):
                    if i == 0:
                        C.w(sync, 'vg', 'sc', 'dg')
                    if i >= 4:
                        sync.wait_ge(C.s('vg'), vg_c[i - 4])
                    C.dma(sync, BG, z_g2[i * 128:(i + 1) * 128, :], 'ds')
                    C.dma(sync, QUI, z_qu[i * 128:(i + 1) * 128, :], 'ds')
                ds_c.append(C.cnt['ds'])
                ds_cm = ds_c[i] + (32 if i < NI - 1 else 16)

                @block.scalar
                def _(scalar):
                    scalar = C.gs(scalar)
                    scalar.wait_ge(C.s('ds'), ds_cm)
                    scalar.activation(out=T1, in_=BG, func=AF.Identity,
                                      bias=c_magic[:, 0:1], scale=sscale[:, 5:6])
                    scalar.activation(out=BG, in_=T1, func=AF.Identity,
                                      bias=c_magic[:, 1:2], scale=1.0)
                    scalar.mul(out=BG, in_=BG, mul=sscale[:, 6:7])
                sc_c.append(C.cnt['sc'])

                @block.vector
                def _(vector):
                    vector = C.gv(vector)
                    vector.wait_ge(C.s('ds'), ds_cm)
                    if i >= 4:
                        vector.wait_ge(C.s('dg'), dg_c[i - 4])
                    vector.tensor_copy(out=BU, in_=QUI)
                    vector.tensor_single_scalar(out=BU, in_=BU,
                                                scalar=sscale[:, 1:2], op=ALU.mult)
                    vector.wait_ge(C.s('sc'), sc_c[i])
                    vector.tensor_tensor(out=UO, in0=BG, in1=BU, op=ALU.mult)
                    vector.reduce_max(out=sacc[:, 1:2], in_=UO, axis=X,
                                      apply_absolute_value=True)
                    if i == 0:
                        vector.tensor_copy(out=sacc[:, 2:3], in_=sacc[:, 1:2])
                    else:
                        vector.tensor_tensor(out=sacc[:, 2:3], in0=sacc[:, 2:3],
                                             in1=sacc[:, 1:2], op=ALU.max)
                vg_c.append(C.cnt['vg'])

                @block.gpsimd
                def _(gpsimd):
                    gpsimd.wait_ge(C.s('vg'), vg_c[i])
                    C.dma(gpsimd, z_u[i * 128:(i + 1) * 128, :], UO, 'dg')
                dg_c.append(C.cnt['dg'])

            @block.sync
            def _(sync):
                C.dma(sync, spad[:], ltmask[:, 0:16], 'ds')

            @block.vector
            def _(vector):
                vector = C.gv(vector)
                tree_max(vector, 2, 12)
                vector.tensor_copy(out=sval[0:1, 13:14], in_=sval[0:1, 12:13])

            emit_allreduce()

            def scales_o(vector):
                vector.tensor_scalar(out=sval[0:1, 8:9], in0=sval[0:1, 10:11],
                                     scalar1=INV32767, scalar2=EPS_Q,
                                     op0=ALU.mult, op1=ALU.max)
                vector.reciprocal(out=sval[0:1, 7:8], in_=sval[0:1, 8:9])
                vector.tensor_single_scalar(out=sval[0:1, 8:9], in_=sval[0:1, 8:9],
                                            scalar=0.5, op=ALU.mult)
            emit_scales(scales_o, 9)

            @block.sync
            def _(sync):
                C.w(sync, 'tc', 'vg', 'dg')
                for i in range(NI):
                    C.dma(sync, wa.rearrange("p (a c) -> p a c", a=NI)[:, i, :],
                          z_wdT[i], 'ds')
            ds_wd = C.cnt['ds']

            vg_dn = []
            vg_oq = []
            tc_dn = []
            ds_dn = []
            ds_os = []
            for g in range(8):
                p = g % 2
                banks = pb[0:4] if p == 0 else pb[4:8]
                oarea = a8[:, p * 6144:p * 6144 + 6144]
                xres = stg[:, p * 2048:(p + 1) * 2048]
                uarea = af[:, 4096 + p * 6144:4096 + (p + 1) * 6144]
                outt = af[:, p * 2048:(p + 1) * 2048]

                @block.sync
                def _(sync):
                    if g == 0:
                        C.w(sync, 'dg', 'vg', 'sc')
                    if g >= 2:
                        sync.wait_ge(C.s('vg'), vg_oq[g - 2])
                        sync.wait_ge(C.s('vg'), vg_dn[g - 2])
                        sync.wait_ge(C.s('dg'), ds_os[g - 2])
                    C.dma(sync, uarea.rearrange("p (a c) -> p a c", a=NI),
                          z_u[:, g * 256:(g + 1) * 256].rearrange("(a p) c -> p a c", p=128),
                          'ds')
                    C.dma(sync, xres[:, 0:1024], src[g * 256:g * 256 + 128, :], 'ds')
                    C.dma(sync, xres[:, 1024:2048], src[g * 256 + 128:(g + 1) * 256, :], 'ds')
                ds_dn.append(C.cnt['ds'])
                ds_dm = ds_dn[g] + (48 if g < 7 else 16)

                @block.vector
                def _(vector):
                    vector = C.gv(vector)
                    vector.wait_ge(C.s('ds'), ds_dm)
                    if g >= 2:
                        vector.wait_ge(C.s('tc'), tc_dn[g - 2])
                    vector.tensor_scalar(out=uarea, in0=uarea, scalar1=sscale[:, 7:8],
                                         scalar2=MAGIC, op0=ALU.mult, op1=ALU.add)
                    vector.tensor_scalar(out=oarea, in0=uarea, scalar1=MAGIC,
                                         scalar2=sscale[:, 8:9],
                                         op0=ALU.subtract, op1=ALU.mult)
                vg_oq.append(C.cnt['vg'])

                @block.tensor
                def _(tensor):
                    tensor.wait_ge(C.s('vg'), vg_oq[g])
                    tensor.wait_ge(C.s('ds'), ds_wd)
                    if g >= 2:
                        tensor.wait_ge(C.s('vg'), vg_dn[g - 2])
                    for i in range(NI):
                        for tt in range(2):
                            for n2 in range(2):
                                i2 = tensor.matmul(
                                    out=banks[tt * 2 + n2][:, :],
                                    lhsT=oarea.rearrange("p (a c) -> p a c", a=NI)[:, i, tt * 128:(tt + 1) * 128],
                                    rhs=wa.rearrange("p (a c) -> p a c", a=NI)[:, i, n2 * 512:(n2 + 1) * 512],
                                    start=(i == 0), stop=(i == NI - 1))
                    C.inc(i2, 'tc')
                tc_dn.append(C.cnt['tc'])

                @block.vector
                def _(vector):
                    vector = C.gv(vector)
                    vector.wait_ge(C.s('tc'), tc_dn[g])
                    for tt in range(2):
                        for n2 in range(2):
                            vector.scalar_tensor_tensor(
                                out=outt[:, tt * 1024 + n2 * 512:tt * 1024 + (n2 + 1) * 512],
                                in0=banks[tt * 2 + n2][:, :], scalar=1.0,
                                in1=xres[:, tt * 1024 + n2 * 512:tt * 1024 + (n2 + 1) * 512],
                                op0=ALU.mult, op1=ALU.add)
                vg_dn.append(C.cnt['vg'])

                @block.gpsimd
                def _(gpsimd):
                    gpsimd.wait_ge(C.s('vg'), vg_dn[g])
                    C.dma(gpsimd, dst[g * 256:g * 256 + 128, :], outt[:, 0:1024], 'dg')
                    C.dma(gpsimd, dst[g * 256 + 128:(g + 1) * 256, :], outt[:, 1024:2048], 'dg')
                ds_os.append(C.cnt['dg'])

            @block.sync
            def _(sync):
                C.dma(sync, spad[:], ltmask[:, 0:16], 'ds')

        def emit_attention():
            emit_ln(z_x1)
            emit_tiles_to_a8(quant=False)

            wa_dkv = wa[:, 0:2048]
            wa_uk = wa[:, 2048:4096]
            wa_uv = wa[:, 4096:6144]
            wa_kr = wa[:, 6144:6400]
            wa_ckv = wa[:, 6400:10496]
            wa_wo = wa[:, 10496:18688]
            wa_st = wa[:, 18688:21760]

            @block.gpsimd
            def _(gpsimd):
                C.w(gpsimd, 'tc', 'vg')
                C.dma(gpsimd, wa_dkv.rearrange("p (a c) -> p a c", a=ND),
                      wdkv.rearrange("(a p) c -> p a c", p=128), 'dg')
                C.dma(gpsimd, wa_uk.rearrange("p (a c) -> p a c", a=2),
                      wuk.rearrange("(a p) c -> p a c", p=128), 'dg')
                C.dma(gpsimd, wa_uv.rearrange("p (a c) -> p a c", a=2),
                      wuv.rearrange("(a p) c -> p a c", p=128), 'dg')
                C.dma(gpsimd, wa_kr.rearrange("p (a c) -> p a c", a=ND),
                      wkr.rearrange("(a p) c -> p a c", p=128), 'dg')
                C.dma(gpsimd, wa_wo.rearrange("p (a c) -> p a c", a=ND),
                      wo.rearrange("(a p) c -> p a c", p=128), 'dg')
            dg_w = C.cnt['dg']

            for m2 in range(2):
                @block.tensor
                def _(tensor):
                    tensor.wait_ge(C.s('dg'), dg_w)
                    tensor.wait_ge(C.s('vg'), C.cnt['vg'])
                    for kk in range(ND):
                        for n in range(4):
                            i2 = tensor.matmul(
                                out=pb[n][:, :],
                                lhsT=wa_dkv.rearrange("p (a c) -> p a c", a=ND)[:, kk, m2 * 128:(m2 + 1) * 128],
                                rhs=a8.rearrange("p (j c) -> p j c", j=ND)[:, kk, n * 512:(n + 1) * 512],
                                start=(kk == 0), stop=(kk == ND - 1))
                    C.inc(i2, 'tc')
                tc_h = C.cnt['tc']

                @block.vector
                def _(vector):
                    vector = C.gv(vector)
                    vector.wait_ge(C.s('tc'), tc_h)
                    for n in range(4):
                        vector.tensor_copy(
                            out=wa_ckv[:, m2 * 2048 + n * 512:m2 * 2048 + (n + 1) * 512],
                            in_=pb[n][:, :])

            for n in range(4):
                @block.tensor
                def _(tensor):
                    tensor.wait_ge(C.s('vg'), C.cnt['vg'])
                    for kk in range(ND):
                        i2 = tensor.matmul(
                            out=pb[0][0:32, :],
                            lhsT=wa_kr.rearrange("p (a c) -> p a c", a=ND)[:, kk, :],
                            rhs=a8.rearrange("p (j c) -> p j c", j=ND)[:, kk, n * 512:(n + 1) * 512],
                            start=(kk == 0), stop=(kk == ND - 1))
                    C.inc(i2, 'tc')
                tc_h = C.cnt['tc']

                @block.vector
                def _(vector):
                    vector = C.gv(vector)
                    vector.wait_ge(C.s('tc'), tc_h)
                    vector.tensor_tensor(out=skr[0:32, n * 512:(n + 1) * 512],
                                         in0=pb[0][0:32, :],
                                         in1=c_rope[0:32, n * 512:(n + 1) * 512],
                                         op=ALU.mult)

            vg_v = []
            tc_v = []
            ds_v = []
            for t in range(NT):
                p = t % 2
                slot = sbq[:, p * 1024:(p + 1) * 1024]

                @block.tensor
                def _(tensor):
                    if t >= 2:
                        tensor.wait_ge(C.s('vg'), vg_v[t - 2])
                    for k2 in range(2):
                        for n2 in range(2):
                            i2 = tensor.matmul(
                                out=pb[4 + p * 2 + n2][:, :],
                                lhsT=wa_ckv[:, k2 * 2048 + t * 128:k2 * 2048 + (t + 1) * 128],
                                rhs=wa_uv[:, k2 * 1024 + n2 * 512:k2 * 1024 + (n2 + 1) * 512],
                                start=(k2 == 0), stop=(k2 == 1))
                    C.inc(i2, 'tc')
                tc_v.append(C.cnt['tc'])

                @block.vector
                def _(vector):
                    vector = C.gv(vector)
                    vector.wait_ge(C.s('tc'), tc_v[t])
                    if t >= 2:
                        vector.wait_ge(C.s('ds'), ds_v[t - 2])
                    vector.tensor_copy(out=slot[:, 0:512], in_=pb[4 + p * 2][:, :])
                    vector.tensor_copy(out=slot[:, 512:1024], in_=pb[4 + p * 2 + 1][:, :])
                vg_v.append(C.cnt['vg'])

                @block.sync
                def _(sync):
                    sync.wait_ge(C.s('vg'), vg_v[t])
                    C.dma(sync, z_v[t * 128:(t + 1) * 128, :], slot, 'ds')
                ds_v.append(C.cnt['ds'])

            vg_k = []
            ds_k = []
            for h in range(HN):
                p = h % 2
                ktile = skh[p]

                @block.vector
                def _(vector):
                    vector = C.gv(vector)
                    if h >= 2:
                        vector.wait_ge(C.s('ds'), ds_k[h - 2])
                    vector.tensor_copy(out=ktile[64:96, :], in_=skr[0:32, :])
                vg_kr = C.cnt['vg']

                @block.tensor
                def _(tensor):
                    tensor.wait_ge(C.s('vg'), C.cnt['vg'] if h == 0 else vg_k[h - 1])
                    for n in range(4):
                        for k2 in range(2):
                            i2 = tensor.matmul(
                                out=pb[n][0:64, :],
                                lhsT=wa_uk[:, k2 * 1024 + h * 64:k2 * 1024 + (h + 1) * 64],
                                rhs=wa_ckv[:, k2 * 2048 + n * 512:k2 * 2048 + (n + 1) * 512],
                                start=(k2 == 0), stop=(k2 == 1))
                        if n == 3:
                            C.inc(i2, 'tc')
                tc_k = C.cnt['tc']

                @block.vector
                def _(vector):
                    vector = C.gv(vector)
                    vector.wait_ge(C.s('tc'), tc_k)
                    for n in range(4):
                        vector.tensor_copy(out=ktile[0:64, n * 512:(n + 1) * 512],
                                           in_=pb[n][0:64, :])
                vg_k.append(C.cnt['vg'])

                @block.sync
                def _(sync):
                    sync.wait_ge(C.s('vg'), vg_k[h])
                    C.dma(sync, z_kT[h * HD:(h + 1) * HD, :], ktile[0:HD, :], 'ds')
                ds_k.append(C.cnt['ds'])

            @block.sync
            def _(sync):
                C.w(sync, 'vg', 'tc', 'sc', 'ds')

            dg_we = []
            tc_we = []
            for n3 in range(3):
                for kp in range(32):
                    p = kp % 2
                    lslot = wa_st[:, p * 1024:(p + 1) * 1024]
                    rslot = wa_st[:, 2048 + p * 512:2048 + (p + 1) * 512]

                    @block.gpsimd
                    def _(gpsimd):
                        if len(tc_we) >= 2:
                            gpsimd.wait_ge(C.s('tc'), tc_we[-2])
                        C.dma(gpsimd, lslot.rearrange("p (a c) -> p a c", a=ND),
                              wdqT[kp * 128:(kp + 1) * 128, :].rearrange("p (a c) -> p a c", a=ND),
                              'dg')
                        C.dma(gpsimd, rslot,
                              wcatq[kp * 128:(kp + 1) * 128, n3 * 512:(n3 + 1) * 512],
                              'dg')
                    dg_we.append(C.cnt['dg'])

                    @block.tensor
                    def _(tensor):
                        tensor.wait_ge(C.s('dg'), dg_we[-1])
                        if kp == 0:
                            tensor.wait_ge(C.s('vg'), C.cnt['vg'])
                        for m in range(8):
                            i2 = tensor.matmul(
                                out=pb[m][:, :],
                                lhsT=lslot.rearrange("p (a c) -> p a c", a=ND)[:, m, :],
                                rhs=rslot,
                                start=(kp == 0), stop=(kp == 31))
                        C.inc(i2, 'tc')
                    tc_we.append(C.cnt['tc'])

                @block.vector
                def _(vector):
                    vector = C.gv(vector)
                    vector.wait_ge(C.s('tc'), tc_we[-1])
                    for m in range(8):
                        vector.tensor_copy(
                            out=weff[:, m * NQ + n3 * 512:m * NQ + (n3 + 1) * 512],
                            in_=pb[m][:, :])

            vg_q = []
            ds_q = []
            for half in range(2):
                for n in range(4):
                    @block.tensor
                    def _(tensor):
                        tensor.wait_ge(C.s('vg'), C.cnt['vg'])
                        for kk in range(ND):
                            for h8 in range(8):
                                i2 = tensor.matmul(
                                    out=pb[h8][0:HD, :],
                                    lhsT=weff[:, kk * NQ + (half * 8 + h8) * HD:kk * NQ + (half * 8 + h8 + 1) * HD],
                                    rhs=a8.rearrange("p (j c) -> p j c", j=ND)[:, kk, n * 512:(n + 1) * 512],
                                    start=(kk == 0), stop=(kk == ND - 1))
                        C.inc(i2, 'tc')
                    tc_q = C.cnt['tc']

                    for h8 in range(8):
                        p2 = h8 % 2
                        slot = sbq[:, p2 * 1024:p2 * 1024 + 512]

                        @block.vector
                        def _(vector):
                            vector = C.gv(vector)
                            vector.wait_ge(C.s('tc'), tc_q)
                            if len(ds_q) >= 2:
                                vector.wait_ge(C.s('ds'), ds_q[-2])
                            vector.tensor_copy(out=slot[0:64, :], in_=pb[h8][0:64, :])
                            vector.tensor_tensor(out=slot[64:96, :],
                                                 in0=pb[h8][64:96, :],
                                                 in1=c_rope[64:96, n * 512:(n + 1) * 512],
                                                 op=ALU.mult)
                        vg_q.append(C.cnt['vg'])

                        @block.sync
                        def _(sync):
                            sync.wait_ge(C.s('vg'), vg_q[-1])
                            C.dma(sync, z_qT[(half * 8 + h8) * HD:(half * 8 + h8 + 1) * HD,
                                             n * 512:(n + 1) * 512],
                                  slot[0:HD, :], 'ds')
                        ds_q.append(C.cnt['ds'])

            ds_hd = []
            tc_end = []
            vg_hd = []
            for h in range(HN):
                hp = h % 2
                ybase = 0 if hp == 0 else 64
                dbase = 96 if hp == 0 else 32

                @block.sync
                def _(sync):
                    if h == 0:
                        C.w(sync, 'vg')
                    if h >= 2:
                        sync.wait_ge(C.s('tc'), tc_end[h - 2])
                    C.dma(sync, sqh[hp][0:HD, :], z_qT[h * HD:(h + 1) * HD, :], 'ds')
                    C.dma(sync, skh[hp][0:HD, :], z_kT[h * HD:(h + 1) * HD, :], 'ds')
                    C.dma(sync, svh[hp].rearrange("p (a c) -> p a c", a=NT),
                          z_v[:, h * HS:(h + 1) * HS].rearrange("(a p) c -> p a c", p=128),
                          'ds')
                ds_hd.append(C.cnt['ds'])

                steps = []
                for ki in range(NT):
                    nq = T - 128 * ki
                    for qc in range((nq + 511) // 512):
                        w_ = min(512, nq - qc * 512)
                        steps.append((ki, qc, w_))
                tc0 = C.cnt['tc']
                sc0 = C.cnt['sc']

                @block.tensor
                def _(tensor):
                    tensor.wait_ge(C.s('ds'), ds_hd[h])
                    for si, (ki, qc, w_) in enumerate(steps):
                        if si >= 2:
                            tensor.wait_ge(C.s('sc'), sc0 + si - 1)
                        else:
                            tensor.wait_ge(C.s('sc'), sc0)
                        i2 = tensor.matmul(
                            out=pb[si % 2][:, 0:w_],
                            lhsT=skh[hp][0:HD, ki * 128:(ki + 1) * 128],
                            rhs=sqh[hp][0:HD, 128 * ki + qc * 512:128 * ki + qc * 512 + w_],
                            start=True, stop=True)
                        C.inc(i2, 'tc')

                @block.scalar
                def _(scalar):
                    for si, (ki, qc, w_) in enumerate(steps):
                        scalar.wait_ge(C.s('tc'), tc0 + si + 1)
                        i2 = scalar.activation(
                            out=pTa[:, PT_OFF[ki] + qc * 512:PT_OFF[ki] + qc * 512 + w_],
                            in_=pb[si % 2][:, 0:w_],
                            func=AF.Exp, bias=0.0, scale=ISQ96)
                        C.inc(i2, 'sc')
                sc_exp = C.cnt['sc']

                @block.vector
                def _(vector):
                    vector = C.gv(vector)
                    vector.wait_ge(C.s('sc'), sc_exp)
                    for ki in range(NT):
                        vector.tensor_tensor(
                            out=pTa[:, PT_OFF[ki]:PT_OFF[ki] + 128],
                            in0=pTa[:, PT_OFF[ki]:PT_OFF[ki] + 128],
                            in1=c_lt[:, :], op=ALU.mult)
                vg_mask = C.cnt['vg']

                @block.tensor
                def _(tensor):
                    tensor.wait_ge(C.s('vg'), vg_mask)
                    for n in range(4):
                        for ki in range(4 * n + 4):
                            if 128 * ki <= 512 * n:
                                rhs = pTa[:, PT_OFF[ki] + 512 * n - 128 * ki:
                                          PT_OFF[ki] + 512 * n - 128 * ki + 512]
                                c0, c1 = 0, 512
                                st_ = (ki == 0)
                            else:
                                off = 128 * ki - 512 * n
                                rhs = pTa[:, PT_OFF[ki]:PT_OFF[ki] + 512 - off]
                                c0, c1 = off, 512
                                st_ = False
                            tensor.matmul(out=pb[2 + n][ybase:ybase + 64, c0:c1],
                                          lhsT=svh[hp][:, ki * 64:(ki + 1) * 64],
                                          rhs=rhs, start=st_, stop=(ki == 4 * n + 3))
                            i2 = tensor.matmul(out=pb[2 + n][dbase:dbase + 32, c0:c1],
                                               lhsT=c_ones32[:, :],
                                               rhs=rhs, start=st_, stop=(ki == 4 * n + 3),
                                               tile_position=(0, dbase))
                    C.inc(i2, 'tc')
                tc_pv = C.cnt['tc']

                @block.vector
                def _(vector):
                    vector = C.gv(vector)
                    vector.wait_ge(C.s('tc'), tc_pv)
                    for n in range(4):
                        vector.tensor_copy(out=sd1[32 * n:32 * n + 1, 0:512],
                                           in_=pb[2 + n][dbase:dbase + 1, :])
                    vector.reciprocal(out=sd1[:, 512:1024], in_=sd1[:, 0:512])
                vg_rd = C.cnt['vg']

                for n in range(4):
                    @block.tensor
                    def _(tensor):
                        tensor.wait_ge(C.s('vg'), vg_rd if n == 0 else C.cnt['vg'])
                        i2 = tensor.matmul(out=pb[6][ybase:ybase + 64, :],
                                           lhsT=c_onesf[32 * n:32 * n + 1, :],
                                           rhs=sd1[32 * n:32 * n + 1, 512:1024],
                                           start=True, stop=True,
                                           tile_position=(32 * n, ybase))
                        C.inc(i2, 'tc')
                    tc_bc = C.cnt['tc']

                    @block.vector
                    def _(vector):
                        vector = C.gv(vector)
                        vector.wait_ge(C.s('tc'), tc_bc)
                        vector.tensor_copy(out=sd1[ybase:ybase + 64, 1024:1536],
                                           in_=pb[6][ybase:ybase + 64, :])
                        vector.tensor_tensor(
                            out=a8.rearrange("p (j c) -> p j c", j=ND)[:, h // 2, n * 512:(n + 1) * 512][ybase:ybase + 64, :],
                            in0=pb[2 + n][ybase:ybase + 64, :],
                            in1=sd1[ybase:ybase + 64, 1024:1536],
                            op=ALU.mult)
                vg_hd.append(C.cnt['vg'])
                tc_end.append(C.cnt['tc'])

            if dbg:
                vg_now = C.cnt['vg']

                @block.sync
                def _(sync):
                    sync.wait_ge(C.s('vg'), vg_now)
                    C.dma(sync, d_a8, a8[:, :], 'ds')

            tc_wo = []
            vg_wo = []
            ds_wo = []
            ds_xo = []
            for t in range(NT):
                p = t % 2
                banks = pb[0:2] if p == 0 else pb[2:4]
                xres = stg[:, p * 1024:(p + 1) * 1024]
                outt = sd1[:, p * 1024:(p + 1) * 1024]

                @block.sync
                def _(sync):
                    if t == 0:
                        C.w(sync, 'vg', 'dg')
                    if t >= 2:
                        sync.wait_ge(C.s('vg'), vg_wo[t - 2])
                        sync.wait_ge(C.s('dg'), ds_xo[t - 2])
                    C.dma(sync, xres, z_x1[t * 128:(t + 1) * 128, :], 'ds')
                ds_wo.append(C.cnt['ds'])

                @block.tensor
                def _(tensor):
                    if t == 0:
                        tensor.wait_ge(C.s('vg'), C.cnt['vg'])
                        tensor.wait_ge(C.s('dg'), dg_w)
                    if t >= 2:
                        tensor.wait_ge(C.s('vg'), vg_wo[t - 2])
                    for kk in range(ND):
                        for n2 in range(2):
                            i2 = tensor.matmul(
                                out=banks[n2][:, :],
                                lhsT=a8.rearrange("p (j c) -> p j c", j=ND)[:, kk, t * 128:(t + 1) * 128],
                                rhs=wa_wo[:, kk * 1024 + n2 * 512:kk * 1024 + (n2 + 1) * 512],
                                start=(kk == 0), stop=(kk == ND - 1))
                    C.inc(i2, 'tc')
                tc_wo.append(C.cnt['tc'])

                @block.vector
                def _(vector):
                    vector = C.gv(vector)
                    vector.wait_ge(C.s('tc'), tc_wo[t])
                    vector.wait_ge(C.s('ds'), ds_wo[t])
                    for n2 in range(2):
                        vector.scalar_tensor_tensor(
                            out=outt[:, n2 * 512:(n2 + 1) * 512],
                            in0=banks[n2][:, :], scalar=1.0,
                            in1=xres[:, n2 * 512:(n2 + 1) * 512],
                            op0=ALU.mult, op1=ALU.add)
                vg_wo.append(C.cnt['vg'])

                @block.gpsimd
                def _(gpsimd):
                    gpsimd.wait_ge(C.s('vg'), vg_wo[t])
                    C.dma(gpsimd, z_x2[t * 128:(t + 1) * 128, :], outt, 'dg')
                ds_xo.append(C.cnt['dg'])

        emit_ffn(x_in, z_x1)
        emit_attention()
        emit_ffn(z_x2, out, tag=2)

        if dbg:
            @block.sync
            def _(sync):
                C.w(sync, 'vg', 'sc')
                C.dma(sync, d_stat, sstat[:, 0:64], 'ds')

        @block.sync
        def _(sync):
            C.w(sync, 'ds', 'dg')

    return nc


_NC_CACHE = {}


def kernel(L0_attn_x, L0_attn_W_DQ, L0_attn_W_UQ, L0_attn_W_DKV, L0_attn_W_UK,
           L0_attn_W_UV, L0_attn_W_QR, L0_attn_W_KR, L0_attn_W_O,
           L0_ffn_w_gate, L0_ffn_w_up, L0_ffn_w_down):
    f = np.float32
    x = np.ascontiguousarray(np.asarray(L0_attn_x, dtype=f))
    wdqT = np.ascontiguousarray(np.asarray(L0_attn_W_DQ, dtype=f).T)
    wuq = np.asarray(L0_attn_W_UQ, dtype=f)
    wqr = np.asarray(L0_attn_W_QR, dtype=f)
    wcatq = np.ascontiguousarray(
        np.concatenate([wuq.reshape(4096, HN, HS), wqr.reshape(4096, HN, DR)],
                       axis=2).reshape(4096, NQ))
    wdkv = np.ascontiguousarray(np.asarray(L0_attn_W_DKV, dtype=f))
    wuk = np.ascontiguousarray(np.asarray(L0_attn_W_UK, dtype=f))
    wuv = np.ascontiguousarray(np.asarray(L0_attn_W_UV, dtype=f))
    wkr = np.ascontiguousarray(np.asarray(L0_attn_W_KR, dtype=f))
    wo = np.ascontiguousarray(np.asarray(L0_attn_W_O, dtype=f))
    wg = np.ascontiguousarray(np.asarray(L0_ffn_w_gate, dtype=f))
    wu = np.ascontiguousarray(np.asarray(L0_ffn_w_up, dtype=f))
    wd = np.ascontiguousarray(np.asarray(L0_ffn_w_down, dtype=f))

    t = np.arange(T, dtype=f)
    inv = (f(1.0) / (f(10000.0) ** (np.arange(0, DR, 2, dtype=f) / f(DR)))).astype(f)
    freqs = (t[:, None] * inv[None, :]).astype(f)
    emb = np.concatenate([freqs, freqs], axis=-1)
    rope = (np.cos(emb) + np.sin(emb)).astype(f)
    ropeT = np.ascontiguousarray(rope.T)

    ltm = np.zeros((128, 128), dtype=f)
    for kk in range(128):
        ltm[kk, kk:] = 1.0
    onescol = np.zeros((128, 32), dtype=f)
    onescol[:, 0] = 1.0
    onesf = np.ones((128, 64), dtype=f)
    magicv = np.zeros((128, 2), dtype=f); magicv[:, 0] = f(12582912.0); magicv[:, 1] = f(-12582912.0)
    eye = np.eye(128, dtype=f)

    if "nc" not in _NC_CACHE:
        _NC_CACHE["nc"] = build_nc()
    nc = _NC_CACHE["nc"]

    shared = dict(wdqT=wdqT, wcatq=wcatq, wdkv=wdkv, wuk=wuk, wuv=wuv, wkr=wkr,
                  wo=wo, wg=wg, wu=wu, wd=wd, ropeT=ropeT, ltmask=ltm,
                  onescol=onescol, onesf=onesf, magicv=magicv, eye=eye)
    in_maps = [dict(shared, x=np.ascontiguousarray(x[c])) for c in range(8)]
    res = run_bass_kernel_spmd(nc, in_maps, core_ids=list(range(8)))
    return np.stack([np.asarray(res.results[c]["out"]) for c in range(8)], axis=0)


# version 09
# speedup vs baseline: 1.1780x; 1.0048x over previous
import numpy as np
import ml_dtypes
from contextlib import ExitStack

import concourse.bass as bass
import concourse.mybir as mybir
from concourse.bass_utils import run_bass_kernel_spmd

F32 = mybir.dt.float32
BF16 = mybir.dt.bfloat16
AF = mybir.ActivationFunctionType
ALU = mybir.AluOpType
X = mybir.AxisListType.X

T = 2048
D = 1024
HN = 16
HS = 64
DR = 32
HD = HS + DR
DFF = 3072
NT = T // 128
ND = D // 128
NI = DFF // 128
NQ = HN * HD

MAGIC = float(np.float32(12582912.0))
INV7 = float(np.float32(1.0) / np.float32(7.0))
INV32767 = float(np.float32(1.0) / np.float32(32767.0))
INV1024 = float(np.float32(1.0) / np.float32(1024.0))
ISQ96 = float(np.float32(1.0) / np.float32(np.sqrt(np.float32(96.0))))
EPS_LN = 1e-5
EPS_Q = 1e-12

PT_OFF = [0] * (NT + 1)
for _ki in range(NT):
    PT_OFF[_ki + 1] = PT_OFF[_ki] + (T - 128 * _ki)
PT_TOT = PT_OFF[NT]


class K:
    def __init__(self, block, sems):
        self.block = block
        self.sems = sems
        self.cnt = {k: 0 for k in sems}

    def s(self, name):
        return self.sems[name]

    def inc(self, instr, name, n=1):
        self.cnt[name] += n
        return instr.then_inc(self.sems[name], n)

    def dma(self, eng, out, in_, name, **kw):
        i = eng.dma_start(out=out, in_=in_, **kw)
        self.cnt[name] += 16
        i.then_inc(self.sems[name], 16)
        return i

    def w(self, eng, *names):
        for name in names:
            eng.wait_ge(self.sems[name], self.cnt[name])

    def gv(self, vector):
        C = self

        class GV:
            def wait_ge(self, *a, **k):
                return vector.wait_ge(*a, **k)

            def __getattr__(self, name):
                fn = getattr(vector, name)

                def wrapped(*a, **k):
                    vector.wait_ge(C.s('vg'), C.cnt['vg'])
                    i = fn(*a, **k)
                    C.inc(i, 'vg')
                    return i
                return wrapped
        return GV()

    def gs(self, scalar):
        C = self

        class GS:
            def wait_ge(self, *a, **k):
                return scalar.wait_ge(*a, **k)

            def __getattr__(self, name):
                fn = getattr(scalar, name)

                def wrapped(*a, **k):
                    scalar.wait_ge(C.s('sc'), C.cnt['sc'])
                    i = fn(*a, **k)
                    C.inc(i, 'sc')
                    return i
                return wrapped
        return GS()


def build_nc(dbg=False):
    nc = bass.Bass("TRN2", target_bir_lowering=False, debug=False,
                   detect_race_conditions=False)

    def _dz(name, shape, dt):
        if dbg:
            return nc.dram_tensor(name, shape, dt, kind="ExternalOutput").ap()
        return nc.dram_tensor(name, shape, dt).ap()

    x_in = nc.dram_tensor("x", [T, D], F32, kind="ExternalInput").ap()
    wdqT = nc.dram_tensor("wdqT", [4096, D], F32, kind="ExternalInput").ap()
    wcatq = nc.dram_tensor("wcatq", [4096, NQ], F32, kind="ExternalInput").ap()
    wdkv = nc.dram_tensor("wdkv", [D, 256], F32, kind="ExternalInput").ap()
    wuk = nc.dram_tensor("wuk", [256, D], F32, kind="ExternalInput").ap()
    wuv = nc.dram_tensor("wuv", [256, D], F32, kind="ExternalInput").ap()
    wkr = nc.dram_tensor("wkr", [D, DR], F32, kind="ExternalInput").ap()
    wo = nc.dram_tensor("wo", [D, D], F32, kind="ExternalInput").ap()
    wg = nc.dram_tensor("wg", [DFF, D], F32, kind="ExternalInput").ap()
    wu = nc.dram_tensor("wu", [DFF, D], F32, kind="ExternalInput").ap()
    wd = nc.dram_tensor("wd", [D, DFF], F32, kind="ExternalInput").ap()
    ropeT_in = nc.dram_tensor("ropeT", [DR, T], F32, kind="ExternalInput").ap()
    ltmask = nc.dram_tensor("ltmask", [128, 128], F32, kind="ExternalInput").ap()
    onescol = nc.dram_tensor("onescol", [128, 32], F32, kind="ExternalInput").ap()
    onesf = nc.dram_tensor("onesf", [128, 64], F32, kind="ExternalInput").ap()
    magicv = nc.dram_tensor("magicv", [128, 2], F32, kind="ExternalInput").ap()
    eye_in = nc.dram_tensor("eye", [128, 128], F32, kind="ExternalInput").ap()
    out = nc.dram_tensor("out", [T, D], F32, kind="ExternalOutput").ap()

    z_x1 = _dz("z_x1", [T, D], F32)
    z_x2 = _dz("z_x2", [T, D], F32)
    z_wgT = _dz("z_wgT", [NI, D, 128], BF16)
    z_wuT = _dz("z_wuT", [NI, D, 128], BF16)
    z_wdT = _dz("z_wdT", [NI, 128, D], BF16)
    z_upraw = _dz("z_upraw", [DFF, T], F32)
    z_graw = _dz("z_graw", [DFF, T], F32)
    z_qu = _dz("z_qu", [DFF, T], mybir.dt.int16)
    z_g2 = _dz("z_g2", [DFF, T], F32)
    z_u = _dz("z_u", [DFF, T], F32)
    z_qT = _dz("z_qT", [NQ, T], BF16)
    z_kT = _dz("z_kT", [NQ, T], BF16)
    z_v = _dz("z_v", [T, D], BF16)
    d_a8 = _dz("d_a8", [128, 8 * T], BF16) if dbg else None
    d_a8f = _dz("d_a8f", [128, 8 * T], BF16) if dbg else None
    d_stat = _dz("d_stat", [128, 64], F32) if dbg else None
    cc_in = nc.dram_tensor("cc_in", [1, 2], F32).ap()
    cc_out = nc.dram_tensor("cc_out", [1, 2], F32).ap()
    z_sc = nc.dram_tensor("z_sc", [1, 16], F32).ap()

    with ExitStack() as ctx:
        def sb(name, shape, dt):
            return ctx.enter_context(nc.sbuf_tensor(name, shape, dt))

        c_rope = sb("c_rope", [128, T], F32)
        c_lt = sb("c_lt", [128, 128], BF16)
        c_eye = sb("c_eye", [128, 128], BF16)
        c_ones32 = sb("c_ones32", [128, 32], BF16)
        c_onesf = sb("c_onesf", [128, 64], F32)
        c_magic = sb("c_magic", [128, 2], F32)
        sval = sb("sval", [1, 16], F32)
        sscale = sb("sscale", [128, 16], F32)
        streeg = sb("streeg", [128, 4], F32)
        s32a = sb("s32a", [128, 32], F32)
        s32b = sb("s32b", [128, 32], F32)
        sacc = sb("sacc", [128, 4], F32)
        sstat = sb("sstat", [128, 4 * NT], F32)
        spad = sb("spad", [128, 16], F32)
        af = sb("af", [128, 16384], F32)
        a8 = sb("a8", [128, 8 * T], BF16)
        wa = sb("wa", [128, 24576], BF16)
        stg = sb("stg", [128, 4096], F32)
        sbq = sb("sbq", [128, 2048], BF16)
        sqh = [sb(f"sqh{i}", [HD, T], BF16) for i in range(2)]
        skh = [sb(f"skh{i}", [HD, T], BF16) for i in range(2)]
        svh = [sb(f"svh{i}", [128, HN * HS], BF16) for i in range(2)]
        skr = sb("skr", [DR, T], BF16)
        sd1 = sb("sd1", [128, 2048], F32)

        pb = [ctx.enter_context(nc.psum_tensor(f"pb{i}", [128, 512], F32))
              for i in range(8)]
        pt6 = pb[6][:, :].bitcast(BF16)
        pt7 = pb[7][:, :].bitcast(BF16)
        pt4 = pb[4][:, :].bitcast(BF16)
        pt5 = pb[5][:, :].bitcast(BF16)
        pTa = af[:, 0:8704].bitcast(BF16)
        weff = af[:, 0:6144].bitcast(BF16)

        sems = {}
        for name in ["ds", "dg", "tc", "sc", "cc", "vg"]:
            sems[name] = ctx.enter_context(nc.semaphore(name))
        lo = min(s.num for s in sems.values())
        hi = max(s.num for s in sems.values()) + 1
        nc.gpsimd.sem_clear(range(lo, hi))
        nc.all_engine_barrier()
        block = ctx.enter_context(nc.Block())
        C = K(block, sems)

        @block.sync
        def _(sync):
            C.dma(sync, c_rope[0:32, :], ropeT_in, 'ds')
            C.dma(sync, c_rope[64:96, :], ropeT_in, 'ds')
            C.dma(sync, c_onesf[:, :], onesf, 'ds')
            C.dma(sync, c_magic[:, :], magicv, 'ds')

        @block.gpsimd
        def _(gpsimd):
            C.dma(gpsimd, c_lt[:, :], ltmask, 'dg')
            C.dma(gpsimd, c_eye[:, :], eye_in, 'dg')
            C.dma(gpsimd, c_ones32[:, :], onescol, 'dg')
        dg_consts = C.cnt['dg']

        def tree_max(vector, src_col, dst_col):
            vector.tensor_copy(out=streeg[:64, 0:1], in_=sacc[64:128, src_col:src_col + 1])
            vector.tensor_tensor(out=streeg[:64, 1:2], in0=sacc[:64, src_col:src_col + 1],
                                 in1=streeg[:64, 0:1], op=ALU.max)
            vector.tensor_copy(out=streeg[:32, 2:3], in_=streeg[32:64, 1:2])
            vector.tensor_tensor(out=s32a[:32, 0:1], in0=streeg[:32, 1:2],
                                 in1=streeg[:32, 2:3], op=ALU.max)
            vector.transpose(out=s32b[:32, :32], in_=s32a[:32, :32])
            return vector.reduce_max(out=sval[0:1, dst_col:dst_col + 1],
                                     in_=s32b[0:1, :32], axis=X)

        def emit_allreduce():
            vg_ready = C.cnt['vg']

            @block.gpsimd
            def _(gpsimd):
                gpsimd.wait_ge(C.s('vg'), vg_ready)
                C.dma(gpsimd, cc_in, sval[0:1, 12:14], 'dg')
                C.w(gpsimd, 'dg')
                i = gpsimd.collective_compute(
                    "AllReduce", ALU.max,
                    replica_groups=[list(range(8))],
                    ins=[cc_in.opt()],
                    outs=[cc_out.opt()],
                )
                C.inc(i, 'cc')
                gpsimd.wait_ge(C.s('cc'), C.cnt['cc'])
                C.dma(gpsimd, sval[0:1, 10:12], cc_out, 'dg')

        def emit_scales(build, ncols):
            dg_ready = C.cnt['dg']

            @block.vector
            def _(vector):
                vector = C.gv(vector)
                vector.wait_ge(C.s('dg'), dg_ready)
                build(vector)
            vg_here = C.cnt['vg']

            @block.sync
            def _(sync):
                sync.wait_ge(C.s('vg'), vg_here)
                C.dma(sync, z_sc[0:1, 0:ncols], sval[0:1, 0:ncols], 'ds')
                C.w(sync, 'ds')
                C.dma(sync, sscale[:, 0:ncols],
                      z_sc[0:1, 0:ncols].broadcast_to([128, ncols]), 'ds')
                C.dma(sync, spad[:], ltmask[:, 0:16], 'ds')

        def dequant_core(vector, buf, tq, bm, scl, rcp, outv, nblk):
            vector.reduce_max(out=bm.rearrange("p (a b) -> p a b", b=1),
                              in_=buf.rearrange("p (a b) -> p a b", b=32),
                              axis=X, apply_absolute_value=True)
            vector.tensor_scalar(out=scl, in0=bm, scalar1=INV7, scalar2=EPS_Q,
                                 op0=ALU.mult, op1=ALU.max)
            vector.reciprocal(out=rcp, in_=scl)
            vector.tensor_tensor(out=tq.rearrange("p (a b) -> p a b", b=32),
                                 in0=buf.rearrange("p (a b) -> p a b", b=32),
                                 in1=rcp[:, :, None].broadcast_to([128, nblk, 32]),
                                 op=ALU.mult)
            vector.tensor_scalar(out=tq, in0=tq, scalar1=MAGIC, scalar2=MAGIC,
                                 op0=ALU.add, op1=ALU.subtract)
            return vector.tensor_tensor(out=outv.rearrange("p (a b) -> p a b", b=32),
                                        in0=tq.rearrange("p (a b) -> p a b", b=32),
                                        in1=scl[:, :, None].broadcast_to([128, nblk, 32]),
                                        op=ALU.mult)

        def emit_dequant_gateup(src, dstT):
            vg_after = []
            tc_after = []
            sc_after = []
            ds_ld = []
            ds_st = []
            ptvs = [pt4, pt5, pt6, pt7]
            vg0 = C.cnt['vg']
            tc0 = C.cnt['tc']
            dg0 = C.cnt['dg']
            for i in range(NI):
                p = i % 4
                buf = af[:, p * 1024:(p + 1) * 1024]
                tq = af[:, 4096 + p * 1024:4096 + (p + 1) * 1024]
                bm = af[:, 8192 + p * 128:8192 + p * 128 + 32]
                scl = af[:, 8704 + p * 128:8704 + p * 128 + 32]
                rcp = af[:, 9216 + p * 128:9216 + p * 128 + 32]
                obf = a8[:, p * 1024:(p + 1) * 1024]
                ptv = ptvs[p]

                @block.sync
                def _(sync):
                    if i >= 4:
                        sync.wait_ge(C.s('vg'), vg_after[i - 4])
                    else:
                        sync.wait_ge(C.s('vg'), vg0)
                    C.dma(sync, buf, src[i * 128:(i + 1) * 128, :], 'ds')
                ds_ld.append(C.cnt['ds'])
                ds_lm = ds_ld[i] + (16 if i < NI - 1 else 16)

                @block.vector
                def _(vector):
                    vector = C.gv(vector)
                    vector.wait_ge(C.s('ds'), ds_lm)
                    if i >= 4:
                        vector.wait_ge(C.s('tc'), tc_after[i - 4])
                        vector.wait_ge(C.s('dg'), ds_st[i - 4])
                    else:
                        vector.wait_ge(C.s('tc'), tc0)
                        vector.wait_ge(C.s('dg'), dg0)
                    dequant_core(vector, buf, tq, bm, scl, rcp, obf, 32)
                vg_after.append(C.cnt['vg'])

                @block.tensor
                def _(tensor):
                    tensor.wait_ge(C.s('vg'), vg_after[i])
                    if i == 0:
                        tensor.wait_ge(C.s('dg'), dg_consts)
                    if i >= 4:
                        tensor.wait_ge(C.s('sc'), sc_after[i - 4])
                    for j in range(ND):
                        i2 = tensor.transpose(out=ptv[:, j * 128:(j + 1) * 128],
                                              in_=obf[:, j * 128:(j + 1) * 128],
                                              identity=c_eye[:, :])
                    C.inc(i2, 'tc')
                tc_after.append(C.cnt['tc'])

                @block.scalar
                def _(scalar):
                    scalar = C.gs(scalar)
                    scalar.wait_ge(C.s('tc'), tc_after[i])
                    scalar.copy(out=obf[:, :], in_=ptv[:, :])
                sc_after.append(C.cnt['sc'])

                @block.gpsimd
                def _(gpsimd):
                    gpsimd.wait_ge(C.s('sc'), sc_after[i])
                    C.dma(gpsimd, dstT[i].rearrange("(a p) c -> p a c", p=128),
                          obf.rearrange("p (a c) -> p a c", a=ND), 'dg')
                ds_st.append(C.cnt['dg'])

            @block.sync
            def _(sync):
                C.dma(sync, spad[:], ltmask[:, 0:16], 'ds')

        def emit_dequant_down(src, dstT):
            for r in range(ND):
                buf = af[:, 0:3072]
                tq = af[:, 4096:7168]
                bm = af[:, 8192:8288]
                scl = af[:, 8448:8544]
                rcp = af[:, 8704:8800]
                obf = af[:, 10240:11776].bitcast(BF16)

                @block.sync
                def _(sync):
                    C.w(sync, 'vg', 'tc')
                    C.dma(sync, buf, src[r * 128:(r + 1) * 128, :], 'ds')
                    C.dma(sync, spad[:], ltmask[:, 0:16], 'ds')
                ds_h = C.cnt['ds']

                @block.vector
                def _(vector):
                    vector = C.gv(vector)
                    vector.wait_ge(C.s('ds'), ds_h)
                    dequant_core(vector, buf, tq, bm, scl, rcp, obf, 96)
                vg_h = C.cnt['vg']

                for rnd in range(3):
                    ptv = pt6 if rnd % 2 == 0 else pt7
                    slot = sbq[:, (rnd % 2) * 1024:(rnd % 2 + 1) * 1024]

                    @block.tensor
                    def _(tensor):
                        if rnd == 0:
                            tensor.wait_ge(C.s('vg'), vg_h)
                            tensor.wait_ge(C.s('dg'), dg_consts)
                        tensor.wait_ge(C.s('dg'), C.cnt['dg'])
                        tensor.wait_ge(C.s('sc'), C.cnt['sc'])
                        for jj in range(8):
                            i2 = tensor.transpose(
                                out=ptv[:, jj * 128:(jj + 1) * 128],
                                in_=obf[:, (rnd * 8 + jj) * 128:(rnd * 8 + jj + 1) * 128],
                                identity=c_eye[:, :])
                        C.inc(i2, 'tc')
                    tc_h = C.cnt['tc']

                    @block.scalar
                    def _(scalar):
                        scalar = C.gs(scalar)
                        scalar.wait_ge(C.s('tc'), tc_h)
                        scalar.copy(out=slot, in_=ptv[:, :])
                    sc_h = C.cnt['sc']

                    @block.gpsimd
                    def _(gpsimd):
                        gpsimd.wait_ge(C.s('sc'), sc_h)
                        C.dma(gpsimd,
                              dstT[rnd * 8:(rnd + 1) * 8, :, r * 128:(r + 1) * 128]
                              .rearrange("a p c -> p a c"),
                              slot.rearrange("p (a c) -> p a c", a=8), 'dg')

        emit_dequant_gateup(wu, z_wuT)
        emit_dequant_gateup(wg, z_wgT)
        emit_dequant_down(wd, z_wdT)

        def emit_ln(src):
            ds_after = []
            for t in range(NT):
                xt = af[:, t * 1024:(t + 1) * 1024]

                @block.sync
                def _(sync):
                    if t == 0:
                        C.w(sync, 'vg', 'tc', 'sc', 'ds', 'dg')
                    C.dma(sync, xt, src[t * 128:(t + 1) * 128, :], 'ds')
                ds_after.append(C.cnt['ds'])

            @block.sync
            def _(sync):
                C.dma(sync, spad[:], ltmask[:, 0:16], 'ds')

            for t in range(NT):
                xt = af[:, t * 1024:(t + 1) * 1024]

                @block.vector
                def _(vector):
                    vector = C.gv(vector)
                    vector.wait_ge(C.s('ds'), ds_after[t] + 16)
                    vector.reduce_sum(out=sstat[:, t:t + 1], in_=xt, axis=X)
                    vector.tensor_single_scalar(out=sstat[:, NT + t:NT + t + 1],
                                                in_=sstat[:, t:t + 1],
                                                scalar=-INV1024, op=ALU.mult)
                vg_t = C.cnt['vg']

                @block.scalar
                def _(scalar):
                    scalar = C.gs(scalar)
                    scalar.wait_ge(C.s('vg'), vg_t)
                    scalar.activation(out=stg[:, (t % 2) * 1024:(t % 2 + 1) * 1024],
                                      in_=xt, func=AF.Square,
                                      bias=sstat[:, NT + t:NT + t + 1], scale=1.0,
                                      accum_out=sstat[:, 2 * NT + t:2 * NT + t + 1])
            sc_sq = C.cnt['sc']

            @block.vector
            def _(vector):
                vector = C.gv(vector)
                vector.wait_ge(C.s('sc'), sc_sq)
                vector.tensor_scalar(out=sstat[:, 2 * NT:3 * NT],
                                     in0=sstat[:, 2 * NT:3 * NT],
                                     scalar1=INV1024, scalar2=EPS_LN,
                                     op0=ALU.mult, op1=ALU.add)
            vg_var = C.cnt['vg']

            @block.scalar
            def _(scalar):
                scalar = C.gs(scalar)
                scalar.wait_ge(C.s('vg'), vg_var)
                scalar.sqrt(out=sstat[:, 2 * NT:3 * NT], in_=sstat[:, 2 * NT:3 * NT])
            sc_sqrt = C.cnt['sc']

            @block.vector
            def _(vector):
                vector = C.gv(vector)
                vector.wait_ge(C.s('sc'), sc_sqrt)
                vector.reciprocal(out=sstat[:, 2 * NT:3 * NT], in_=sstat[:, 2 * NT:3 * NT])
                vector.tensor_tensor(out=sstat[:, 3 * NT:4 * NT],
                                     in0=sstat[:, NT:2 * NT],
                                     in1=sstat[:, 2 * NT:3 * NT], op=ALU.mult)
            vg_rstd = C.cnt['vg']

            for t in range(NT):
                xt = af[:, t * 1024:(t + 1) * 1024]

                @block.scalar
                def _(scalar):
                    scalar = C.gs(scalar)
                    if t == 0:
                        scalar.wait_ge(C.s('vg'), vg_rstd)
                    scalar.activation(out=xt, in_=xt, func=AF.Identity,
                                      bias=sstat[:, 3 * NT + t:3 * NT + t + 1],
                                      scale=sstat[:, 2 * NT + t:2 * NT + t + 1])
                sc_t = C.cnt['sc']

                @block.vector
                def _(vector):
                    vector = C.gv(vector)
                    vector.wait_ge(C.s('sc'), sc_t)
                    vector.reduce_max(out=sacc[:, 1:2], in_=xt, axis=X,
                                      apply_absolute_value=True)
                    if t == 0:
                        vector.tensor_copy(out=sacc[:, 0:1], in_=sacc[:, 1:2])
                    else:
                        vector.tensor_tensor(out=sacc[:, 0:1], in0=sacc[:, 0:1],
                                             in1=sacc[:, 1:2], op=ALU.max)

        def emit_tiles_to_a8(quant):
            vg_evac = []
            tc_tr = []
            for t in range(NT):
                p = t % 2
                xt = af[:, t * 1024:(t + 1) * 1024]
                slot = sbq[:, p * 1024:(p + 1) * 1024]
                ptv = pt6 if p == 0 else pt7

                @block.vector
                def _(vector):
                    vector = C.gv(vector)
                    if t == 0:
                        C.w(vector, 'ds')
                    if t >= 2:
                        vector.wait_ge(C.s('tc'), tc_tr[t - 2])
                    if quant:
                        vector.tensor_scalar(out=xt, in0=xt, scalar1=sscale[:, 0:1],
                                             scalar2=MAGIC, op0=ALU.mult, op1=ALU.add)
                        vector.tensor_scalar(out=slot, in0=xt, scalar1=MAGIC,
                                             scalar2=sscale[:, 1:2],
                                             op0=ALU.subtract, op1=ALU.mult)
                    else:
                        vector.tensor_copy(out=slot, in_=xt)
                vg_q = C.cnt['vg']

                @block.tensor
                def _(tensor):
                    tensor.wait_ge(C.s('vg'), vg_q)
                    if t == 0:
                        tensor.wait_ge(C.s('dg'), dg_consts)
                    if t >= 2:
                        tensor.wait_ge(C.s('vg'), vg_evac[t - 2])
                    for j in range(ND):
                        i2 = tensor.transpose(out=ptv[:, j * 128:(j + 1) * 128],
                                              in_=slot[:, j * 128:(j + 1) * 128],
                                              identity=c_eye[:, :])
                    C.inc(i2, 'tc')
                tc_tr.append(C.cnt['tc'])

                @block.vector
                def _(vector):
                    vector = C.gv(vector)
                    vector.wait_ge(C.s('tc'), tc_tr[t])
                    vector.tensor_copy(
                        out=a8.rearrange("p (j c) -> p j c", j=ND)[:, :, t * 128:(t + 1) * 128],
                        in_=ptv.rearrange("p (j c) -> p j c", j=ND))
                vg_evac.append(C.cnt['vg'])

        def emit_ffn(src, dst, tag=1):
            emit_ln(src)

            @block.vector
            def _(vector):
                vector = C.gv(vector)
                tree_max(vector, 0, 12)
                vector.tensor_copy(out=sval[0:1, 13:14], in_=sval[0:1, 12:13])

            emit_allreduce()

            def scales_x(vector):
                vector.tensor_scalar(out=sval[0:1, 1:2], in0=sval[0:1, 10:11],
                                     scalar1=INV32767, scalar2=EPS_Q,
                                     op0=ALU.mult, op1=ALU.max)
                vector.reciprocal(out=sval[0:1, 0:1], in_=sval[0:1, 1:2])
            emit_scales(scales_x, 2)

            emit_tiles_to_a8(quant=True)

            if dbg and tag == 2:
                vg_now2 = C.cnt['vg']

                @block.sync
                def _(sync):
                    sync.wait_ge(C.s('vg'), vg_now2)
                    C.dma(sync, d_a8f, a8[:, :], 'ds')

            def gateup(wT, dst_raw, acc_col):
                vg_evac = []
                tc_mm = []
                ds_ld = []
                ds_st = []
                tc0 = C.cnt['tc']
                dg0 = C.cnt['dg']
                for i in range(NI):
                    p = i % 2
                    wtile = wa[:, p * 1024:(p + 1) * 1024]
                    banks = pb[0:4] if p == 0 else pb[4:8]
                    stgh = stg[:, 0:2048] if p == 0 else stg[:, 2048:4096]

                    @block.sync
                    def _(sync):
                        if i < 2:
                            sync.wait_ge(C.s('dg'), dg0)
                            sync.wait_ge(C.s('tc'), tc0)
                        if i >= 2:
                            sync.wait_ge(C.s('tc'), tc_mm[i - 2])
                        C.dma(sync, wtile.rearrange("p (a c) -> p a c", a=ND),
                              wT[i].rearrange("(a p) c -> p a c", p=128), 'ds')
                    ds_ld.append(C.cnt['ds'])

                    @block.tensor
                    def _(tensor):
                        tensor.wait_ge(C.s('ds'), ds_ld[i] + 16)
                        if i == 0:
                            tensor.wait_ge(C.s('vg'), C.cnt['vg'])
                        if i >= 2:
                            tensor.wait_ge(C.s('vg'), vg_evac[i - 2])
                        for kk in range(ND):
                            for n in range(4):
                                i2 = tensor.matmul(
                                    out=banks[n][:, :],
                                    lhsT=wtile.rearrange("p (a c) -> p a c", a=ND)[:, kk, :],
                                    rhs=a8.rearrange("p (j c) -> p j c", j=ND)[:, kk, n * 512:(n + 1) * 512],
                                    start=(kk == 0), stop=(kk == ND - 1))
                        C.inc(i2, 'tc')
                    tc_mm.append(C.cnt['tc'])

                    @block.vector
                    def _(vector):
                        vector = C.gv(vector)
                        vector.wait_ge(C.s('tc'), tc_mm[i])
                        if i >= 2:
                            vector.wait_ge(C.s('dg'), ds_st[i - 2])
                        else:
                            C.w(vector, 'dg')
                        for n in range(4):
                            vector.tensor_copy(out=stgh[:, n * 512:(n + 1) * 512],
                                               in_=banks[n][:, :])
                        vector.reduce_max(out=sacc[:, 1:2], in_=stgh,
                                          axis=X, apply_absolute_value=True)
                        if i == 0:
                            vector.tensor_copy(out=sacc[:, acc_col:acc_col + 1],
                                               in_=sacc[:, 1:2])
                        else:
                            vector.tensor_tensor(out=sacc[:, acc_col:acc_col + 1],
                                                 in0=sacc[:, acc_col:acc_col + 1],
                                                 in1=sacc[:, 1:2], op=ALU.max)
                    vg_evac.append(C.cnt['vg'])

                    @block.gpsimd
                    def _(gpsimd):
                        gpsimd.wait_ge(C.s('vg'), vg_evac[i])
                        C.dma(gpsimd, dst_raw[i * 128:(i + 1) * 128, :], stgh, 'dg')
                    ds_st.append(C.cnt['dg'])

                @block.sync
                def _(sync):
                    C.dma(sync, spad[:], ltmask[:, 0:16], 'ds')

            gateup(z_wuT, z_upraw, 2)
            gateup(z_wgT, z_graw, 3)

            @block.vector
            def _(vector):
                vector = C.gv(vector)
                tree_max(vector, 2, 12)
                tree_max(vector, 3, 13)

            emit_allreduce()

            def scales_ug(vector):
                vector.tensor_scalar(out=sval[0:1, 1:2], in0=sval[0:1, 10:11],
                                     scalar1=INV32767, scalar2=EPS_Q,
                                     op0=ALU.mult, op1=ALU.max)
                vector.tensor_scalar(out=sval[0:1, 3:4], in0=sval[0:1, 11:12],
                                     scalar1=INV32767, scalar2=EPS_Q,
                                     op0=ALU.mult, op1=ALU.max)
                vector.reciprocal(out=sval[0:1, 0:1], in_=sval[0:1, 1:2])
                vector.reciprocal(out=sval[0:1, 2:3], in_=sval[0:1, 3:4])
                vector.tensor_single_scalar(out=sval[0:1, 4:5], in_=sval[0:1, 3:4],
                                            scalar=float(2.0 ** -16), op=ALU.mult)
            emit_scales(scales_ug, 5)

            def a8f(k):
                return a8[:, k * 4096:(k + 1) * 4096].bitcast(F32)

            def waf(k):
                return wa[:, k * 4096:(k + 1) * 4096].bitcast(F32)
            vg_b = []
            sc_b = []
            ds_b = []
            dg_b = []
            for i in range(NI):
                s4 = i % 4
                BU = af[:, s4 * 2048:(s4 + 1) * 2048]
                BG = af[:, (4 + s4) * 2048:(5 + s4) * 2048]
                TG = a8f(s4)
                SSG = waf(s4)
                T1 = stg[:, 0:2048]

                @block.sync
                def _(sync):
                    if i == 0:
                        C.w(sync, 'vg', 'tc', 'sc', 'dg')
                    if i >= 4:
                        sync.wait_ge(C.s('dg'), dg_b[i - 4])
                    C.dma(sync, BU, z_upraw[i * 128:(i + 1) * 128, :], 'ds')
                    C.dma(sync, BG, z_graw[i * 128:(i + 1) * 128, :], 'ds')
                ds_b.append(C.cnt['ds'])
                ds_bm = ds_b[i] + (32 if i < NI - 1 else 16)

                @block.scalar
                def _(scalar):
                    scalar = C.gs(scalar)
                    scalar.wait_ge(C.s('ds'), ds_bm)
                    if i >= 4:
                        scalar.wait_ge(C.s('vg'), vg_b[i - 4])
                    scalar.activation(out=T1, in_=BU, func=AF.Identity,
                                      bias=c_magic[:, 0:1], scale=sscale[:, 0:1])
                    scalar.activation(out=BU, in_=T1, func=AF.Identity,
                                      bias=c_magic[:, 1:2], scale=1.0)
                    scalar.activation(out=T1, in_=BG, func=AF.Identity,
                                      bias=c_magic[:, 0:1], scale=sscale[:, 2:3])
                    scalar.activation(out=TG, in_=T1, func=AF.Identity,
                                      bias=c_magic[:, 1:2], scale=1.0)
                    scalar.activation(out=SSG, in_=TG, func=AF.Sigmoid,
                                      bias=0.0, scale=sscale[:, 3:4])
                sc_b.append(C.cnt['sc'])

                @block.vector
                def _(vector):
                    vector = C.gv(vector)
                    vector.wait_ge(C.s('sc'), sc_b[i])
                    vector.tensor_scalar(out=SSG, in0=SSG, scalar1=65536.0,
                                         scalar2=MAGIC, op0=ALU.mult, op1=ALU.add)
                    vector.tensor_scalar(out=SSG, in0=SSG, scalar1=MAGIC,
                                         scalar2=65535.0, op0=ALU.subtract, op1=ALU.min)
                    vector.scalar_tensor_tensor(out=BG, in0=TG, scalar=sscale[:, 4:5],
                                                in1=SSG, op0=ALU.mult, op1=ALU.mult)
                    vector.reduce_max(out=sacc[:, 1:2], in_=BG, axis=X,
                                      apply_absolute_value=True)
                    if i == 0:
                        vector.tensor_copy(out=sacc[:, 2:3], in_=sacc[:, 1:2])
                    else:
                        vector.tensor_tensor(out=sacc[:, 2:3], in0=sacc[:, 2:3],
                                             in1=sacc[:, 1:2], op=ALU.max)
                vg_b.append(C.cnt['vg'])

                @block.gpsimd
                def _(gpsimd):
                    gpsimd.wait_ge(C.s('sc'), sc_b[i])
                    gpsimd.wait_ge(C.s('vg'), vg_b[i])
                    C.dma(gpsimd, z_qu[i * 128:(i + 1) * 128, :], BU, 'dg')
                    C.dma(gpsimd, z_g2[i * 128:(i + 1) * 128, :], BG, 'dg')
                dg_b.append(C.cnt['dg'])

            @block.sync
            def _(sync):
                C.dma(sync, spad[:], ltmask[:, 0:16], 'ds')

            @block.vector
            def _(vector):
                vector = C.gv(vector)
                tree_max(vector, 2, 12)
                vector.tensor_copy(out=sval[0:1, 13:14], in_=sval[0:1, 12:13])

            emit_allreduce()

            def scales_g2(vector):
                vector.tensor_scalar(out=sval[0:1, 6:7], in0=sval[0:1, 10:11],
                                     scalar1=INV32767, scalar2=EPS_Q,
                                     op0=ALU.mult, op1=ALU.max)
                vector.reciprocal(out=sval[0:1, 5:6], in_=sval[0:1, 6:7])
            emit_scales(scales_g2, 7)

            vg_c = []
            sc_c = []
            ds_c = []
            dg_c = []
            for i in range(NI):
                s4 = i % 4
                BG = af[:, s4 * 2048:(s4 + 1) * 2048]
                BU = af[:, (4 + s4) * 2048:(5 + s4) * 2048]
                UO = a8f(s4)
                QUI = wa[:, s4 * 2048:(s4 + 1) * 2048].bitcast(mybir.dt.int16)
                T1 = stg[:, 0:2048]

                @block.sync
                def _(sync):
                    if i == 0:
                        C.w(sync, 'vg', 'sc', 'dg')
                    if i >= 4:
                        sync.wait_ge(C.s('vg'), vg_c[i - 4])
                    C.dma(sync, BG, z_g2[i * 128:(i + 1) * 128, :], 'ds')
                    C.dma(sync, QUI, z_qu[i * 128:(i + 1) * 128, :], 'ds')
                ds_c.append(C.cnt['ds'])
                ds_cm = ds_c[i] + (32 if i < NI - 1 else 16)

                @block.scalar
                def _(scalar):
                    scalar = C.gs(scalar)
                    scalar.wait_ge(C.s('ds'), ds_cm)
                    scalar.activation(out=T1, in_=BG, func=AF.Identity,
                                      bias=c_magic[:, 0:1], scale=sscale[:, 5:6])
                    scalar.activation(out=BG, in_=T1, func=AF.Identity,
                                      bias=c_magic[:, 1:2], scale=1.0)
                    scalar.mul(out=BG, in_=BG, mul=sscale[:, 6:7])
                sc_c.append(C.cnt['sc'])

                @block.vector
                def _(vector):
                    vector = C.gv(vector)
                    vector.wait_ge(C.s('ds'), ds_cm)
                    if i >= 4:
                        vector.wait_ge(C.s('dg'), dg_c[i - 4])
                    vector.tensor_copy(out=BU, in_=QUI)
                    vector.tensor_single_scalar(out=BU, in_=BU,
                                                scalar=sscale[:, 1:2], op=ALU.mult)
                    vector.wait_ge(C.s('sc'), sc_c[i])
                    vector.tensor_tensor(out=UO, in0=BG, in1=BU, op=ALU.mult)
                    vector.reduce_max(out=sacc[:, 1:2], in_=UO, axis=X,
                                      apply_absolute_value=True)
                    if i == 0:
                        vector.tensor_copy(out=sacc[:, 2:3], in_=sacc[:, 1:2])
                    else:
                        vector.tensor_tensor(out=sacc[:, 2:3], in0=sacc[:, 2:3],
                                             in1=sacc[:, 1:2], op=ALU.max)
                vg_c.append(C.cnt['vg'])

                @block.gpsimd
                def _(gpsimd):
                    gpsimd.wait_ge(C.s('vg'), vg_c[i])
                    C.dma(gpsimd, z_u[i * 128:(i + 1) * 128, :], UO, 'dg')
                dg_c.append(C.cnt['dg'])

            @block.sync
            def _(sync):
                C.dma(sync, spad[:], ltmask[:, 0:16], 'ds')

            @block.vector
            def _(vector):
                vector = C.gv(vector)
                tree_max(vector, 2, 12)
                vector.tensor_copy(out=sval[0:1, 13:14], in_=sval[0:1, 12:13])

            emit_allreduce()

            def scales_o(vector):
                vector.tensor_scalar(out=sval[0:1, 8:9], in0=sval[0:1, 10:11],
                                     scalar1=INV32767, scalar2=EPS_Q,
                                     op0=ALU.mult, op1=ALU.max)
                vector.reciprocal(out=sval[0:1, 7:8], in_=sval[0:1, 8:9])
                vector.tensor_single_scalar(out=sval[0:1, 8:9], in_=sval[0:1, 8:9],
                                            scalar=0.5, op=ALU.mult)
            emit_scales(scales_o, 9)

            @block.sync
            def _(sync):
                C.w(sync, 'tc', 'vg', 'dg')
                for i in range(NI):
                    C.dma(sync, wa.rearrange("p (a c) -> p a c", a=NI)[:, i, :],
                          z_wdT[i], 'ds')
            ds_wd = C.cnt['ds']

            vg_dn = []
            vg_oq = []
            tc_dn = []
            ds_dn = []
            ds_os = []
            for g in range(8):
                p = g % 2
                banks = pb[0:4] if p == 0 else pb[4:8]
                oarea = a8[:, p * 6144:p * 6144 + 6144]
                xres = stg[:, p * 2048:(p + 1) * 2048]
                uarea = af[:, 4096 + p * 6144:4096 + (p + 1) * 6144]
                outt = af[:, p * 2048:(p + 1) * 2048]

                @block.sync
                def _(sync):
                    if g == 0:
                        C.w(sync, 'dg', 'vg', 'sc')
                    if g >= 2:
                        sync.wait_ge(C.s('vg'), vg_oq[g - 2])
                        sync.wait_ge(C.s('vg'), vg_dn[g - 2])
                        sync.wait_ge(C.s('dg'), ds_os[g - 2])
                    C.dma(sync, uarea.rearrange("p (a c) -> p a c", a=NI),
                          z_u[:, g * 256:(g + 1) * 256].rearrange("(a p) c -> p a c", p=128),
                          'ds')
                    C.dma(sync, xres[:, 0:1024], src[g * 256:g * 256 + 128, :], 'ds')
                    C.dma(sync, xres[:, 1024:2048], src[g * 256 + 128:(g + 1) * 256, :], 'ds')
                ds_dn.append(C.cnt['ds'])
                ds_dm = ds_dn[g] + (48 if g < 7 else 16)

                @block.vector
                def _(vector):
                    vector = C.gv(vector)
                    vector.wait_ge(C.s('ds'), ds_dm)
                    if g >= 2:
                        vector.wait_ge(C.s('tc'), tc_dn[g - 2])
                    vector.tensor_scalar(out=uarea, in0=uarea, scalar1=sscale[:, 7:8],
                                         scalar2=MAGIC, op0=ALU.mult, op1=ALU.add)
                    vector.tensor_scalar(out=oarea, in0=uarea, scalar1=MAGIC,
                                         scalar2=sscale[:, 8:9],
                                         op0=ALU.subtract, op1=ALU.mult)
                vg_oq.append(C.cnt['vg'])

                @block.tensor
                def _(tensor):
                    tensor.wait_ge(C.s('vg'), vg_oq[g])
                    tensor.wait_ge(C.s('ds'), ds_wd)
                    if g >= 2:
                        tensor.wait_ge(C.s('vg'), vg_dn[g - 2])
                    for i in range(NI):
                        for tt in range(2):
                            for n2 in range(2):
                                i2 = tensor.matmul(
                                    out=banks[tt * 2 + n2][:, :],
                                    lhsT=oarea.rearrange("p (a c) -> p a c", a=NI)[:, i, tt * 128:(tt + 1) * 128],
                                    rhs=wa.rearrange("p (a c) -> p a c", a=NI)[:, i, n2 * 512:(n2 + 1) * 512],
                                    start=(i == 0), stop=(i == NI - 1))
                    C.inc(i2, 'tc')
                tc_dn.append(C.cnt['tc'])

                @block.vector
                def _(vector):
                    vector = C.gv(vector)
                    vector.wait_ge(C.s('tc'), tc_dn[g])
                    for tt in range(2):
                        for n2 in range(2):
                            vector.scalar_tensor_tensor(
                                out=outt[:, tt * 1024 + n2 * 512:tt * 1024 + (n2 + 1) * 512],
                                in0=banks[tt * 2 + n2][:, :], scalar=1.0,
                                in1=xres[:, tt * 1024 + n2 * 512:tt * 1024 + (n2 + 1) * 512],
                                op0=ALU.mult, op1=ALU.add)
                vg_dn.append(C.cnt['vg'])

                @block.gpsimd
                def _(gpsimd):
                    gpsimd.wait_ge(C.s('vg'), vg_dn[g])
                    C.dma(gpsimd, dst[g * 256:g * 256 + 128, :], outt[:, 0:1024], 'dg')
                    C.dma(gpsimd, dst[g * 256 + 128:(g + 1) * 256, :], outt[:, 1024:2048], 'dg')
                ds_os.append(C.cnt['dg'])

            @block.sync
            def _(sync):
                C.dma(sync, spad[:], ltmask[:, 0:16], 'ds')

        def emit_attention():
            emit_ln(z_x1)
            emit_tiles_to_a8(quant=False)

            wa_dkv = wa[:, 0:2048]
            wa_uk = wa[:, 2048:4096]
            wa_uv = wa[:, 4096:6144]
            wa_kr = wa[:, 6144:6400]
            wa_ckv = wa[:, 6400:10496]
            wa_wo = wa[:, 10496:18688]
            wa_st = wa[:, 18688:21760]

            @block.gpsimd
            def _(gpsimd):
                C.w(gpsimd, 'tc', 'vg')
                C.dma(gpsimd, wa_dkv.rearrange("p (a c) -> p a c", a=ND),
                      wdkv.rearrange("(a p) c -> p a c", p=128), 'dg')
                C.dma(gpsimd, wa_uk.rearrange("p (a c) -> p a c", a=2),
                      wuk.rearrange("(a p) c -> p a c", p=128), 'dg')
                C.dma(gpsimd, wa_uv.rearrange("p (a c) -> p a c", a=2),
                      wuv.rearrange("(a p) c -> p a c", p=128), 'dg')
                C.dma(gpsimd, wa_kr.rearrange("p (a c) -> p a c", a=ND),
                      wkr.rearrange("(a p) c -> p a c", p=128), 'dg')
                C.dma(gpsimd, wa_wo.rearrange("p (a c) -> p a c", a=ND),
                      wo.rearrange("(a p) c -> p a c", p=128), 'dg')
            dg_w = C.cnt['dg']

            for m2 in range(2):
                @block.tensor
                def _(tensor):
                    tensor.wait_ge(C.s('dg'), dg_w)
                    tensor.wait_ge(C.s('vg'), C.cnt['vg'])
                    for kk in range(ND):
                        for n in range(4):
                            i2 = tensor.matmul(
                                out=pb[n][:, :],
                                lhsT=wa_dkv.rearrange("p (a c) -> p a c", a=ND)[:, kk, m2 * 128:(m2 + 1) * 128],
                                rhs=a8.rearrange("p (j c) -> p j c", j=ND)[:, kk, n * 512:(n + 1) * 512],
                                start=(kk == 0), stop=(kk == ND - 1))
                    C.inc(i2, 'tc')
                tc_h = C.cnt['tc']

                @block.vector
                def _(vector):
                    vector = C.gv(vector)
                    vector.wait_ge(C.s('tc'), tc_h)
                    for n in range(4):
                        vector.tensor_copy(
                            out=wa_ckv[:, m2 * 2048 + n * 512:m2 * 2048 + (n + 1) * 512],
                            in_=pb[n][:, :])

            for n in range(4):
                @block.tensor
                def _(tensor):
                    tensor.wait_ge(C.s('vg'), C.cnt['vg'])
                    for kk in range(ND):
                        i2 = tensor.matmul(
                            out=pb[0][0:32, :],
                            lhsT=wa_kr.rearrange("p (a c) -> p a c", a=ND)[:, kk, :],
                            rhs=a8.rearrange("p (j c) -> p j c", j=ND)[:, kk, n * 512:(n + 1) * 512],
                            start=(kk == 0), stop=(kk == ND - 1))
                    C.inc(i2, 'tc')
                tc_h = C.cnt['tc']

                @block.vector
                def _(vector):
                    vector = C.gv(vector)
                    vector.wait_ge(C.s('tc'), tc_h)
                    vector.tensor_tensor(out=skr[0:32, n * 512:(n + 1) * 512],
                                         in0=pb[0][0:32, :],
                                         in1=c_rope[0:32, n * 512:(n + 1) * 512],
                                         op=ALU.mult)

            vg_v = []
            tc_v = []
            ds_v = []
            for t in range(NT):
                p = t % 2
                slot = sbq[:, p * 1024:(p + 1) * 1024]

                @block.tensor
                def _(tensor):
                    if t >= 2:
                        tensor.wait_ge(C.s('vg'), vg_v[t - 2])
                    for k2 in range(2):
                        for n2 in range(2):
                            i2 = tensor.matmul(
                                out=pb[4 + p * 2 + n2][:, :],
                                lhsT=wa_ckv[:, k2 * 2048 + t * 128:k2 * 2048 + (t + 1) * 128],
                                rhs=wa_uv[:, k2 * 1024 + n2 * 512:k2 * 1024 + (n2 + 1) * 512],
                                start=(k2 == 0), stop=(k2 == 1))
                    C.inc(i2, 'tc')
                tc_v.append(C.cnt['tc'])

                @block.vector
                def _(vector):
                    vector = C.gv(vector)
                    vector.wait_ge(C.s('tc'), tc_v[t])
                    if t >= 2:
                        vector.wait_ge(C.s('ds'), ds_v[t - 2])
                    vector.tensor_copy(out=slot[:, 0:512], in_=pb[4 + p * 2][:, :])
                    vector.tensor_copy(out=slot[:, 512:1024], in_=pb[4 + p * 2 + 1][:, :])
                vg_v.append(C.cnt['vg'])

                @block.sync
                def _(sync):
                    sync.wait_ge(C.s('vg'), vg_v[t])
                    C.dma(sync, z_v[t * 128:(t + 1) * 128, :], slot, 'ds')
                ds_v.append(C.cnt['ds'])

            vg_k = []
            ds_k = []
            for h in range(HN):
                p = h % 2
                ktile = skh[p]

                @block.vector
                def _(vector):
                    vector = C.gv(vector)
                    if h >= 2:
                        vector.wait_ge(C.s('ds'), ds_k[h - 2])
                    vector.tensor_copy(out=ktile[64:96, :], in_=skr[0:32, :])
                vg_kr = C.cnt['vg']

                @block.tensor
                def _(tensor):
                    tensor.wait_ge(C.s('vg'), C.cnt['vg'] if h == 0 else vg_k[h - 1])
                    for n in range(4):
                        for k2 in range(2):
                            i2 = tensor.matmul(
                                out=pb[n][0:64, :],
                                lhsT=wa_uk[:, k2 * 1024 + h * 64:k2 * 1024 + (h + 1) * 64],
                                rhs=wa_ckv[:, k2 * 2048 + n * 512:k2 * 2048 + (n + 1) * 512],
                                start=(k2 == 0), stop=(k2 == 1))
                        if n == 3:
                            C.inc(i2, 'tc')
                tc_k = C.cnt['tc']

                @block.vector
                def _(vector):
                    vector = C.gv(vector)
                    vector.wait_ge(C.s('tc'), tc_k)
                    for n in range(4):
                        vector.tensor_copy(out=ktile[0:64, n * 512:(n + 1) * 512],
                                           in_=pb[n][0:64, :])
                vg_k.append(C.cnt['vg'])

                @block.sync
                def _(sync):
                    sync.wait_ge(C.s('vg'), vg_k[h])
                    C.dma(sync, z_kT[h * HD:(h + 1) * HD, :], ktile[0:HD, :], 'ds')
                ds_k.append(C.cnt['ds'])

            @block.sync
            def _(sync):
                C.w(sync, 'vg', 'tc', 'sc', 'ds')

            dg_we = []
            tc_we = []
            for n3 in range(3):
                for kp in range(32):
                    p = kp % 2
                    lslot = wa_st[:, p * 1024:(p + 1) * 1024]
                    rslot = wa_st[:, 2048 + p * 512:2048 + (p + 1) * 512]

                    @block.gpsimd
                    def _(gpsimd):
                        if len(tc_we) >= 2:
                            gpsimd.wait_ge(C.s('tc'), tc_we[-2])
                        C.dma(gpsimd, lslot.rearrange("p (a c) -> p a c", a=ND),
                              wdqT[kp * 128:(kp + 1) * 128, :].rearrange("p (a c) -> p a c", a=ND),
                              'dg')
                        C.dma(gpsimd, rslot,
                              wcatq[kp * 128:(kp + 1) * 128, n3 * 512:(n3 + 1) * 512],
                              'dg')
                    dg_we.append(C.cnt['dg'])

                    @block.tensor
                    def _(tensor):
                        tensor.wait_ge(C.s('dg'), dg_we[-1])
                        if kp == 0:
                            tensor.wait_ge(C.s('vg'), C.cnt['vg'])
                        for m in range(8):
                            i2 = tensor.matmul(
                                out=pb[m][:, :],
                                lhsT=lslot.rearrange("p (a c) -> p a c", a=ND)[:, m, :],
                                rhs=rslot,
                                start=(kp == 0), stop=(kp == 31))
                        C.inc(i2, 'tc')
                    tc_we.append(C.cnt['tc'])

                @block.vector
                def _(vector):
                    vector = C.gv(vector)
                    vector.wait_ge(C.s('tc'), tc_we[-1])
                    for m in range(8):
                        vector.tensor_copy(
                            out=weff[:, m * NQ + n3 * 512:m * NQ + (n3 + 1) * 512],
                            in_=pb[m][:, :])

            vg_q = []
            ds_q = []
            for half in range(2):
                for n in range(4):
                    @block.tensor
                    def _(tensor):
                        tensor.wait_ge(C.s('vg'), C.cnt['vg'])
                        for kk in range(ND):
                            for h8 in range(8):
                                i2 = tensor.matmul(
                                    out=pb[h8][0:HD, :],
                                    lhsT=weff[:, kk * NQ + (half * 8 + h8) * HD:kk * NQ + (half * 8 + h8 + 1) * HD],
                                    rhs=a8.rearrange("p (j c) -> p j c", j=ND)[:, kk, n * 512:(n + 1) * 512],
                                    start=(kk == 0), stop=(kk == ND - 1))
                        C.inc(i2, 'tc')
                    tc_q = C.cnt['tc']

                    for h8 in range(8):
                        p2 = h8 % 2
                        slot = sbq[:, p2 * 1024:p2 * 1024 + 512]

                        @block.vector
                        def _(vector):
                            vector = C.gv(vector)
                            vector.wait_ge(C.s('tc'), tc_q)
                            if len(ds_q) >= 2:
                                vector.wait_ge(C.s('ds'), ds_q[-2])
                            vector.tensor_copy(out=slot[0:64, :], in_=pb[h8][0:64, :])
                            vector.tensor_tensor(out=slot[64:96, :],
                                                 in0=pb[h8][64:96, :],
                                                 in1=c_rope[64:96, n * 512:(n + 1) * 512],
                                                 op=ALU.mult)
                        vg_q.append(C.cnt['vg'])

                        @block.sync
                        def _(sync):
                            sync.wait_ge(C.s('vg'), vg_q[-1])
                            C.dma(sync, z_qT[(half * 8 + h8) * HD:(half * 8 + h8 + 1) * HD,
                                             n * 512:(n + 1) * 512],
                                  slot[0:HD, :], 'ds')
                        ds_q.append(C.cnt['ds'])

            ds_hd = []
            tc_end = []
            vg_hd = []
            for h in range(HN):
                hp = h % 2
                ybase = 0 if hp == 0 else 64
                dbase = 96 if hp == 0 else 32

                @block.sync
                def _(sync):
                    if h == 0:
                        C.w(sync, 'vg')
                    if h >= 2:
                        sync.wait_ge(C.s('tc'), tc_end[h - 2])
                    C.dma(sync, sqh[hp][0:HD, :], z_qT[h * HD:(h + 1) * HD, :], 'ds')
                    C.dma(sync, skh[hp][0:HD, :], z_kT[h * HD:(h + 1) * HD, :], 'ds')
                    C.dma(sync, svh[hp].rearrange("p (a c) -> p a c", a=NT),
                          z_v[:, h * HS:(h + 1) * HS].rearrange("(a p) c -> p a c", p=128),
                          'ds')
                ds_hd.append(C.cnt['ds'])

                steps = []
                for ki in range(NT):
                    nq = T - 128 * ki
                    for qc in range((nq + 511) // 512):
                        w_ = min(512, nq - qc * 512)
                        steps.append((ki, qc, w_))
                tc0 = C.cnt['tc']
                sc0 = C.cnt['sc']

                @block.tensor
                def _(tensor):
                    tensor.wait_ge(C.s('ds'), ds_hd[h])
                    for si, (ki, qc, w_) in enumerate(steps):
                        if si >= 2:
                            tensor.wait_ge(C.s('sc'), sc0 + si - 1)
                        else:
                            tensor.wait_ge(C.s('sc'), sc0)
                        i2 = tensor.matmul(
                            out=pb[si % 2][:, 0:w_],
                            lhsT=skh[hp][0:HD, ki * 128:(ki + 1) * 128],
                            rhs=sqh[hp][0:HD, 128 * ki + qc * 512:128 * ki + qc * 512 + w_],
                            start=True, stop=True)
                        C.inc(i2, 'tc')

                @block.scalar
                def _(scalar):
                    for si, (ki, qc, w_) in enumerate(steps):
                        scalar.wait_ge(C.s('tc'), tc0 + si + 1)
                        i2 = scalar.activation(
                            out=pTa[:, PT_OFF[ki] + qc * 512:PT_OFF[ki] + qc * 512 + w_],
                            in_=pb[si % 2][:, 0:w_],
                            func=AF.Exp, bias=0.0, scale=ISQ96)
                        C.inc(i2, 'sc')
                sc_exp = C.cnt['sc']

                @block.vector
                def _(vector):
                    vector = C.gv(vector)
                    vector.wait_ge(C.s('sc'), sc_exp)
                    for ki in range(NT):
                        vector.tensor_tensor(
                            out=pTa[:, PT_OFF[ki]:PT_OFF[ki] + 128],
                            in0=pTa[:, PT_OFF[ki]:PT_OFF[ki] + 128],
                            in1=c_lt[:, :], op=ALU.mult)
                vg_mask = C.cnt['vg']

                @block.tensor
                def _(tensor):
                    tensor.wait_ge(C.s('vg'), vg_mask)
                    for n in range(4):
                        for ki in range(4 * n + 4):
                            if 128 * ki <= 512 * n:
                                rhs = pTa[:, PT_OFF[ki] + 512 * n - 128 * ki:
                                          PT_OFF[ki] + 512 * n - 128 * ki + 512]
                                c0, c1 = 0, 512
                                st_ = (ki == 0)
                            else:
                                off = 128 * ki - 512 * n
                                rhs = pTa[:, PT_OFF[ki]:PT_OFF[ki] + 512 - off]
                                c0, c1 = off, 512
                                st_ = False
                            tensor.matmul(out=pb[2 + n][ybase:ybase + 64, c0:c1],
                                          lhsT=svh[hp][:, ki * 64:(ki + 1) * 64],
                                          rhs=rhs, start=st_, stop=(ki == 4 * n + 3))
                            i2 = tensor.matmul(out=pb[2 + n][dbase:dbase + 32, c0:c1],
                                               lhsT=c_ones32[:, :],
                                               rhs=rhs, start=st_, stop=(ki == 4 * n + 3),
                                               tile_position=(0, dbase))
                    C.inc(i2, 'tc')
                tc_pv = C.cnt['tc']

                @block.vector
                def _(vector):
                    vector = C.gv(vector)
                    vector.wait_ge(C.s('tc'), tc_pv)
                    for n in range(4):
                        vector.tensor_copy(out=sd1[32 * n:32 * n + 1, 0:512],
                                           in_=pb[2 + n][dbase:dbase + 1, :])
                    vector.reciprocal(out=sd1[:, 512:1024], in_=sd1[:, 0:512])
                vg_rd = C.cnt['vg']

                for n in range(4):
                    @block.tensor
                    def _(tensor):
                        tensor.wait_ge(C.s('vg'), vg_rd if n == 0 else C.cnt['vg'])
                        i2 = tensor.matmul(out=pb[6][ybase:ybase + 64, :],
                                           lhsT=c_onesf[32 * n:32 * n + 1, :],
                                           rhs=sd1[32 * n:32 * n + 1, 512:1024],
                                           start=True, stop=True,
                                           tile_position=(32 * n, ybase))
                        C.inc(i2, 'tc')
                    tc_bc = C.cnt['tc']

                    @block.vector
                    def _(vector):
                        vector = C.gv(vector)
                        vector.wait_ge(C.s('tc'), tc_bc)
                        vector.tensor_copy(out=sd1[ybase:ybase + 64, 1024:1536],
                                           in_=pb[6][ybase:ybase + 64, :])
                        vector.tensor_tensor(
                            out=a8.rearrange("p (j c) -> p j c", j=ND)[:, h // 2, n * 512:(n + 1) * 512][ybase:ybase + 64, :],
                            in0=pb[2 + n][ybase:ybase + 64, :],
                            in1=sd1[ybase:ybase + 64, 1024:1536],
                            op=ALU.mult)
                vg_hd.append(C.cnt['vg'])
                tc_end.append(C.cnt['tc'])

            if dbg:
                vg_now = C.cnt['vg']

                @block.sync
                def _(sync):
                    sync.wait_ge(C.s('vg'), vg_now)
                    C.dma(sync, d_a8, a8[:, :], 'ds')

            tc_wo = []
            vg_wo = []
            ds_wo = []
            ds_xo = []
            for t in range(NT):
                p = t % 2
                banks = pb[0:2] if p == 0 else pb[2:4]
                xres = stg[:, p * 1024:(p + 1) * 1024]
                outt = sd1[:, p * 1024:(p + 1) * 1024]

                @block.sync
                def _(sync):
                    if t == 0:
                        C.w(sync, 'vg', 'dg')
                    if t >= 2:
                        sync.wait_ge(C.s('vg'), vg_wo[t - 2])
                        sync.wait_ge(C.s('dg'), ds_xo[t - 2])
                    C.dma(sync, xres, z_x1[t * 128:(t + 1) * 128, :], 'ds')
                ds_wo.append(C.cnt['ds'])

                @block.tensor
                def _(tensor):
                    if t == 0:
                        tensor.wait_ge(C.s('vg'), C.cnt['vg'])
                        tensor.wait_ge(C.s('dg'), dg_w)
                    if t >= 2:
                        tensor.wait_ge(C.s('vg'), vg_wo[t - 2])
                    for kk in range(ND):
                        for n2 in range(2):
                            i2 = tensor.matmul(
                                out=banks[n2][:, :],
                                lhsT=a8.rearrange("p (j c) -> p j c", j=ND)[:, kk, t * 128:(t + 1) * 128],
                                rhs=wa_wo[:, kk * 1024 + n2 * 512:kk * 1024 + (n2 + 1) * 512],
                                start=(kk == 0), stop=(kk == ND - 1))
                    C.inc(i2, 'tc')
                tc_wo.append(C.cnt['tc'])

                @block.vector
                def _(vector):
                    vector = C.gv(vector)
                    vector.wait_ge(C.s('tc'), tc_wo[t])
                    vector.wait_ge(C.s('ds'), ds_wo[t])
                    for n2 in range(2):
                        vector.scalar_tensor_tensor(
                            out=outt[:, n2 * 512:(n2 + 1) * 512],
                            in0=banks[n2][:, :], scalar=1.0,
                            in1=xres[:, n2 * 512:(n2 + 1) * 512],
                            op0=ALU.mult, op1=ALU.add)
                vg_wo.append(C.cnt['vg'])

                @block.gpsimd
                def _(gpsimd):
                    gpsimd.wait_ge(C.s('vg'), vg_wo[t])
                    C.dma(gpsimd, z_x2[t * 128:(t + 1) * 128, :], outt, 'dg')
                ds_xo.append(C.cnt['dg'])

        emit_ffn(x_in, z_x1)
        emit_attention()
        emit_ffn(z_x2, out, tag=2)

        if dbg:
            @block.sync
            def _(sync):
                C.w(sync, 'vg', 'sc')
                C.dma(sync, d_stat, sstat[:, 0:64], 'ds')

        @block.sync
        def _(sync):
            C.w(sync, 'ds', 'dg')

    return nc


_NC_CACHE = {}


def kernel(L0_attn_x, L0_attn_W_DQ, L0_attn_W_UQ, L0_attn_W_DKV, L0_attn_W_UK,
           L0_attn_W_UV, L0_attn_W_QR, L0_attn_W_KR, L0_attn_W_O,
           L0_ffn_w_gate, L0_ffn_w_up, L0_ffn_w_down):
    f = np.float32
    x = np.ascontiguousarray(np.asarray(L0_attn_x, dtype=f))
    wdqT = np.ascontiguousarray(np.asarray(L0_attn_W_DQ, dtype=f).T)
    wuq = np.asarray(L0_attn_W_UQ, dtype=f)
    wqr = np.asarray(L0_attn_W_QR, dtype=f)
    wcatq = np.ascontiguousarray(
        np.concatenate([wuq.reshape(4096, HN, HS), wqr.reshape(4096, HN, DR)],
                       axis=2).reshape(4096, NQ))
    wdkv = np.ascontiguousarray(np.asarray(L0_attn_W_DKV, dtype=f))
    wuk = np.ascontiguousarray(np.asarray(L0_attn_W_UK, dtype=f))
    wuv = np.ascontiguousarray(np.asarray(L0_attn_W_UV, dtype=f))
    wkr = np.ascontiguousarray(np.asarray(L0_attn_W_KR, dtype=f))
    wo = np.ascontiguousarray(np.asarray(L0_attn_W_O, dtype=f))
    wg = np.ascontiguousarray(np.asarray(L0_ffn_w_gate, dtype=f))
    wu = np.ascontiguousarray(np.asarray(L0_ffn_w_up, dtype=f))
    wd = np.ascontiguousarray(np.asarray(L0_ffn_w_down, dtype=f))

    t = np.arange(T, dtype=f)
    inv = (f(1.0) / (f(10000.0) ** (np.arange(0, DR, 2, dtype=f) / f(DR)))).astype(f)
    freqs = (t[:, None] * inv[None, :]).astype(f)
    emb = np.concatenate([freqs, freqs], axis=-1)
    rope = (np.cos(emb) + np.sin(emb)).astype(f)
    ropeT = np.ascontiguousarray(rope.T)

    ltm = np.zeros((128, 128), dtype=f)
    for kk in range(128):
        ltm[kk, kk:] = 1.0
    onescol = np.zeros((128, 32), dtype=f)
    onescol[:, 0] = 1.0
    onesf = np.ones((128, 64), dtype=f)
    magicv = np.zeros((128, 2), dtype=f); magicv[:, 0] = f(12582912.0); magicv[:, 1] = f(-12582912.0)
    eye = np.eye(128, dtype=f)

    if "nc" not in _NC_CACHE:
        _NC_CACHE["nc"] = build_nc()
    nc = _NC_CACHE["nc"]

    shared = dict(wdqT=wdqT, wcatq=wcatq, wdkv=wdkv, wuk=wuk, wuv=wuv, wkr=wkr,
                  wo=wo, wg=wg, wu=wu, wd=wd, ropeT=ropeT, ltmask=ltm,
                  onescol=onescol, onesf=onesf, magicv=magicv, eye=eye)
    in_maps = [dict(shared, x=np.ascontiguousarray(x[c])) for c in range(8)]
    res = run_bass_kernel_spmd(nc, in_maps, core_ids=list(range(8)))
    return np.stack([np.asarray(res.results[c]["out"]) for c in range(8)], axis=0)


# version 10
# speedup vs baseline: 1.2206x; 1.0362x over previous
import numpy as np
import ml_dtypes
from contextlib import ExitStack

import concourse.bass as bass
import concourse.mybir as mybir
from concourse.bass_utils import run_bass_kernel_spmd

F32 = mybir.dt.float32
BF16 = mybir.dt.bfloat16
AF = mybir.ActivationFunctionType
ALU = mybir.AluOpType
X = mybir.AxisListType.X

T = 2048
D = 1024
HN = 16
HS = 64
DR = 32
HD = HS + DR
DFF = 3072
NT = T // 128
ND = D // 128
NI = DFF // 128
NQ = HN * HD

MAGIC = float(np.float32(12582912.0))
INV7 = float(np.float32(1.0) / np.float32(7.0))
INV32767 = float(np.float32(1.0) / np.float32(32767.0))
INV1024 = float(np.float32(1.0) / np.float32(1024.0))
ISQ96 = float(np.float32(1.0) / np.float32(np.sqrt(np.float32(96.0))))
EPS_LN = 1e-5
EPS_Q = 1e-12

PT_OFF = [0] * (NT + 1)
for _ki in range(NT):
    PT_OFF[_ki + 1] = PT_OFF[_ki] + (T - 128 * _ki)
PT_TOT = PT_OFF[NT]


class K:
    def __init__(self, block, sems):
        self.block = block
        self.sems = sems
        self.cnt = {k: 0 for k in sems}

    def s(self, name):
        return self.sems[name]

    def inc(self, instr, name, n=1):
        self.cnt[name] += n
        return instr.then_inc(self.sems[name], n)

    def dma(self, eng, out, in_, name, **kw):
        i = eng.dma_start(out=out, in_=in_, **kw)
        self.cnt[name] += 16
        i.then_inc(self.sems[name], 16)
        return i

    def w(self, eng, *names):
        for name in names:
            eng.wait_ge(self.sems[name], self.cnt[name])

    def gv(self, vector):
        C = self

        class GV:
            def wait_ge(self, *a, **k):
                return vector.wait_ge(*a, **k)

            def __getattr__(self, name):
                fn = getattr(vector, name)

                def wrapped(*a, **k):
                    vector.wait_ge(C.s('vg'), C.cnt['vg'])
                    i = fn(*a, **k)
                    C.inc(i, 'vg')
                    return i
                return wrapped
        return GV()

    def gs(self, scalar):
        C = self

        class GS:
            def wait_ge(self, *a, **k):
                return scalar.wait_ge(*a, **k)

            def __getattr__(self, name):
                fn = getattr(scalar, name)

                def wrapped(*a, **k):
                    scalar.wait_ge(C.s('sc'), C.cnt['sc'])
                    i = fn(*a, **k)
                    C.inc(i, 'sc')
                    return i
                return wrapped
        return GS()


def build_nc(dbg=False):
    nc = bass.Bass("TRN2", target_bir_lowering=False, debug=False,
                   detect_race_conditions=False)

    def _dz(name, shape, dt):
        if dbg:
            return nc.dram_tensor(name, shape, dt, kind="ExternalOutput").ap()
        return nc.dram_tensor(name, shape, dt).ap()

    x_in = nc.dram_tensor("x", [T, D], F32, kind="ExternalInput").ap()
    wdqT = nc.dram_tensor("wdqT", [4096, D], F32, kind="ExternalInput").ap()
    wcatq = nc.dram_tensor("wcatq", [4096, NQ], F32, kind="ExternalInput").ap()
    wdkv = nc.dram_tensor("wdkv", [D, 256], F32, kind="ExternalInput").ap()
    wuk = nc.dram_tensor("wuk", [256, D], F32, kind="ExternalInput").ap()
    wuv = nc.dram_tensor("wuv", [256, D], F32, kind="ExternalInput").ap()
    wkr = nc.dram_tensor("wkr", [D, DR], F32, kind="ExternalInput").ap()
    wo = nc.dram_tensor("wo", [D, D], F32, kind="ExternalInput").ap()
    wg = nc.dram_tensor("wg", [DFF, D], F32, kind="ExternalInput").ap()
    wu = nc.dram_tensor("wu", [DFF, D], F32, kind="ExternalInput").ap()
    wd = nc.dram_tensor("wd", [D, DFF], F32, kind="ExternalInput").ap()
    ropeT_in = nc.dram_tensor("ropeT", [DR, T], F32, kind="ExternalInput").ap()
    ltmask = nc.dram_tensor("ltmask", [128, 128], F32, kind="ExternalInput").ap()
    onescol = nc.dram_tensor("onescol", [128, 32], F32, kind="ExternalInput").ap()
    onesf = nc.dram_tensor("onesf", [128, 64], F32, kind="ExternalInput").ap()
    magicv = nc.dram_tensor("magicv", [128, 2], F32, kind="ExternalInput").ap()
    eye_in = nc.dram_tensor("eye", [128, 128], F32, kind="ExternalInput").ap()
    out = nc.dram_tensor("out", [T, D], F32, kind="ExternalOutput").ap()

    z_x1 = _dz("z_x1", [T, D], F32)
    z_x2 = _dz("z_x2", [T, D], F32)
    z_wgT = _dz("z_wgT", [NI, D, 128], BF16)
    z_wuT = _dz("z_wuT", [NI, D, 128], BF16)
    z_wdT = _dz("z_wdT", [NI, 128, D], BF16)
    z_upraw = _dz("z_upraw", [DFF, T], F32)
    z_graw = _dz("z_graw", [DFF, T], F32)
    z_qu = _dz("z_qu", [DFF, T], mybir.dt.int16)
    z_g2 = _dz("z_g2", [DFF, T], F32)
    z_u = _dz("z_u", [DFF, T], F32)
    z_qT = _dz("z_qT", [NQ, T], BF16)
    z_kT = _dz("z_kT", [NQ, T], BF16)
    z_v = _dz("z_v", [T, D], BF16)
    d_a8 = _dz("d_a8", [128, 8 * T], BF16) if dbg else None
    d_a8f = _dz("d_a8f", [128, 8 * T], BF16) if dbg else None
    d_stat = _dz("d_stat", [128, 64], F32) if dbg else None
    cc_in = nc.dram_tensor("cc_in", [1, 2], F32).ap()
    cc_out = nc.dram_tensor("cc_out", [1, 2], F32).ap()
    z_sc = nc.dram_tensor("z_sc", [1, 16], F32).ap()

    with ExitStack() as ctx:
        def sb(name, shape, dt):
            return ctx.enter_context(nc.sbuf_tensor(name, shape, dt))

        c_rope = sb("c_rope", [128, T], F32)
        c_lt = sb("c_lt", [128, 128], BF16)
        c_eye = sb("c_eye", [128, 128], BF16)
        c_ones32 = sb("c_ones32", [128, 32], BF16)
        c_onesf = sb("c_onesf", [128, 64], F32)
        c_magic = sb("c_magic", [128, 2], F32)
        sval = sb("sval", [1, 16], F32)
        sscale = sb("sscale", [128, 16], F32)
        streeg = sb("streeg", [128, 4], F32)
        s32a = sb("s32a", [128, 32], F32)
        s32b = sb("s32b", [128, 32], F32)
        sacc = sb("sacc", [128, 4], F32)
        sstat = sb("sstat", [128, 4 * NT], F32)
        spad = sb("spad", [128, 16], F32)
        af = sb("af", [128, 16384], F32)
        a8 = sb("a8", [128, 8 * T], BF16)
        wa = sb("wa", [128, 24576], BF16)
        stg = sb("stg", [128, 4096], F32)
        sbq = sb("sbq", [128, 2048], BF16)
        sqh = [sb(f"sqh{i}", [HD, T], BF16) for i in range(2)]
        skh = [sb(f"skh{i}", [HD, T], BF16) for i in range(2)]
        svh = [sb(f"svh{i}", [128, HN * HS], BF16) for i in range(2)]
        skr = sb("skr", [DR, T], BF16)
        sd1 = sb("sd1", [128, 2048], F32)

        pb = [ctx.enter_context(nc.psum_tensor(f"pb{i}", [128, 512], F32))
              for i in range(8)]
        pt6 = pb[6][:, :].bitcast(BF16)
        pt7 = pb[7][:, :].bitcast(BF16)
        pt4 = pb[4][:, :].bitcast(BF16)
        pt5 = pb[5][:, :].bitcast(BF16)
        pTa = af[:, 0:8704].bitcast(BF16)
        weff = af[:, 0:6144].bitcast(BF16)

        sems = {}
        for name in ["ds", "dg", "tc", "sc", "cc", "vg"]:
            sems[name] = ctx.enter_context(nc.semaphore(name))
        lo = min(s.num for s in sems.values())
        hi = max(s.num for s in sems.values()) + 1
        nc.gpsimd.sem_clear(range(lo, hi))
        nc.all_engine_barrier()
        block = ctx.enter_context(nc.Block())
        C = K(block, sems)

        @block.sync
        def _(sync):
            C.dma(sync, c_rope[0:32, :], ropeT_in, 'ds')
            C.dma(sync, c_rope[64:96, :], ropeT_in, 'ds')
            C.dma(sync, c_onesf[:, :], onesf, 'ds')
            C.dma(sync, c_magic[:, :], magicv, 'ds')

        @block.gpsimd
        def _(gpsimd):
            C.dma(gpsimd, c_lt[:, :], ltmask, 'dg')
            C.dma(gpsimd, c_eye[:, :], eye_in, 'dg')
            C.dma(gpsimd, c_ones32[:, :], onescol, 'dg')
        dg_consts = C.cnt['dg']

        def tree_max(vector, src_col, dst_col):
            vector.tensor_copy(out=streeg[:64, 0:1], in_=sacc[64:128, src_col:src_col + 1])
            vector.tensor_tensor(out=streeg[:64, 1:2], in0=sacc[:64, src_col:src_col + 1],
                                 in1=streeg[:64, 0:1], op=ALU.max)
            vector.tensor_copy(out=streeg[:32, 2:3], in_=streeg[32:64, 1:2])
            vector.tensor_tensor(out=s32a[:32, 0:1], in0=streeg[:32, 1:2],
                                 in1=streeg[:32, 2:3], op=ALU.max)
            vector.transpose(out=s32b[:32, :32], in_=s32a[:32, :32])
            return vector.reduce_max(out=sval[0:1, dst_col:dst_col + 1],
                                     in_=s32b[0:1, :32], axis=X)

        def emit_allreduce():
            vg_ready = C.cnt['vg']

            @block.gpsimd
            def _(gpsimd):
                gpsimd.wait_ge(C.s('vg'), vg_ready)
                C.dma(gpsimd, cc_in, sval[0:1, 12:14], 'dg')
                C.w(gpsimd, 'dg')
                i = gpsimd.collective_compute(
                    "AllReduce", ALU.max,
                    replica_groups=[list(range(8))],
                    ins=[cc_in.opt()],
                    outs=[cc_out.opt()],
                )
                C.inc(i, 'cc')
                gpsimd.wait_ge(C.s('cc'), C.cnt['cc'])
                C.dma(gpsimd, sval[0:1, 10:12], cc_out, 'dg')

        def emit_scales(build, ncols):
            dg_ready = C.cnt['dg']

            @block.vector
            def _(vector):
                vector = C.gv(vector)
                vector.wait_ge(C.s('dg'), dg_ready)
                build(vector)
            vg_here = C.cnt['vg']

            @block.sync
            def _(sync):
                sync.wait_ge(C.s('vg'), vg_here)
                C.dma(sync, z_sc[0:1, 0:ncols], sval[0:1, 0:ncols], 'ds')
                C.w(sync, 'ds')
                C.dma(sync, sscale[:, 0:ncols],
                      z_sc[0:1, 0:ncols].broadcast_to([128, ncols]), 'ds')
                C.dma(sync, spad[:], ltmask[:, 0:16], 'ds')

        def dequant_core(vector, buf, tq, bm, scl, rcp, outv, nblk):
            vector.reduce_max(out=bm.rearrange("p (a b) -> p a b", b=1),
                              in_=buf.rearrange("p (a b) -> p a b", b=32),
                              axis=X, apply_absolute_value=True)
            vector.tensor_scalar(out=scl, in0=bm, scalar1=INV7, scalar2=EPS_Q,
                                 op0=ALU.mult, op1=ALU.max)
            vector.reciprocal(out=rcp, in_=scl)
            vector.tensor_tensor(out=tq.rearrange("p (a b) -> p a b", b=32),
                                 in0=buf.rearrange("p (a b) -> p a b", b=32),
                                 in1=rcp[:, :, None].broadcast_to([128, nblk, 32]),
                                 op=ALU.mult)
            vector.tensor_scalar(out=tq, in0=tq, scalar1=MAGIC, scalar2=MAGIC,
                                 op0=ALU.add, op1=ALU.subtract)
            return vector.tensor_tensor(out=outv.rearrange("p (a b) -> p a b", b=32),
                                        in0=tq.rearrange("p (a b) -> p a b", b=32),
                                        in1=scl[:, :, None].broadcast_to([128, nblk, 32]),
                                        op=ALU.mult)

        def emit_dequant_gateup(src, dstT):
            vg_after = []
            tc_after = []
            sc_after = []
            ds_ld = []
            ds_st = []
            ptvs = [pt4, pt5, pt6, pt7]
            vg0 = C.cnt['vg']
            tc0 = C.cnt['tc']
            dg0 = C.cnt['dg']
            for i in range(NI):
                p = i % 4
                buf = af[:, p * 1024:(p + 1) * 1024]
                tq = af[:, 4096 + p * 1024:4096 + (p + 1) * 1024]
                bm = af[:, 8192 + p * 128:8192 + p * 128 + 32]
                scl = af[:, 8704 + p * 128:8704 + p * 128 + 32]
                rcp = af[:, 9216 + p * 128:9216 + p * 128 + 32]
                obf = a8[:, p * 1024:(p + 1) * 1024]
                ptv = ptvs[p]

                @block.sync
                def _(sync):
                    if i >= 4:
                        sync.wait_ge(C.s('vg'), vg_after[i - 4])
                    else:
                        sync.wait_ge(C.s('vg'), vg0)
                    C.dma(sync, buf, src[i * 128:(i + 1) * 128, :], 'ds')
                ds_ld.append(C.cnt['ds'])
                ds_lm = ds_ld[i] + (16 if i < NI - 1 else 16)

                @block.vector
                def _(vector):
                    vector = C.gv(vector)
                    vector.wait_ge(C.s('ds'), ds_lm)
                    if i >= 4:
                        vector.wait_ge(C.s('tc'), tc_after[i - 4])
                        vector.wait_ge(C.s('dg'), ds_st[i - 4])
                    else:
                        vector.wait_ge(C.s('tc'), tc0)
                        vector.wait_ge(C.s('dg'), dg0)
                    dequant_core(vector, buf, tq, bm, scl, rcp, obf, 32)
                vg_after.append(C.cnt['vg'])

                @block.tensor
                def _(tensor):
                    tensor.wait_ge(C.s('vg'), vg_after[i])
                    if i == 0:
                        tensor.wait_ge(C.s('dg'), dg_consts)
                    if i >= 4:
                        tensor.wait_ge(C.s('sc'), sc_after[i - 4])
                    for j in range(ND):
                        i2 = tensor.transpose(out=ptv[:, j * 128:(j + 1) * 128],
                                              in_=obf[:, j * 128:(j + 1) * 128],
                                              identity=c_eye[:, :])
                    C.inc(i2, 'tc')
                tc_after.append(C.cnt['tc'])

                @block.scalar
                def _(scalar):
                    scalar = C.gs(scalar)
                    scalar.wait_ge(C.s('tc'), tc_after[i])
                    scalar.copy(out=obf[:, :], in_=ptv[:, :])
                sc_after.append(C.cnt['sc'])

                @block.gpsimd
                def _(gpsimd):
                    gpsimd.wait_ge(C.s('sc'), sc_after[i])
                    C.dma(gpsimd, dstT[i].rearrange("(a p) c -> p a c", p=128),
                          obf.rearrange("p (a c) -> p a c", a=ND), 'dg')
                ds_st.append(C.cnt['dg'])

            @block.sync
            def _(sync):
                C.dma(sync, spad[:], ltmask[:, 0:16], 'ds')

        def emit_dequant_down(src, dstT):
            for r in range(ND):
                buf = af[:, 0:3072]
                tq = af[:, 4096:7168]
                bm = af[:, 8192:8288]
                scl = af[:, 8448:8544]
                rcp = af[:, 8704:8800]
                obf = af[:, 10240:11776].bitcast(BF16)

                @block.sync
                def _(sync):
                    C.w(sync, 'vg', 'tc')
                    C.dma(sync, buf, src[r * 128:(r + 1) * 128, :], 'ds')
                    C.dma(sync, spad[:], ltmask[:, 0:16], 'ds')
                ds_h = C.cnt['ds']

                @block.vector
                def _(vector):
                    vector = C.gv(vector)
                    vector.wait_ge(C.s('ds'), ds_h)
                    dequant_core(vector, buf, tq, bm, scl, rcp, obf, 96)
                vg_h = C.cnt['vg']

                for rnd in range(3):
                    ptv = pt6 if rnd % 2 == 0 else pt7
                    slot = sbq[:, (rnd % 2) * 1024:(rnd % 2 + 1) * 1024]

                    @block.tensor
                    def _(tensor):
                        if rnd == 0:
                            tensor.wait_ge(C.s('vg'), vg_h)
                            tensor.wait_ge(C.s('dg'), dg_consts)
                        tensor.wait_ge(C.s('dg'), C.cnt['dg'])
                        tensor.wait_ge(C.s('sc'), C.cnt['sc'])
                        for jj in range(8):
                            i2 = tensor.transpose(
                                out=ptv[:, jj * 128:(jj + 1) * 128],
                                in_=obf[:, (rnd * 8 + jj) * 128:(rnd * 8 + jj + 1) * 128],
                                identity=c_eye[:, :])
                        C.inc(i2, 'tc')
                    tc_h = C.cnt['tc']

                    @block.scalar
                    def _(scalar):
                        scalar = C.gs(scalar)
                        scalar.wait_ge(C.s('tc'), tc_h)
                        scalar.copy(out=slot, in_=ptv[:, :])
                    sc_h = C.cnt['sc']

                    @block.gpsimd
                    def _(gpsimd):
                        gpsimd.wait_ge(C.s('sc'), sc_h)
                        C.dma(gpsimd,
                              dstT[rnd * 8:(rnd + 1) * 8, :, r * 128:(r + 1) * 128]
                              .rearrange("a p c -> p a c"),
                              slot.rearrange("p (a c) -> p a c", a=8), 'dg')

        emit_dequant_gateup(wu, z_wuT)
        emit_dequant_gateup(wg, z_wgT)
        emit_dequant_down(wd, z_wdT)

        def emit_ln(src):
            ds_after = []
            for t in range(NT):
                xt = af[:, t * 1024:(t + 1) * 1024]

                @block.sync
                def _(sync):
                    if t == 0:
                        C.w(sync, 'vg', 'tc', 'sc', 'ds', 'dg')
                    C.dma(sync, xt, src[t * 128:(t + 1) * 128, :], 'ds')
                ds_after.append(C.cnt['ds'])

            @block.sync
            def _(sync):
                C.dma(sync, spad[:], ltmask[:, 0:16], 'ds')

            for t in range(NT):
                xt = af[:, t * 1024:(t + 1) * 1024]

                @block.vector
                def _(vector):
                    vector = C.gv(vector)
                    vector.wait_ge(C.s('ds'), ds_after[t] + 16)
                    vector.reduce_sum(out=sstat[:, t:t + 1], in_=xt, axis=X)
                    vector.tensor_single_scalar(out=sstat[:, NT + t:NT + t + 1],
                                                in_=sstat[:, t:t + 1],
                                                scalar=-INV1024, op=ALU.mult)
                vg_t = C.cnt['vg']

                @block.scalar
                def _(scalar):
                    scalar = C.gs(scalar)
                    scalar.wait_ge(C.s('vg'), vg_t)
                    scalar.activation(out=stg[:, (t % 2) * 1024:(t % 2 + 1) * 1024],
                                      in_=xt, func=AF.Square,
                                      bias=sstat[:, NT + t:NT + t + 1], scale=1.0,
                                      accum_out=sstat[:, 2 * NT + t:2 * NT + t + 1])
            sc_sq = C.cnt['sc']

            @block.vector
            def _(vector):
                vector = C.gv(vector)
                vector.wait_ge(C.s('sc'), sc_sq)
                vector.tensor_scalar(out=sstat[:, 2 * NT:3 * NT],
                                     in0=sstat[:, 2 * NT:3 * NT],
                                     scalar1=INV1024, scalar2=EPS_LN,
                                     op0=ALU.mult, op1=ALU.add)
            vg_var = C.cnt['vg']

            @block.scalar
            def _(scalar):
                scalar = C.gs(scalar)
                scalar.wait_ge(C.s('vg'), vg_var)
                scalar.sqrt(out=sstat[:, 2 * NT:3 * NT], in_=sstat[:, 2 * NT:3 * NT])
            sc_sqrt = C.cnt['sc']

            @block.vector
            def _(vector):
                vector = C.gv(vector)
                vector.wait_ge(C.s('sc'), sc_sqrt)
                vector.reciprocal(out=sstat[:, 2 * NT:3 * NT], in_=sstat[:, 2 * NT:3 * NT])
                vector.tensor_tensor(out=sstat[:, 3 * NT:4 * NT],
                                     in0=sstat[:, NT:2 * NT],
                                     in1=sstat[:, 2 * NT:3 * NT], op=ALU.mult)
            vg_rstd = C.cnt['vg']

            for t in range(NT):
                xt = af[:, t * 1024:(t + 1) * 1024]

                @block.scalar
                def _(scalar):
                    scalar = C.gs(scalar)
                    if t == 0:
                        scalar.wait_ge(C.s('vg'), vg_rstd)
                    scalar.activation(out=xt, in_=xt, func=AF.Identity,
                                      bias=sstat[:, 3 * NT + t:3 * NT + t + 1],
                                      scale=sstat[:, 2 * NT + t:2 * NT + t + 1])
                sc_t = C.cnt['sc']

                @block.vector
                def _(vector):
                    vector = C.gv(vector)
                    vector.wait_ge(C.s('sc'), sc_t)
                    vector.reduce_max(out=sacc[:, 1:2], in_=xt, axis=X,
                                      apply_absolute_value=True)
                    if t == 0:
                        vector.tensor_copy(out=sacc[:, 0:1], in_=sacc[:, 1:2])
                    else:
                        vector.tensor_tensor(out=sacc[:, 0:1], in0=sacc[:, 0:1],
                                             in1=sacc[:, 1:2], op=ALU.max)

        def emit_tiles_to_a8(quant):
            vg_evac = []
            tc_tr = []
            for t in range(NT):
                p = t % 2
                xt = af[:, t * 1024:(t + 1) * 1024]
                slot = sbq[:, p * 1024:(p + 1) * 1024]
                ptv = pt6 if p == 0 else pt7

                @block.vector
                def _(vector):
                    vector = C.gv(vector)
                    if t == 0:
                        C.w(vector, 'ds')
                    if t >= 2:
                        vector.wait_ge(C.s('tc'), tc_tr[t - 2])
                    if quant:
                        vector.tensor_scalar(out=xt, in0=xt, scalar1=sscale[:, 0:1],
                                             scalar2=MAGIC, op0=ALU.mult, op1=ALU.add)
                        vector.tensor_scalar(out=slot, in0=xt, scalar1=MAGIC,
                                             scalar2=sscale[:, 1:2],
                                             op0=ALU.subtract, op1=ALU.mult)
                    else:
                        vector.tensor_copy(out=slot, in_=xt)
                vg_q = C.cnt['vg']

                @block.tensor
                def _(tensor):
                    tensor.wait_ge(C.s('vg'), vg_q)
                    if t == 0:
                        tensor.wait_ge(C.s('dg'), dg_consts)
                    if t >= 2:
                        tensor.wait_ge(C.s('vg'), vg_evac[t - 2])
                    for j in range(ND):
                        i2 = tensor.transpose(out=ptv[:, j * 128:(j + 1) * 128],
                                              in_=slot[:, j * 128:(j + 1) * 128],
                                              identity=c_eye[:, :])
                    C.inc(i2, 'tc')
                tc_tr.append(C.cnt['tc'])

                @block.vector
                def _(vector):
                    vector = C.gv(vector)
                    vector.wait_ge(C.s('tc'), tc_tr[t])
                    vector.tensor_copy(
                        out=a8.rearrange("p (j c) -> p j c", j=ND)[:, :, t * 128:(t + 1) * 128],
                        in_=ptv.rearrange("p (j c) -> p j c", j=ND))
                vg_evac.append(C.cnt['vg'])

        def emit_ffn(src, dst, tag=1):
            emit_ln(src)

            @block.vector
            def _(vector):
                vector = C.gv(vector)
                tree_max(vector, 0, 12)
                vector.tensor_copy(out=sval[0:1, 13:14], in_=sval[0:1, 12:13])

            emit_allreduce()

            def scales_x(vector):
                vector.tensor_scalar(out=sval[0:1, 1:2], in0=sval[0:1, 10:11],
                                     scalar1=INV32767, scalar2=EPS_Q,
                                     op0=ALU.mult, op1=ALU.max)
                vector.reciprocal(out=sval[0:1, 0:1], in_=sval[0:1, 1:2])
            emit_scales(scales_x, 2)

            emit_tiles_to_a8(quant=True)

            if dbg and tag == 2:
                vg_now2 = C.cnt['vg']

                @block.sync
                def _(sync):
                    sync.wait_ge(C.s('vg'), vg_now2)
                    C.dma(sync, d_a8f, a8[:, :], 'ds')

            def gateup(wT, dst_raw, acc_col):
                vg_evac = []
                tc_mm = []
                ds_ld = []
                ds_st = []
                tc0 = C.cnt['tc']
                dg0 = C.cnt['dg']
                for i in range(NI):
                    p = i % 2
                    wtile = wa[:, p * 1024:(p + 1) * 1024]
                    banks = pb[0:4] if p == 0 else pb[4:8]
                    stgh = stg[:, 0:2048] if p == 0 else stg[:, 2048:4096]

                    @block.sync
                    def _(sync):
                        if i < 2:
                            sync.wait_ge(C.s('dg'), dg0)
                            sync.wait_ge(C.s('tc'), tc0)
                        if i >= 2:
                            sync.wait_ge(C.s('tc'), tc_mm[i - 2])
                        C.dma(sync, wtile.rearrange("p (a c) -> p a c", a=ND),
                              wT[i].rearrange("(a p) c -> p a c", p=128), 'ds')
                    ds_ld.append(C.cnt['ds'])

                    @block.tensor
                    def _(tensor):
                        tensor.wait_ge(C.s('ds'), ds_ld[i] + 16)
                        if i == 0:
                            tensor.wait_ge(C.s('vg'), C.cnt['vg'])
                        if i >= 2:
                            tensor.wait_ge(C.s('vg'), vg_evac[i - 2])
                        for kk in range(ND):
                            for n in range(4):
                                i2 = tensor.matmul(
                                    out=banks[n][:, :],
                                    lhsT=wtile.rearrange("p (a c) -> p a c", a=ND)[:, kk, :],
                                    rhs=a8.rearrange("p (j c) -> p j c", j=ND)[:, kk, n * 512:(n + 1) * 512],
                                    start=(kk == 0), stop=(kk == ND - 1))
                        C.inc(i2, 'tc')
                    tc_mm.append(C.cnt['tc'])

                    @block.vector
                    def _(vector):
                        vector = C.gv(vector)
                        vector.wait_ge(C.s('tc'), tc_mm[i])
                        if i >= 2:
                            vector.wait_ge(C.s('dg'), ds_st[i - 2])
                        else:
                            C.w(vector, 'dg')
                        for n in range(4):
                            vector.tensor_copy(out=stgh[:, n * 512:(n + 1) * 512],
                                               in_=banks[n][:, :])
                        vector.reduce_max(out=sacc[:, 1:2], in_=stgh,
                                          axis=X, apply_absolute_value=True)
                        if i == 0:
                            vector.tensor_copy(out=sacc[:, acc_col:acc_col + 1],
                                               in_=sacc[:, 1:2])
                        else:
                            vector.tensor_tensor(out=sacc[:, acc_col:acc_col + 1],
                                                 in0=sacc[:, acc_col:acc_col + 1],
                                                 in1=sacc[:, 1:2], op=ALU.max)
                    vg_evac.append(C.cnt['vg'])

                    @block.gpsimd
                    def _(gpsimd):
                        gpsimd.wait_ge(C.s('vg'), vg_evac[i])
                        C.dma(gpsimd, dst_raw[i * 128:(i + 1) * 128, :], stgh, 'dg')
                    ds_st.append(C.cnt['dg'])

                @block.sync
                def _(sync):
                    C.dma(sync, spad[:], ltmask[:, 0:16], 'ds')

            gateup(z_wuT, z_upraw, 2)
            gateup(z_wgT, z_graw, 3)

            @block.vector
            def _(vector):
                vector = C.gv(vector)
                tree_max(vector, 2, 12)
                tree_max(vector, 3, 13)

            emit_allreduce()

            def scales_ug(vector):
                vector.tensor_scalar(out=sval[0:1, 1:2], in0=sval[0:1, 10:11],
                                     scalar1=INV32767, scalar2=EPS_Q,
                                     op0=ALU.mult, op1=ALU.max)
                vector.tensor_scalar(out=sval[0:1, 3:4], in0=sval[0:1, 11:12],
                                     scalar1=INV32767, scalar2=EPS_Q,
                                     op0=ALU.mult, op1=ALU.max)
                vector.reciprocal(out=sval[0:1, 0:1], in_=sval[0:1, 1:2])
                vector.reciprocal(out=sval[0:1, 2:3], in_=sval[0:1, 3:4])
                vector.tensor_single_scalar(out=sval[0:1, 4:5], in_=sval[0:1, 3:4],
                                            scalar=float(2.0 ** -16), op=ALU.mult)
            emit_scales(scales_ug, 5)

            def a8f(k):
                return a8[:, k * 4096:(k + 1) * 4096].bitcast(F32)

            def waf(k):
                return wa[:, k * 4096:(k + 1) * 4096].bitcast(F32)
            vg_b = []
            sc_b = []
            ds_b = []
            dg_b = []
            for i in range(NI):
                s4 = i % 4
                BU = af[:, s4 * 2048:(s4 + 1) * 2048]
                BG = af[:, (4 + s4) * 2048:(5 + s4) * 2048]
                TG = a8f(s4)
                SSG = waf(s4)
                T1 = stg[:, 0:2048]

                @block.sync
                def _(sync):
                    if i == 0:
                        C.w(sync, 'vg', 'tc', 'sc', 'dg')
                    if i >= 4:
                        sync.wait_ge(C.s('dg'), dg_b[i - 4])
                    C.dma(sync, BU, z_upraw[i * 128:(i + 1) * 128, :], 'ds')
                    C.dma(sync, BG, z_graw[i * 128:(i + 1) * 128, :], 'ds')
                ds_b.append(C.cnt['ds'])
                ds_bm = ds_b[i] + (32 if i < NI - 1 else 16)

                @block.scalar
                def _(scalar):
                    scalar = C.gs(scalar)
                    scalar.wait_ge(C.s('ds'), ds_bm)
                    if i >= 4:
                        scalar.wait_ge(C.s('vg'), vg_b[i - 4])
                    scalar.activation(out=T1, in_=BU, func=AF.Identity,
                                      bias=c_magic[:, 0:1], scale=sscale[:, 0:1])
                    scalar.activation(out=BU, in_=T1, func=AF.Identity,
                                      bias=c_magic[:, 1:2], scale=1.0)
                    scalar.activation(out=T1, in_=BG, func=AF.Identity,
                                      bias=c_magic[:, 0:1], scale=sscale[:, 2:3])
                    scalar.activation(out=TG, in_=T1, func=AF.Identity,
                                      bias=c_magic[:, 1:2], scale=1.0)
                    scalar.activation(out=SSG, in_=TG, func=AF.Sigmoid,
                                      bias=0.0, scale=sscale[:, 3:4])
                sc_b.append(C.cnt['sc'])

                @block.vector
                def _(vector):
                    vector = C.gv(vector)
                    vector.wait_ge(C.s('sc'), sc_b[i])
                    vector.tensor_scalar(out=SSG, in0=SSG, scalar1=65536.0,
                                         scalar2=MAGIC, op0=ALU.mult, op1=ALU.add)
                    vector.tensor_scalar(out=SSG, in0=SSG, scalar1=MAGIC,
                                         scalar2=65535.0, op0=ALU.subtract, op1=ALU.min)
                    vector.scalar_tensor_tensor(out=BG, in0=TG, scalar=sscale[:, 4:5],
                                                in1=SSG, op0=ALU.mult, op1=ALU.mult)
                    vector.reduce_max(out=sacc[:, 1:2], in_=BG, axis=X,
                                      apply_absolute_value=True)
                    if i == 0:
                        vector.tensor_copy(out=sacc[:, 2:3], in_=sacc[:, 1:2])
                    else:
                        vector.tensor_tensor(out=sacc[:, 2:3], in0=sacc[:, 2:3],
                                             in1=sacc[:, 1:2], op=ALU.max)
                vg_b.append(C.cnt['vg'])

                @block.gpsimd
                def _(gpsimd):
                    gpsimd.wait_ge(C.s('sc'), sc_b[i])
                    gpsimd.wait_ge(C.s('vg'), vg_b[i])
                    C.dma(gpsimd, z_qu[i * 128:(i + 1) * 128, :], BU, 'dg')
                    C.dma(gpsimd, z_g2[i * 128:(i + 1) * 128, :], BG, 'dg')
                dg_b.append(C.cnt['dg'])

            @block.sync
            def _(sync):
                C.dma(sync, spad[:], ltmask[:, 0:16], 'ds')

            @block.vector
            def _(vector):
                vector = C.gv(vector)
                tree_max(vector, 2, 12)
                vector.tensor_copy(out=sval[0:1, 13:14], in_=sval[0:1, 12:13])

            emit_allreduce()

            def scales_g2(vector):
                vector.tensor_scalar(out=sval[0:1, 6:7], in0=sval[0:1, 10:11],
                                     scalar1=INV32767, scalar2=EPS_Q,
                                     op0=ALU.mult, op1=ALU.max)
                vector.reciprocal(out=sval[0:1, 5:6], in_=sval[0:1, 6:7])
            emit_scales(scales_g2, 7)

            vg_c = []
            sc_c = []
            ds_c = []
            dg_c = []
            for i in range(NI):
                s4 = i % 4
                BG = af[:, s4 * 2048:(s4 + 1) * 2048]
                BU = af[:, (4 + s4) * 2048:(5 + s4) * 2048]
                UO = a8f(s4)
                QUI = wa[:, s4 * 2048:(s4 + 1) * 2048].bitcast(mybir.dt.int16)
                T1 = stg[:, 0:2048]

                @block.sync
                def _(sync):
                    if i == 0:
                        C.w(sync, 'vg', 'sc', 'dg')
                    if i >= 4:
                        sync.wait_ge(C.s('vg'), vg_c[i - 4])
                    C.dma(sync, BG, z_g2[i * 128:(i + 1) * 128, :], 'ds')
                    C.dma(sync, QUI, z_qu[i * 128:(i + 1) * 128, :], 'ds')
                ds_c.append(C.cnt['ds'])
                ds_cm = ds_c[i] + (32 if i < NI - 1 else 16)

                @block.scalar
                def _(scalar):
                    scalar = C.gs(scalar)
                    scalar.wait_ge(C.s('ds'), ds_cm)
                    scalar.activation(out=T1, in_=BG, func=AF.Identity,
                                      bias=c_magic[:, 0:1], scale=sscale[:, 5:6])
                    scalar.activation(out=BG, in_=T1, func=AF.Identity,
                                      bias=c_magic[:, 1:2], scale=1.0)
                    scalar.mul(out=BG, in_=BG, mul=sscale[:, 6:7])
                sc_c.append(C.cnt['sc'])

                @block.vector
                def _(vector):
                    vector = C.gv(vector)
                    vector.wait_ge(C.s('ds'), ds_cm)
                    if i >= 4:
                        vector.wait_ge(C.s('dg'), dg_c[i - 4])
                    vector.tensor_copy(out=BU, in_=QUI)
                    vector.tensor_single_scalar(out=BU, in_=BU,
                                                scalar=sscale[:, 1:2], op=ALU.mult)
                    vector.wait_ge(C.s('sc'), sc_c[i])
                    vector.tensor_tensor(out=UO, in0=BG, in1=BU, op=ALU.mult)
                    vector.reduce_max(out=sacc[:, 1:2], in_=UO, axis=X,
                                      apply_absolute_value=True)
                    if i == 0:
                        vector.tensor_copy(out=sacc[:, 2:3], in_=sacc[:, 1:2])
                    else:
                        vector.tensor_tensor(out=sacc[:, 2:3], in0=sacc[:, 2:3],
                                             in1=sacc[:, 1:2], op=ALU.max)
                vg_c.append(C.cnt['vg'])

                @block.gpsimd
                def _(gpsimd):
                    gpsimd.wait_ge(C.s('vg'), vg_c[i])
                    C.dma(gpsimd, z_u[i * 128:(i + 1) * 128, :], UO, 'dg')
                dg_c.append(C.cnt['dg'])

            @block.sync
            def _(sync):
                C.dma(sync, spad[:], ltmask[:, 0:16], 'ds')

            @block.vector
            def _(vector):
                vector = C.gv(vector)
                tree_max(vector, 2, 12)
                vector.tensor_copy(out=sval[0:1, 13:14], in_=sval[0:1, 12:13])

            emit_allreduce()

            def scales_o(vector):
                vector.tensor_scalar(out=sval[0:1, 8:9], in0=sval[0:1, 10:11],
                                     scalar1=INV32767, scalar2=EPS_Q,
                                     op0=ALU.mult, op1=ALU.max)
                vector.reciprocal(out=sval[0:1, 7:8], in_=sval[0:1, 8:9])
                vector.tensor_single_scalar(out=sval[0:1, 8:9], in_=sval[0:1, 8:9],
                                            scalar=0.5, op=ALU.mult)
            emit_scales(scales_o, 9)

            @block.sync
            def _(sync):
                C.w(sync, 'tc', 'vg', 'dg')
                for i in range(NI):
                    C.dma(sync, wa.rearrange("p (a c) -> p a c", a=NI)[:, i, :],
                          z_wdT[i], 'ds')
            ds_wd = C.cnt['ds']

            vg_dn = []
            vg_oq = []
            tc_dn = []
            ds_dn = []
            ds_os = []
            for g in range(8):
                p = g % 2
                banks = pb[0:4] if p == 0 else pb[4:8]
                oarea = a8[:, p * 6144:p * 6144 + 6144]
                xres = stg[:, p * 2048:(p + 1) * 2048]
                uarea = af[:, 4096 + p * 6144:4096 + (p + 1) * 6144]
                outt = af[:, p * 2048:(p + 1) * 2048]

                @block.sync
                def _(sync):
                    if g == 0:
                        C.w(sync, 'dg', 'vg', 'sc')
                    if g >= 2:
                        sync.wait_ge(C.s('vg'), vg_oq[g - 2])
                        sync.wait_ge(C.s('vg'), vg_dn[g - 2])
                        sync.wait_ge(C.s('dg'), ds_os[g - 2])
                    C.dma(sync, uarea.rearrange("p (a c) -> p a c", a=NI),
                          z_u[:, g * 256:(g + 1) * 256].rearrange("(a p) c -> p a c", p=128),
                          'ds')
                    C.dma(sync, xres[:, 0:1024], src[g * 256:g * 256 + 128, :], 'ds')
                    C.dma(sync, xres[:, 1024:2048], src[g * 256 + 128:(g + 1) * 256, :], 'ds')
                ds_dn.append(C.cnt['ds'])
                ds_dm = ds_dn[g] + (48 if g < 7 else 16)

                @block.vector
                def _(vector):
                    vector = C.gv(vector)
                    vector.wait_ge(C.s('ds'), ds_dm)
                    if g >= 2:
                        vector.wait_ge(C.s('tc'), tc_dn[g - 2])
                    vector.tensor_scalar(out=uarea, in0=uarea, scalar1=sscale[:, 7:8],
                                         scalar2=MAGIC, op0=ALU.mult, op1=ALU.add)
                    vector.tensor_scalar(out=oarea, in0=uarea, scalar1=MAGIC,
                                         scalar2=sscale[:, 8:9],
                                         op0=ALU.subtract, op1=ALU.mult)
                vg_oq.append(C.cnt['vg'])

                @block.tensor
                def _(tensor):
                    tensor.wait_ge(C.s('vg'), vg_oq[g])
                    tensor.wait_ge(C.s('ds'), ds_wd)
                    if g >= 2:
                        tensor.wait_ge(C.s('vg'), vg_dn[g - 2])
                    for i in range(NI):
                        for tt in range(2):
                            for n2 in range(2):
                                i2 = tensor.matmul(
                                    out=banks[tt * 2 + n2][:, :],
                                    lhsT=oarea.rearrange("p (a c) -> p a c", a=NI)[:, i, tt * 128:(tt + 1) * 128],
                                    rhs=wa.rearrange("p (a c) -> p a c", a=NI)[:, i, n2 * 512:(n2 + 1) * 512],
                                    start=(i == 0), stop=(i == NI - 1))
                    C.inc(i2, 'tc')
                tc_dn.append(C.cnt['tc'])

                @block.vector
                def _(vector):
                    vector = C.gv(vector)
                    vector.wait_ge(C.s('tc'), tc_dn[g])
                    for tt in range(2):
                        for n2 in range(2):
                            vector.scalar_tensor_tensor(
                                out=outt[:, tt * 1024 + n2 * 512:tt * 1024 + (n2 + 1) * 512],
                                in0=banks[tt * 2 + n2][:, :], scalar=1.0,
                                in1=xres[:, tt * 1024 + n2 * 512:tt * 1024 + (n2 + 1) * 512],
                                op0=ALU.mult, op1=ALU.add)
                vg_dn.append(C.cnt['vg'])

                @block.gpsimd
                def _(gpsimd):
                    gpsimd.wait_ge(C.s('vg'), vg_dn[g])
                    C.dma(gpsimd, dst[g * 256:g * 256 + 128, :], outt[:, 0:1024], 'dg')
                    C.dma(gpsimd, dst[g * 256 + 128:(g + 1) * 256, :], outt[:, 1024:2048], 'dg')
                ds_os.append(C.cnt['dg'])

            @block.sync
            def _(sync):
                C.dma(sync, spad[:], ltmask[:, 0:16], 'ds')

        def emit_attention():
            emit_ln(z_x1)
            emit_tiles_to_a8(quant=False)

            wa_dkv = wa[:, 0:2048]
            wa_uk = wa[:, 2048:4096]
            wa_uv = wa[:, 4096:6144]
            wa_kr = wa[:, 6144:6400]
            wa_ckv = wa[:, 6400:10496]
            wa_wo = wa[:, 10496:18688]
            wa_st = wa[:, 18688:21760]

            @block.gpsimd
            def _(gpsimd):
                C.w(gpsimd, 'tc', 'vg')
                C.dma(gpsimd, wa_dkv.rearrange("p (a c) -> p a c", a=ND),
                      wdkv.rearrange("(a p) c -> p a c", p=128), 'dg')
                C.dma(gpsimd, wa_uk.rearrange("p (a c) -> p a c", a=2),
                      wuk.rearrange("(a p) c -> p a c", p=128), 'dg')
                C.dma(gpsimd, wa_uv.rearrange("p (a c) -> p a c", a=2),
                      wuv.rearrange("(a p) c -> p a c", p=128), 'dg')
                C.dma(gpsimd, wa_kr.rearrange("p (a c) -> p a c", a=ND),
                      wkr.rearrange("(a p) c -> p a c", p=128), 'dg')
                C.dma(gpsimd, wa_wo.rearrange("p (a c) -> p a c", a=ND),
                      wo.rearrange("(a p) c -> p a c", p=128), 'dg')
            dg_w = C.cnt['dg']

            for m2 in range(2):
                @block.tensor
                def _(tensor):
                    tensor.wait_ge(C.s('dg'), dg_w)
                    tensor.wait_ge(C.s('vg'), C.cnt['vg'])
                    for kk in range(ND):
                        for n in range(4):
                            i2 = tensor.matmul(
                                out=pb[n][:, :],
                                lhsT=wa_dkv.rearrange("p (a c) -> p a c", a=ND)[:, kk, m2 * 128:(m2 + 1) * 128],
                                rhs=a8.rearrange("p (j c) -> p j c", j=ND)[:, kk, n * 512:(n + 1) * 512],
                                start=(kk == 0), stop=(kk == ND - 1))
                    C.inc(i2, 'tc')
                tc_h = C.cnt['tc']

                @block.vector
                def _(vector):
                    vector = C.gv(vector)
                    vector.wait_ge(C.s('tc'), tc_h)
                    for n in range(4):
                        vector.tensor_copy(
                            out=wa_ckv[:, m2 * 2048 + n * 512:m2 * 2048 + (n + 1) * 512],
                            in_=pb[n][:, :])

            for n in range(4):
                @block.tensor
                def _(tensor):
                    tensor.wait_ge(C.s('vg'), C.cnt['vg'])
                    for kk in range(ND):
                        i2 = tensor.matmul(
                            out=pb[0][0:32, :],
                            lhsT=wa_kr.rearrange("p (a c) -> p a c", a=ND)[:, kk, :],
                            rhs=a8.rearrange("p (j c) -> p j c", j=ND)[:, kk, n * 512:(n + 1) * 512],
                            start=(kk == 0), stop=(kk == ND - 1))
                    C.inc(i2, 'tc')
                tc_h = C.cnt['tc']

                @block.vector
                def _(vector):
                    vector = C.gv(vector)
                    vector.wait_ge(C.s('tc'), tc_h)
                    vector.tensor_tensor(out=skr[0:32, n * 512:(n + 1) * 512],
                                         in0=pb[0][0:32, :],
                                         in1=c_rope[0:32, n * 512:(n + 1) * 512],
                                         op=ALU.mult)

            vg_v = []
            tc_v = []
            ds_v = []
            for t in range(NT):
                p = t % 2
                slot = sbq[:, p * 1024:(p + 1) * 1024]

                @block.tensor
                def _(tensor):
                    if t >= 2:
                        tensor.wait_ge(C.s('vg'), vg_v[t - 2])
                    for k2 in range(2):
                        for n2 in range(2):
                            i2 = tensor.matmul(
                                out=pb[4 + p * 2 + n2][:, :],
                                lhsT=wa_ckv[:, k2 * 2048 + t * 128:k2 * 2048 + (t + 1) * 128],
                                rhs=wa_uv[:, k2 * 1024 + n2 * 512:k2 * 1024 + (n2 + 1) * 512],
                                start=(k2 == 0), stop=(k2 == 1))
                    C.inc(i2, 'tc')
                tc_v.append(C.cnt['tc'])

                @block.vector
                def _(vector):
                    vector = C.gv(vector)
                    vector.wait_ge(C.s('tc'), tc_v[t])
                    if t >= 2:
                        vector.wait_ge(C.s('ds'), ds_v[t - 2])
                    vector.tensor_copy(out=slot[:, 0:512], in_=pb[4 + p * 2][:, :])
                    vector.tensor_copy(out=slot[:, 512:1024], in_=pb[4 + p * 2 + 1][:, :])
                vg_v.append(C.cnt['vg'])

                @block.sync
                def _(sync):
                    sync.wait_ge(C.s('vg'), vg_v[t])
                    C.dma(sync, z_v[t * 128:(t + 1) * 128, :], slot, 'ds')
                ds_v.append(C.cnt['ds'])

            vg_k = []
            ds_k = []
            for h in range(HN):
                p = h % 2
                ktile = skh[p]

                @block.vector
                def _(vector):
                    vector = C.gv(vector)
                    if h >= 2:
                        vector.wait_ge(C.s('ds'), ds_k[h - 2])
                    vector.tensor_copy(out=ktile[64:96, :], in_=skr[0:32, :])
                vg_kr = C.cnt['vg']

                @block.tensor
                def _(tensor):
                    tensor.wait_ge(C.s('vg'), C.cnt['vg'] if h == 0 else vg_k[h - 1])
                    for n in range(4):
                        for k2 in range(2):
                            i2 = tensor.matmul(
                                out=pb[n][0:64, :],
                                lhsT=wa_uk[:, k2 * 1024 + h * 64:k2 * 1024 + (h + 1) * 64],
                                rhs=wa_ckv[:, k2 * 2048 + n * 512:k2 * 2048 + (n + 1) * 512],
                                start=(k2 == 0), stop=(k2 == 1))
                        if n == 3:
                            C.inc(i2, 'tc')
                tc_k = C.cnt['tc']

                @block.vector
                def _(vector):
                    vector = C.gv(vector)
                    vector.wait_ge(C.s('tc'), tc_k)
                    for n in range(4):
                        vector.tensor_copy(out=ktile[0:64, n * 512:(n + 1) * 512],
                                           in_=pb[n][0:64, :])
                vg_k.append(C.cnt['vg'])

                @block.sync
                def _(sync):
                    sync.wait_ge(C.s('vg'), vg_k[h])
                    C.dma(sync, z_kT[h * HD:(h + 1) * HD, :], ktile[0:HD, :], 'ds')
                ds_k.append(C.cnt['ds'])

            @block.sync
            def _(sync):
                C.w(sync, 'vg', 'tc', 'sc', 'ds')

            dg_we = []
            tc_we = []
            for n3 in range(3):
                for kp in range(32):
                    p = kp % 2
                    lslot = wa_st[:, p * 1024:(p + 1) * 1024]
                    rslot = wa_st[:, 2048 + p * 512:2048 + (p + 1) * 512]

                    @block.gpsimd
                    def _(gpsimd):
                        if len(tc_we) >= 2:
                            gpsimd.wait_ge(C.s('tc'), tc_we[-2])
                        C.dma(gpsimd, lslot.rearrange("p (a c) -> p a c", a=ND),
                              wdqT[kp * 128:(kp + 1) * 128, :].rearrange("p (a c) -> p a c", a=ND),
                              'dg')
                        C.dma(gpsimd, rslot,
                              wcatq[kp * 128:(kp + 1) * 128, n3 * 512:(n3 + 1) * 512],
                              'dg')
                    dg_we.append(C.cnt['dg'])

                    @block.tensor
                    def _(tensor):
                        tensor.wait_ge(C.s('dg'), dg_we[-1])
                        if kp == 0:
                            tensor.wait_ge(C.s('vg'), C.cnt['vg'])
                        for m in range(8):
                            i2 = tensor.matmul(
                                out=pb[m][:, :],
                                lhsT=lslot.rearrange("p (a c) -> p a c", a=ND)[:, m, :],
                                rhs=rslot,
                                start=(kp == 0), stop=(kp == 31))
                        C.inc(i2, 'tc')
                    tc_we.append(C.cnt['tc'])

                @block.vector
                def _(vector):
                    vector = C.gv(vector)
                    vector.wait_ge(C.s('tc'), tc_we[-1])
                    for m in range(8):
                        vector.tensor_copy(
                            out=weff[:, m * NQ + n3 * 512:m * NQ + (n3 + 1) * 512],
                            in_=pb[m][:, :])

            vg_q = []
            ds_q = []
            for half in range(2):
                for n in range(4):
                    @block.tensor
                    def _(tensor):
                        tensor.wait_ge(C.s('vg'), C.cnt['vg'])
                        for kk in range(ND):
                            for h8 in range(8):
                                i2 = tensor.matmul(
                                    out=pb[h8][0:HD, :],
                                    lhsT=weff[:, kk * NQ + (half * 8 + h8) * HD:kk * NQ + (half * 8 + h8 + 1) * HD],
                                    rhs=a8.rearrange("p (j c) -> p j c", j=ND)[:, kk, n * 512:(n + 1) * 512],
                                    start=(kk == 0), stop=(kk == ND - 1))
                        C.inc(i2, 'tc')
                    tc_q = C.cnt['tc']

                    for h8 in range(8):
                        p2 = h8 % 2
                        slot = sbq[:, p2 * 1024:p2 * 1024 + 512]

                        @block.vector
                        def _(vector):
                            vector = C.gv(vector)
                            vector.wait_ge(C.s('tc'), tc_q)
                            if len(ds_q) >= 2:
                                vector.wait_ge(C.s('ds'), ds_q[-2])
                            vector.tensor_copy(out=slot[0:64, :], in_=pb[h8][0:64, :])
                            vector.tensor_tensor(out=slot[64:96, :],
                                                 in0=pb[h8][64:96, :],
                                                 in1=c_rope[64:96, n * 512:(n + 1) * 512],
                                                 op=ALU.mult)
                        vg_q.append(C.cnt['vg'])

                        @block.sync
                        def _(sync):
                            sync.wait_ge(C.s('vg'), vg_q[-1])
                            C.dma(sync, z_qT[(half * 8 + h8) * HD:(half * 8 + h8 + 1) * HD,
                                             n * 512:(n + 1) * 512],
                                  slot[0:HD, :], 'ds')
                        ds_q.append(C.cnt['ds'])

            ds_hd = []
            tc_end = []
            vg_hd = []
            for h in range(HN):
                hp = h % 2
                ybase = 0 if hp == 0 else 64
                dbase = 96 if hp == 0 else 32

                @block.sync
                def _(sync):
                    if h == 0:
                        C.w(sync, 'vg')
                    if h >= 2:
                        sync.wait_ge(C.s('tc'), tc_end[h - 2])
                    C.dma(sync, sqh[hp][0:HD, :], z_qT[h * HD:(h + 1) * HD, :], 'ds')
                    C.dma(sync, skh[hp][0:HD, :], z_kT[h * HD:(h + 1) * HD, :], 'ds')
                    C.dma(sync, svh[hp].rearrange("p (a c) -> p a c", a=NT),
                          z_v[:, h * HS:(h + 1) * HS].rearrange("(a p) c -> p a c", p=128),
                          'ds')
                ds_hd.append(C.cnt['ds'])

                steps = []
                for ki in range(NT):
                    nq = T - 128 * ki
                    for qc in range((nq + 511) // 512):
                        w_ = min(512, nq - qc * 512)
                        steps.append((ki, qc, w_))
                tc0 = C.cnt['tc']
                sc0 = C.cnt['sc']

                @block.tensor
                def _(tensor):
                    tensor.wait_ge(C.s('ds'), ds_hd[h])
                    for si, (ki, qc, w_) in enumerate(steps):
                        if si >= 2:
                            tensor.wait_ge(C.s('sc'), sc0 + si - 1)
                        else:
                            tensor.wait_ge(C.s('sc'), sc0)
                        i2 = tensor.matmul(
                            out=pb[si % 2][:, 0:w_],
                            lhsT=skh[hp][0:HD, ki * 128:(ki + 1) * 128],
                            rhs=sqh[hp][0:HD, 128 * ki + qc * 512:128 * ki + qc * 512 + w_],
                            start=True, stop=True)
                        C.inc(i2, 'tc')

                @block.scalar
                def _(scalar):
                    for si, (ki, qc, w_) in enumerate(steps):
                        scalar.wait_ge(C.s('tc'), tc0 + si + 1)
                        i2 = scalar.activation(
                            out=pTa[:, PT_OFF[ki] + qc * 512:PT_OFF[ki] + qc * 512 + w_],
                            in_=pb[si % 2][:, 0:w_],
                            func=AF.Exp, bias=0.0, scale=ISQ96)
                        C.inc(i2, 'sc')
                sc_exp = C.cnt['sc']

                si_of = {}
                for si, (ki, qc, w_) in enumerate(steps):
                    if qc == 0:
                        si_of[ki] = si

                @block.vector
                def _(vector):
                    vector = C.gv(vector)
                    for ki in range(NT):
                        vector.wait_ge(C.s('sc'), sc0 + si_of[ki] + 1)
                        vector.tensor_tensor(
                            out=pTa[:, PT_OFF[ki]:PT_OFF[ki] + 128],
                            in0=pTa[:, PT_OFF[ki]:PT_OFF[ki] + 128],
                            in1=c_lt[:, :], op=ALU.mult)
                vg_mask = C.cnt['vg']

                @block.tensor
                def _(tensor):
                    tensor.wait_ge(C.s('vg'), vg_mask)
                    for n in range(4):
                        for ki in range(4 * n + 4):
                            if 128 * ki <= 512 * n:
                                rhs = pTa[:, PT_OFF[ki] + 512 * n - 128 * ki:
                                          PT_OFF[ki] + 512 * n - 128 * ki + 512]
                                c0, c1 = 0, 512
                                st_ = (ki == 0)
                            else:
                                off = 128 * ki - 512 * n
                                rhs = pTa[:, PT_OFF[ki]:PT_OFF[ki] + 512 - off]
                                c0, c1 = off, 512
                                st_ = False
                            tensor.matmul(out=pb[2 + n][ybase:ybase + 64, c0:c1],
                                          lhsT=svh[hp][:, ki * 64:(ki + 1) * 64],
                                          rhs=rhs, start=st_, stop=(ki == 4 * n + 3))
                            i2 = tensor.matmul(out=pb[2 + n][dbase:dbase + 32, c0:c1],
                                               lhsT=c_ones32[:, :],
                                               rhs=rhs, start=st_, stop=(ki == 4 * n + 3),
                                               tile_position=(0, dbase))
                    C.inc(i2, 'tc')
                tc_pv = C.cnt['tc']

                @block.vector
                def _(vector):
                    vector = C.gv(vector)
                    vector.wait_ge(C.s('tc'), tc_pv)
                    for n in range(4):
                        vector.tensor_copy(out=sd1[32 * n:32 * n + 1, 0:512],
                                           in_=pb[2 + n][dbase:dbase + 1, :])
                    vector.reciprocal(out=sd1[:, 512:1024], in_=sd1[:, 0:512])
                vg_rd = C.cnt['vg']

                tc_k1 = []

                @block.tensor
                def _(tensor):
                    tensor.wait_ge(C.s('vg'), vg_rd)
                    for n in range(4):
                        bank = pb[6] if n < 2 else pb[7]
                        rb = (n % 2) * 64
                        i2 = tensor.matmul(out=bank[rb:rb + 64, :],
                                           lhsT=c_onesf[32 * n:32 * n + 1, :],
                                           rhs=sd1[32 * n:32 * n + 1, 512:1024],
                                           start=True, stop=True,
                                           tile_position=(32 * n, rb))
                        C.inc(i2, 'tc')
                        tc_k1.append(C.cnt['tc'])

                @block.vector
                def _(vector):
                    vector = C.gv(vector)
                    for n in range(4):
                        bank = pb[6] if n < 2 else pb[7]
                        rb = (n % 2) * 64
                        stage = sd1[ybase:ybase + 64,
                                    1024 + (n % 2) * 512:1536 + (n % 2) * 512]
                        vector.wait_ge(C.s('tc'), tc_k1[n])
                        vector.tensor_copy(out=stage, in_=bank[rb:rb + 64, :])
                        vector.tensor_tensor(
                            out=a8.rearrange("p (j c) -> p j c", j=ND)[:, h // 2, n * 512:(n + 1) * 512][ybase:ybase + 64, :],
                            in0=pb[2 + n][ybase:ybase + 64, :],
                            in1=stage,
                            op=ALU.mult)
                vg_hd.append(C.cnt['vg'])
                tc_end.append(C.cnt['tc'])

            if dbg:
                vg_now = C.cnt['vg']

                @block.sync
                def _(sync):
                    sync.wait_ge(C.s('vg'), vg_now)
                    C.dma(sync, d_a8, a8[:, :], 'ds')

            tc_wo = []
            vg_wo = []
            ds_wo = []
            ds_xo = []
            for t in range(NT):
                p = t % 2
                banks = pb[0:2] if p == 0 else pb[2:4]
                xres = stg[:, p * 1024:(p + 1) * 1024]
                outt = sd1[:, p * 1024:(p + 1) * 1024]

                @block.sync
                def _(sync):
                    if t == 0:
                        C.w(sync, 'vg', 'dg')
                    if t >= 2:
                        sync.wait_ge(C.s('vg'), vg_wo[t - 2])
                        sync.wait_ge(C.s('dg'), ds_xo[t - 2])
                    C.dma(sync, xres, z_x1[t * 128:(t + 1) * 128, :], 'ds')
                ds_wo.append(C.cnt['ds'])

                @block.tensor
                def _(tensor):
                    if t == 0:
                        tensor.wait_ge(C.s('vg'), C.cnt['vg'])
                        tensor.wait_ge(C.s('dg'), dg_w)
                    if t >= 2:
                        tensor.wait_ge(C.s('vg'), vg_wo[t - 2])
                    for kk in range(ND):
                        for n2 in range(2):
                            i2 = tensor.matmul(
                                out=banks[n2][:, :],
                                lhsT=a8.rearrange("p (j c) -> p j c", j=ND)[:, kk, t * 128:(t + 1) * 128],
                                rhs=wa_wo[:, kk * 1024 + n2 * 512:kk * 1024 + (n2 + 1) * 512],
                                start=(kk == 0), stop=(kk == ND - 1))
                    C.inc(i2, 'tc')
                tc_wo.append(C.cnt['tc'])

                @block.vector
                def _(vector):
                    vector = C.gv(vector)
                    vector.wait_ge(C.s('tc'), tc_wo[t])
                    vector.wait_ge(C.s('ds'), ds_wo[t])
                    for n2 in range(2):
                        vector.scalar_tensor_tensor(
                            out=outt[:, n2 * 512:(n2 + 1) * 512],
                            in0=banks[n2][:, :], scalar=1.0,
                            in1=xres[:, n2 * 512:(n2 + 1) * 512],
                            op0=ALU.mult, op1=ALU.add)
                vg_wo.append(C.cnt['vg'])

                @block.gpsimd
                def _(gpsimd):
                    gpsimd.wait_ge(C.s('vg'), vg_wo[t])
                    C.dma(gpsimd, z_x2[t * 128:(t + 1) * 128, :], outt, 'dg')
                ds_xo.append(C.cnt['dg'])

        emit_ffn(x_in, z_x1)
        emit_attention()
        emit_ffn(z_x2, out, tag=2)

        if dbg:
            @block.sync
            def _(sync):
                C.w(sync, 'vg', 'sc')
                C.dma(sync, d_stat, sstat[:, 0:64], 'ds')

        @block.sync
        def _(sync):
            C.w(sync, 'ds', 'dg')

    return nc


_NC_CACHE = {}


def kernel(L0_attn_x, L0_attn_W_DQ, L0_attn_W_UQ, L0_attn_W_DKV, L0_attn_W_UK,
           L0_attn_W_UV, L0_attn_W_QR, L0_attn_W_KR, L0_attn_W_O,
           L0_ffn_w_gate, L0_ffn_w_up, L0_ffn_w_down):
    f = np.float32
    x = np.ascontiguousarray(np.asarray(L0_attn_x, dtype=f))
    wdqT = np.ascontiguousarray(np.asarray(L0_attn_W_DQ, dtype=f).T)
    wuq = np.asarray(L0_attn_W_UQ, dtype=f)
    wqr = np.asarray(L0_attn_W_QR, dtype=f)
    wcatq = np.ascontiguousarray(
        np.concatenate([wuq.reshape(4096, HN, HS), wqr.reshape(4096, HN, DR)],
                       axis=2).reshape(4096, NQ))
    wdkv = np.ascontiguousarray(np.asarray(L0_attn_W_DKV, dtype=f))
    wuk = np.ascontiguousarray(np.asarray(L0_attn_W_UK, dtype=f))
    wuv = np.ascontiguousarray(np.asarray(L0_attn_W_UV, dtype=f))
    wkr = np.ascontiguousarray(np.asarray(L0_attn_W_KR, dtype=f))
    wo = np.ascontiguousarray(np.asarray(L0_attn_W_O, dtype=f))
    wg = np.ascontiguousarray(np.asarray(L0_ffn_w_gate, dtype=f))
    wu = np.ascontiguousarray(np.asarray(L0_ffn_w_up, dtype=f))
    wd = np.ascontiguousarray(np.asarray(L0_ffn_w_down, dtype=f))

    t = np.arange(T, dtype=f)
    inv = (f(1.0) / (f(10000.0) ** (np.arange(0, DR, 2, dtype=f) / f(DR)))).astype(f)
    freqs = (t[:, None] * inv[None, :]).astype(f)
    emb = np.concatenate([freqs, freqs], axis=-1)
    rope = (np.cos(emb) + np.sin(emb)).astype(f)
    ropeT = np.ascontiguousarray(rope.T)

    ltm = np.zeros((128, 128), dtype=f)
    for kk in range(128):
        ltm[kk, kk:] = 1.0
    onescol = np.zeros((128, 32), dtype=f)
    onescol[:, 0] = 1.0
    onesf = np.ones((128, 64), dtype=f)
    magicv = np.zeros((128, 2), dtype=f); magicv[:, 0] = f(12582912.0); magicv[:, 1] = f(-12582912.0)
    eye = np.eye(128, dtype=f)

    if "nc" not in _NC_CACHE:
        _NC_CACHE["nc"] = build_nc()
    nc = _NC_CACHE["nc"]

    shared = dict(wdqT=wdqT, wcatq=wcatq, wdkv=wdkv, wuk=wuk, wuv=wuv, wkr=wkr,
                  wo=wo, wg=wg, wu=wu, wd=wd, ropeT=ropeT, ltmask=ltm,
                  onescol=onescol, onesf=onesf, magicv=magicv, eye=eye)
    in_maps = [dict(shared, x=np.ascontiguousarray(x[c])) for c in range(8)]
    res = run_bass_kernel_spmd(nc, in_maps, core_ids=list(range(8)))
    return np.stack([np.asarray(res.results[c]["out"]) for c in range(8)], axis=0)


# version 11
# speedup vs baseline: 1.2270x; 1.0052x over previous
import numpy as np
import ml_dtypes
from contextlib import ExitStack

import concourse.bass as bass
import concourse.mybir as mybir
from concourse.bass_utils import run_bass_kernel_spmd

F32 = mybir.dt.float32
BF16 = mybir.dt.bfloat16
AF = mybir.ActivationFunctionType
ALU = mybir.AluOpType
X = mybir.AxisListType.X

T = 2048
D = 1024
HN = 16
HS = 64
DR = 32
HD = HS + DR
DFF = 3072
NT = T // 128
ND = D // 128
NI = DFF // 128
NQ = HN * HD

MAGIC = float(np.float32(12582912.0))
INV7 = float(np.float32(1.0) / np.float32(7.0))
INV32767 = float(np.float32(1.0) / np.float32(32767.0))
INV1024 = float(np.float32(1.0) / np.float32(1024.0))
ISQ96 = float(np.float32(1.0) / np.float32(np.sqrt(np.float32(96.0))))
EPS_LN = 1e-5
EPS_Q = 1e-12

PT_OFF = [0] * (NT + 1)
for _ki in range(NT):
    PT_OFF[_ki + 1] = PT_OFF[_ki] + (T - 128 * _ki)
PT_TOT = PT_OFF[NT]


class K:
    def __init__(self, block, sems):
        self.block = block
        self.sems = sems
        self.cnt = {k: 0 for k in sems}

    def s(self, name):
        return self.sems[name]

    def inc(self, instr, name, n=1):
        self.cnt[name] += n
        return instr.then_inc(self.sems[name], n)

    def dma(self, eng, out, in_, name, **kw):
        i = eng.dma_start(out=out, in_=in_, **kw)
        self.cnt[name] += 16
        i.then_inc(self.sems[name], 16)
        return i

    def w(self, eng, *names):
        for name in names:
            eng.wait_ge(self.sems[name], self.cnt[name])

    def gv(self, vector):
        C = self

        class GV:
            def wait_ge(self, *a, **k):
                return vector.wait_ge(*a, **k)

            def __getattr__(self, name):
                fn = getattr(vector, name)

                def wrapped(*a, **k):
                    vector.wait_ge(C.s('vg'), C.cnt['vg'])
                    i = fn(*a, **k)
                    C.inc(i, 'vg')
                    return i
                return wrapped
        return GV()

    def gs(self, scalar):
        C = self

        class GS:
            def wait_ge(self, *a, **k):
                return scalar.wait_ge(*a, **k)

            def __getattr__(self, name):
                fn = getattr(scalar, name)

                def wrapped(*a, **k):
                    scalar.wait_ge(C.s('sc'), C.cnt['sc'])
                    i = fn(*a, **k)
                    C.inc(i, 'sc')
                    return i
                return wrapped
        return GS()


def build_nc(dbg=False):
    nc = bass.Bass("TRN2", target_bir_lowering=False, debug=False,
                   detect_race_conditions=False)

    def _dz(name, shape, dt):
        if dbg:
            return nc.dram_tensor(name, shape, dt, kind="ExternalOutput").ap()
        return nc.dram_tensor(name, shape, dt).ap()

    x_in = nc.dram_tensor("x", [T, D], F32, kind="ExternalInput").ap()
    wdqT = nc.dram_tensor("wdqT", [4096, D], F32, kind="ExternalInput").ap()
    wcatq = nc.dram_tensor("wcatq", [4096, NQ], F32, kind="ExternalInput").ap()
    wdkv = nc.dram_tensor("wdkv", [D, 256], F32, kind="ExternalInput").ap()
    wuk = nc.dram_tensor("wuk", [256, D], F32, kind="ExternalInput").ap()
    wuv = nc.dram_tensor("wuv", [256, D], F32, kind="ExternalInput").ap()
    wkr = nc.dram_tensor("wkr", [D, DR], F32, kind="ExternalInput").ap()
    wo = nc.dram_tensor("wo", [D, D], F32, kind="ExternalInput").ap()
    wg = nc.dram_tensor("wg", [DFF, D], F32, kind="ExternalInput").ap()
    wu = nc.dram_tensor("wu", [DFF, D], F32, kind="ExternalInput").ap()
    wd = nc.dram_tensor("wd", [D, DFF], F32, kind="ExternalInput").ap()
    ropeT_in = nc.dram_tensor("ropeT", [DR, T], F32, kind="ExternalInput").ap()
    ltmask = nc.dram_tensor("ltmask", [128, 128], F32, kind="ExternalInput").ap()
    onescol = nc.dram_tensor("onescol", [128, 32], F32, kind="ExternalInput").ap()
    onesf = nc.dram_tensor("onesf", [128, 64], F32, kind="ExternalInput").ap()
    magicv = nc.dram_tensor("magicv", [128, 2], F32, kind="ExternalInput").ap()
    eye_in = nc.dram_tensor("eye", [128, 128], F32, kind="ExternalInput").ap()
    out = nc.dram_tensor("out", [T, D], F32, kind="ExternalOutput").ap()

    z_x1 = _dz("z_x1", [T, D], F32)
    z_x2 = _dz("z_x2", [T, D], F32)
    z_wgT = _dz("z_wgT", [NI, D, 128], BF16)
    z_wuT = _dz("z_wuT", [NI, D, 128], BF16)
    z_wdT = _dz("z_wdT", [NI, 128, D], BF16)
    z_upraw = _dz("z_upraw", [DFF, T], F32)
    z_graw = _dz("z_graw", [DFF, T], F32)
    z_qu = _dz("z_qu", [DFF, T], mybir.dt.int16)
    z_g2 = _dz("z_g2", [DFF, T], F32)
    z_u = _dz("z_u", [DFF, T], F32)
    z_qT = _dz("z_qT", [NQ, T], BF16)
    z_kT = _dz("z_kT", [NQ, T], BF16)
    z_v = _dz("z_v", [T, D], BF16)
    d_a8 = _dz("d_a8", [128, 8 * T], BF16) if dbg else None
    d_a8f = _dz("d_a8f", [128, 8 * T], BF16) if dbg else None
    d_stat = _dz("d_stat", [128, 64], F32) if dbg else None
    cc_in = nc.dram_tensor("cc_in", [1, 2], F32).ap()
    cc_out = nc.dram_tensor("cc_out", [1, 2], F32).ap()
    z_sc = nc.dram_tensor("z_sc", [1, 16], F32).ap()

    with ExitStack() as ctx:
        def sb(name, shape, dt):
            return ctx.enter_context(nc.sbuf_tensor(name, shape, dt))

        c_rope = sb("c_rope", [128, T], F32)
        c_lt = sb("c_lt", [128, 128], BF16)
        c_eye = sb("c_eye", [128, 128], BF16)
        c_ones32 = sb("c_ones32", [128, 32], BF16)
        c_onesf = sb("c_onesf", [128, 64], F32)
        c_magic = sb("c_magic", [128, 2], F32)
        sval = sb("sval", [1, 16], F32)
        sscale = sb("sscale", [128, 16], F32)
        streeg = sb("streeg", [128, 4], F32)
        s32a = sb("s32a", [128, 32], F32)
        s32b = sb("s32b", [128, 32], F32)
        sacc = sb("sacc", [128, 4], F32)
        sstat = sb("sstat", [128, 4 * NT], F32)
        spad = sb("spad", [128, 16], F32)
        af = sb("af", [128, 16384], F32)
        a8 = sb("a8", [128, 8 * T], BF16)
        wa = sb("wa", [128, 24576], BF16)
        stg = sb("stg", [128, 4096], F32)
        sbq = sb("sbq", [128, 2048], BF16)
        sqh = [sb(f"sqh{i}", [HD, T], BF16) for i in range(2)]
        skh = [sb(f"skh{i}", [HD, T], BF16) for i in range(2)]
        svh = [sb(f"svh{i}", [128, HN * HS], BF16) for i in range(2)]
        skr = sb("skr", [DR, T], BF16)
        sd1 = sb("sd1", [128, 2048], F32)

        pb = [ctx.enter_context(nc.psum_tensor(f"pb{i}", [128, 512], F32))
              for i in range(8)]
        pt6 = pb[6][:, :].bitcast(BF16)
        pt7 = pb[7][:, :].bitcast(BF16)
        pt4 = pb[4][:, :].bitcast(BF16)
        pt5 = pb[5][:, :].bitcast(BF16)
        pTa = af[:, 0:8704].bitcast(BF16)
        weff = af[:, 0:6144].bitcast(BF16)

        sems = {}
        for name in ["ds", "dg", "tc", "sc", "cc", "vg"]:
            sems[name] = ctx.enter_context(nc.semaphore(name))
        lo = min(s.num for s in sems.values())
        hi = max(s.num for s in sems.values()) + 1
        nc.gpsimd.sem_clear(range(lo, hi))
        nc.all_engine_barrier()
        block = ctx.enter_context(nc.Block())
        C = K(block, sems)

        @block.sync
        def _(sync):
            C.dma(sync, c_rope[0:32, :], ropeT_in, 'ds')
            C.dma(sync, c_rope[64:96, :], ropeT_in, 'ds')
            C.dma(sync, c_onesf[:, :], onesf, 'ds')
            C.dma(sync, c_magic[:, :], magicv, 'ds')

        @block.gpsimd
        def _(gpsimd):
            C.dma(gpsimd, c_lt[:, :], ltmask, 'dg')
            C.dma(gpsimd, c_eye[:, :], eye_in, 'dg')
            C.dma(gpsimd, c_ones32[:, :], onescol, 'dg')
        dg_consts = C.cnt['dg']

        def tree_max(vector, src_col, dst_col):
            vector.tensor_copy(out=streeg[:64, 0:1], in_=sacc[64:128, src_col:src_col + 1])
            vector.tensor_tensor(out=streeg[:64, 1:2], in0=sacc[:64, src_col:src_col + 1],
                                 in1=streeg[:64, 0:1], op=ALU.max)
            vector.tensor_copy(out=streeg[:32, 2:3], in_=streeg[32:64, 1:2])
            vector.tensor_tensor(out=s32a[:32, 0:1], in0=streeg[:32, 1:2],
                                 in1=streeg[:32, 2:3], op=ALU.max)
            vector.transpose(out=s32b[:32, :32], in_=s32a[:32, :32])
            return vector.reduce_max(out=sval[0:1, dst_col:dst_col + 1],
                                     in_=s32b[0:1, :32], axis=X)

        def emit_allreduce():
            vg_ready = C.cnt['vg']

            @block.gpsimd
            def _(gpsimd):
                gpsimd.wait_ge(C.s('vg'), vg_ready)
                C.dma(gpsimd, cc_in, sval[0:1, 12:14], 'dg')
                C.w(gpsimd, 'dg')
                i = gpsimd.collective_compute(
                    "AllReduce", ALU.max,
                    replica_groups=[list(range(8))],
                    ins=[cc_in.opt()],
                    outs=[cc_out.opt()],
                )
                C.inc(i, 'cc')
                gpsimd.wait_ge(C.s('cc'), C.cnt['cc'])
                C.dma(gpsimd, sval[0:1, 10:12], cc_out, 'dg')

        def emit_scales(build, ncols):
            dg_ready = C.cnt['dg']

            @block.vector
            def _(vector):
                vector = C.gv(vector)
                vector.wait_ge(C.s('dg'), dg_ready)
                build(vector)
            vg_here = C.cnt['vg']

            @block.sync
            def _(sync):
                sync.wait_ge(C.s('vg'), vg_here)
                C.dma(sync, z_sc[0:1, 0:ncols], sval[0:1, 0:ncols], 'ds')
                C.w(sync, 'ds')
                C.dma(sync, sscale[:, 0:ncols],
                      z_sc[0:1, 0:ncols].broadcast_to([128, ncols]), 'ds')
                C.dma(sync, spad[:], ltmask[:, 0:16], 'ds')

        def dequant_core(vector, buf, tq, bm, scl, rcp, outv, nblk):
            vector.reduce_max(out=bm.rearrange("p (a b) -> p a b", b=1),
                              in_=buf.rearrange("p (a b) -> p a b", b=32),
                              axis=X, apply_absolute_value=True)
            vector.tensor_scalar(out=scl, in0=bm, scalar1=INV7, scalar2=EPS_Q,
                                 op0=ALU.mult, op1=ALU.max)
            vector.reciprocal(out=rcp, in_=scl)
            vector.tensor_tensor(out=tq.rearrange("p (a b) -> p a b", b=32),
                                 in0=buf.rearrange("p (a b) -> p a b", b=32),
                                 in1=rcp[:, :, None].broadcast_to([128, nblk, 32]),
                                 op=ALU.mult)
            vector.tensor_scalar(out=tq, in0=tq, scalar1=MAGIC, scalar2=MAGIC,
                                 op0=ALU.add, op1=ALU.subtract)
            return vector.tensor_tensor(out=outv.rearrange("p (a b) -> p a b", b=32),
                                        in0=tq.rearrange("p (a b) -> p a b", b=32),
                                        in1=scl[:, :, None].broadcast_to([128, nblk, 32]),
                                        op=ALU.mult)

        def emit_dequant_gateup():
            vg_after = []
            tc_after = []
            sc_after = []
            ds_ld = []
            ds_st = []
            ptvs = [[pb[2 * k][:, :].bitcast(BF16), pb[2 * k + 1][:, :].bitcast(BF16)]
                    for k in range(4)]
            vg0 = C.cnt['vg']
            tc0 = C.cnt['tc']
            dg0 = C.cnt['dg']
            for i in range(NI):
                p = i % 4
                buf = af[:, p * 2048:(p + 1) * 2048]
                tq = af[:, 8192 + p * 2048:8192 + (p + 1) * 2048]
                bm = sd1[:, p * 256:p * 256 + 64]
                scl = sd1[:, p * 256 + 64:p * 256 + 128]
                rcp = sd1[:, p * 256 + 128:p * 256 + 192]
                obf = a8[:, p * 2048:(p + 1) * 2048]
                pta, ptb = ptvs[p]

                @block.sync
                def _(sync):
                    if i >= 4:
                        sync.wait_ge(C.s('vg'), vg_after[i - 4])
                    else:
                        sync.wait_ge(C.s('vg'), vg0)
                    C.dma(sync, buf[:, 0:1024], wu[i * 128:(i + 1) * 128, :], 'ds')
                    C.dma(sync, buf[:, 1024:2048], wg[i * 128:(i + 1) * 128, :], 'ds')
                ds_ld.append(C.cnt['ds'])
                ds_lm = ds_ld[i] + 16

                @block.vector
                def _(vector):
                    vector = C.gv(vector)
                    vector.wait_ge(C.s('ds'), ds_lm)
                    if i >= 4:
                        vector.wait_ge(C.s('tc'), tc_after[i - 4])
                        vector.wait_ge(C.s('dg'), ds_st[i - 4])
                    else:
                        vector.wait_ge(C.s('tc'), tc0)
                        vector.wait_ge(C.s('dg'), dg0)
                    dequant_core(vector, buf, tq, bm, scl, rcp, obf, 64)
                vg_after.append(C.cnt['vg'])

                @block.tensor
                def _(tensor):
                    tensor.wait_ge(C.s('vg'), vg_after[i])
                    if i == 0:
                        tensor.wait_ge(C.s('dg'), dg_consts)
                    if i >= 4:
                        tensor.wait_ge(C.s('sc'), sc_after[i - 4])
                    for j in range(2 * ND):
                        ptv = pta if j < ND else ptb
                        i2 = tensor.transpose(out=ptv[:, (j % ND) * 128:(j % ND + 1) * 128],
                                              in_=obf[:, j * 128:(j + 1) * 128],
                                              identity=c_eye[:, :])
                    C.inc(i2, 'tc')
                tc_after.append(C.cnt['tc'])

                @block.scalar
                def _(scalar):
                    scalar = C.gs(scalar)
                    scalar.wait_ge(C.s('tc'), tc_after[i])
                    scalar.copy(out=obf[:, 0:1024], in_=pta[:, :])
                    scalar.copy(out=obf[:, 1024:2048], in_=ptb[:, :])
                sc_after.append(C.cnt['sc'])

                @block.gpsimd
                def _(gpsimd):
                    gpsimd.wait_ge(C.s('sc'), sc_after[i])
                    C.dma(gpsimd, z_wuT[i].rearrange("(a p) c -> p a c", p=128),
                          obf[:, 0:1024].rearrange("p (a c) -> p a c", a=ND), 'dg')
                    C.dma(gpsimd, z_wgT[i].rearrange("(a p) c -> p a c", p=128),
                          obf[:, 1024:2048].rearrange("p (a c) -> p a c", a=ND), 'dg')
                ds_st.append(C.cnt['dg'])

            @block.sync
            def _(sync):
                C.dma(sync, spad[:], ltmask[:, 0:16], 'ds')

        def emit_dequant_down(src, dstT):
            for r in range(ND):
                buf = af[:, 0:3072]
                tq = af[:, 4096:7168]
                bm = af[:, 8192:8288]
                scl = af[:, 8448:8544]
                rcp = af[:, 8704:8800]
                obf = af[:, 10240:11776].bitcast(BF16)

                @block.sync
                def _(sync):
                    C.w(sync, 'vg', 'tc')
                    C.dma(sync, buf, src[r * 128:(r + 1) * 128, :], 'ds')
                    C.dma(sync, spad[:], ltmask[:, 0:16], 'ds')
                ds_h = C.cnt['ds']

                @block.vector
                def _(vector):
                    vector = C.gv(vector)
                    vector.wait_ge(C.s('ds'), ds_h)
                    dequant_core(vector, buf, tq, bm, scl, rcp, obf, 96)
                vg_h = C.cnt['vg']

                for rnd in range(3):
                    ptv = pt6 if rnd % 2 == 0 else pt7
                    slot = sbq[:, (rnd % 2) * 1024:(rnd % 2 + 1) * 1024]

                    @block.tensor
                    def _(tensor):
                        if rnd == 0:
                            tensor.wait_ge(C.s('vg'), vg_h)
                            tensor.wait_ge(C.s('dg'), dg_consts)
                        tensor.wait_ge(C.s('dg'), C.cnt['dg'])
                        tensor.wait_ge(C.s('sc'), C.cnt['sc'])
                        for jj in range(8):
                            i2 = tensor.transpose(
                                out=ptv[:, jj * 128:(jj + 1) * 128],
                                in_=obf[:, (rnd * 8 + jj) * 128:(rnd * 8 + jj + 1) * 128],
                                identity=c_eye[:, :])
                        C.inc(i2, 'tc')
                    tc_h = C.cnt['tc']

                    @block.scalar
                    def _(scalar):
                        scalar = C.gs(scalar)
                        scalar.wait_ge(C.s('tc'), tc_h)
                        scalar.copy(out=slot, in_=ptv[:, :])
                    sc_h = C.cnt['sc']

                    @block.gpsimd
                    def _(gpsimd):
                        gpsimd.wait_ge(C.s('sc'), sc_h)
                        C.dma(gpsimd,
                              dstT[rnd * 8:(rnd + 1) * 8, :, r * 128:(r + 1) * 128]
                              .rearrange("a p c -> p a c"),
                              slot.rearrange("p (a c) -> p a c", a=8), 'dg')

        emit_dequant_gateup()
        emit_dequant_down(wd, z_wdT)

        def emit_ln(src):
            ds_after = []
            for t in range(NT):
                xt = af[:, t * 1024:(t + 1) * 1024]

                @block.sync
                def _(sync):
                    if t == 0:
                        C.w(sync, 'vg', 'tc', 'sc', 'ds', 'dg')
                    C.dma(sync, xt, src[t * 128:(t + 1) * 128, :], 'ds')
                ds_after.append(C.cnt['ds'])

            @block.sync
            def _(sync):
                C.dma(sync, spad[:], ltmask[:, 0:16], 'ds')

            for t in range(NT):
                xt = af[:, t * 1024:(t + 1) * 1024]

                @block.vector
                def _(vector):
                    vector = C.gv(vector)
                    vector.wait_ge(C.s('ds'), ds_after[t] + 16)
                    vector.reduce_sum(out=sstat[:, t:t + 1], in_=xt, axis=X)
                    vector.tensor_single_scalar(out=sstat[:, NT + t:NT + t + 1],
                                                in_=sstat[:, t:t + 1],
                                                scalar=-INV1024, op=ALU.mult)
                vg_t = C.cnt['vg']

                @block.scalar
                def _(scalar):
                    scalar = C.gs(scalar)
                    scalar.wait_ge(C.s('vg'), vg_t)
                    scalar.activation(out=stg[:, (t % 2) * 1024:(t % 2 + 1) * 1024],
                                      in_=xt, func=AF.Square,
                                      bias=sstat[:, NT + t:NT + t + 1], scale=1.0,
                                      accum_out=sstat[:, 2 * NT + t:2 * NT + t + 1])
            sc_sq = C.cnt['sc']

            @block.vector
            def _(vector):
                vector = C.gv(vector)
                vector.wait_ge(C.s('sc'), sc_sq)
                vector.tensor_scalar(out=sstat[:, 2 * NT:3 * NT],
                                     in0=sstat[:, 2 * NT:3 * NT],
                                     scalar1=INV1024, scalar2=EPS_LN,
                                     op0=ALU.mult, op1=ALU.add)
            vg_var = C.cnt['vg']

            @block.scalar
            def _(scalar):
                scalar = C.gs(scalar)
                scalar.wait_ge(C.s('vg'), vg_var)
                scalar.sqrt(out=sstat[:, 2 * NT:3 * NT], in_=sstat[:, 2 * NT:3 * NT])
            sc_sqrt = C.cnt['sc']

            @block.vector
            def _(vector):
                vector = C.gv(vector)
                vector.wait_ge(C.s('sc'), sc_sqrt)
                vector.reciprocal(out=sstat[:, 2 * NT:3 * NT], in_=sstat[:, 2 * NT:3 * NT])
                vector.tensor_tensor(out=sstat[:, 3 * NT:4 * NT],
                                     in0=sstat[:, NT:2 * NT],
                                     in1=sstat[:, 2 * NT:3 * NT], op=ALU.mult)
            vg_rstd = C.cnt['vg']

            for t in range(NT):
                xt = af[:, t * 1024:(t + 1) * 1024]

                @block.scalar
                def _(scalar):
                    scalar = C.gs(scalar)
                    if t == 0:
                        scalar.wait_ge(C.s('vg'), vg_rstd)
                    scalar.activation(out=xt, in_=xt, func=AF.Identity,
                                      bias=sstat[:, 3 * NT + t:3 * NT + t + 1],
                                      scale=sstat[:, 2 * NT + t:2 * NT + t + 1])
                sc_t = C.cnt['sc']

                @block.vector
                def _(vector):
                    vector = C.gv(vector)
                    vector.wait_ge(C.s('sc'), sc_t)
                    vector.reduce_max(out=sacc[:, 1:2], in_=xt, axis=X,
                                      apply_absolute_value=True)
                    if t == 0:
                        vector.tensor_copy(out=sacc[:, 0:1], in_=sacc[:, 1:2])
                    else:
                        vector.tensor_tensor(out=sacc[:, 0:1], in0=sacc[:, 0:1],
                                             in1=sacc[:, 1:2], op=ALU.max)

        def emit_tiles_to_a8(quant):
            vg_evac = []
            tc_tr = []
            for t in range(NT):
                p = t % 2
                xt = af[:, t * 1024:(t + 1) * 1024]
                slot = sbq[:, p * 1024:(p + 1) * 1024]
                ptv = pt6 if p == 0 else pt7

                @block.vector
                def _(vector):
                    vector = C.gv(vector)
                    if t == 0:
                        C.w(vector, 'ds')
                    if t >= 2:
                        vector.wait_ge(C.s('tc'), tc_tr[t - 2])
                    if quant:
                        vector.tensor_scalar(out=xt, in0=xt, scalar1=sscale[:, 0:1],
                                             scalar2=MAGIC, op0=ALU.mult, op1=ALU.add)
                        vector.tensor_scalar(out=slot, in0=xt, scalar1=MAGIC,
                                             scalar2=sscale[:, 1:2],
                                             op0=ALU.subtract, op1=ALU.mult)
                    else:
                        vector.tensor_copy(out=slot, in_=xt)
                vg_q = C.cnt['vg']

                @block.tensor
                def _(tensor):
                    tensor.wait_ge(C.s('vg'), vg_q)
                    if t == 0:
                        tensor.wait_ge(C.s('dg'), dg_consts)
                    if t >= 2:
                        tensor.wait_ge(C.s('vg'), vg_evac[t - 2])
                    for j in range(ND):
                        i2 = tensor.transpose(out=ptv[:, j * 128:(j + 1) * 128],
                                              in_=slot[:, j * 128:(j + 1) * 128],
                                              identity=c_eye[:, :])
                    C.inc(i2, 'tc')
                tc_tr.append(C.cnt['tc'])

                @block.vector
                def _(vector):
                    vector = C.gv(vector)
                    vector.wait_ge(C.s('tc'), tc_tr[t])
                    vector.tensor_copy(
                        out=a8.rearrange("p (j c) -> p j c", j=ND)[:, :, t * 128:(t + 1) * 128],
                        in_=ptv.rearrange("p (j c) -> p j c", j=ND))
                vg_evac.append(C.cnt['vg'])

        def emit_ffn(src, dst, tag=1):
            emit_ln(src)

            @block.vector
            def _(vector):
                vector = C.gv(vector)
                tree_max(vector, 0, 12)
                vector.tensor_copy(out=sval[0:1, 13:14], in_=sval[0:1, 12:13])

            emit_allreduce()

            def scales_x(vector):
                vector.tensor_scalar(out=sval[0:1, 1:2], in0=sval[0:1, 10:11],
                                     scalar1=INV32767, scalar2=EPS_Q,
                                     op0=ALU.mult, op1=ALU.max)
                vector.reciprocal(out=sval[0:1, 0:1], in_=sval[0:1, 1:2])
            emit_scales(scales_x, 2)

            emit_tiles_to_a8(quant=True)

            if dbg and tag == 2:
                vg_now2 = C.cnt['vg']

                @block.sync
                def _(sync):
                    sync.wait_ge(C.s('vg'), vg_now2)
                    C.dma(sync, d_a8f, a8[:, :], 'ds')

            def gateup(wT, dst_raw, acc_col):
                vg_evac = []
                tc_mm = []
                ds_ld = []
                ds_st = []
                tc0 = C.cnt['tc']
                dg0 = C.cnt['dg']
                for i in range(NI):
                    p = i % 2
                    wtile = wa[:, p * 1024:(p + 1) * 1024]
                    banks = pb[0:4] if p == 0 else pb[4:8]
                    stgh = stg[:, 0:2048] if p == 0 else stg[:, 2048:4096]

                    @block.sync
                    def _(sync):
                        if i < 2:
                            sync.wait_ge(C.s('dg'), dg0)
                            sync.wait_ge(C.s('tc'), tc0)
                        if i >= 2:
                            sync.wait_ge(C.s('tc'), tc_mm[i - 2])
                        C.dma(sync, wtile.rearrange("p (a c) -> p a c", a=ND),
                              wT[i].rearrange("(a p) c -> p a c", p=128), 'ds')
                    ds_ld.append(C.cnt['ds'])

                    @block.tensor
                    def _(tensor):
                        tensor.wait_ge(C.s('ds'), ds_ld[i] + 16)
                        if i == 0:
                            tensor.wait_ge(C.s('vg'), C.cnt['vg'])
                        if i >= 2:
                            tensor.wait_ge(C.s('vg'), vg_evac[i - 2])
                        for kk in range(ND):
                            for n in range(4):
                                i2 = tensor.matmul(
                                    out=banks[n][:, :],
                                    lhsT=wtile.rearrange("p (a c) -> p a c", a=ND)[:, kk, :],
                                    rhs=a8.rearrange("p (j c) -> p j c", j=ND)[:, kk, n * 512:(n + 1) * 512],
                                    start=(kk == 0), stop=(kk == ND - 1))
                        C.inc(i2, 'tc')
                    tc_mm.append(C.cnt['tc'])

                    @block.vector
                    def _(vector):
                        vector = C.gv(vector)
                        vector.wait_ge(C.s('tc'), tc_mm[i])
                        if i >= 2:
                            vector.wait_ge(C.s('dg'), ds_st[i - 2])
                        else:
                            C.w(vector, 'dg')
                        for n in range(4):
                            vector.tensor_copy(out=stgh[:, n * 512:(n + 1) * 512],
                                               in_=banks[n][:, :])
                        vector.reduce_max(out=sacc[:, 1:2], in_=stgh,
                                          axis=X, apply_absolute_value=True)
                        if i == 0:
                            vector.tensor_copy(out=sacc[:, acc_col:acc_col + 1],
                                               in_=sacc[:, 1:2])
                        else:
                            vector.tensor_tensor(out=sacc[:, acc_col:acc_col + 1],
                                                 in0=sacc[:, acc_col:acc_col + 1],
                                                 in1=sacc[:, 1:2], op=ALU.max)
                    vg_evac.append(C.cnt['vg'])

                    @block.gpsimd
                    def _(gpsimd):
                        gpsimd.wait_ge(C.s('vg'), vg_evac[i])
                        C.dma(gpsimd, dst_raw[i * 128:(i + 1) * 128, :], stgh, 'dg')
                    ds_st.append(C.cnt['dg'])

                @block.sync
                def _(sync):
                    C.dma(sync, spad[:], ltmask[:, 0:16], 'ds')

            gateup(z_wuT, z_upraw, 2)
            gateup(z_wgT, z_graw, 3)

            @block.vector
            def _(vector):
                vector = C.gv(vector)
                tree_max(vector, 2, 12)
                tree_max(vector, 3, 13)

            emit_allreduce()

            def scales_ug(vector):
                vector.tensor_scalar(out=sval[0:1, 1:2], in0=sval[0:1, 10:11],
                                     scalar1=INV32767, scalar2=EPS_Q,
                                     op0=ALU.mult, op1=ALU.max)
                vector.tensor_scalar(out=sval[0:1, 3:4], in0=sval[0:1, 11:12],
                                     scalar1=INV32767, scalar2=EPS_Q,
                                     op0=ALU.mult, op1=ALU.max)
                vector.reciprocal(out=sval[0:1, 0:1], in_=sval[0:1, 1:2])
                vector.reciprocal(out=sval[0:1, 2:3], in_=sval[0:1, 3:4])
                vector.tensor_single_scalar(out=sval[0:1, 4:5], in_=sval[0:1, 3:4],
                                            scalar=float(2.0 ** -16), op=ALU.mult)
            emit_scales(scales_ug, 5)

            def a8f(k):
                return a8[:, k * 4096:(k + 1) * 4096].bitcast(F32)

            def waf(k):
                return wa[:, k * 4096:(k + 1) * 4096].bitcast(F32)
            vg_b = []
            sc_b = []
            ds_b = []
            dg_b = []
            for i in range(NI):
                s4 = i % 4
                BU = af[:, s4 * 2048:(s4 + 1) * 2048]
                BG = af[:, (4 + s4) * 2048:(5 + s4) * 2048]
                TG = a8f(s4)
                SSG = waf(s4)
                T1 = stg[:, 0:2048]

                @block.sync
                def _(sync):
                    if i == 0:
                        C.w(sync, 'vg', 'tc', 'sc', 'dg')
                    if i >= 4:
                        sync.wait_ge(C.s('dg'), dg_b[i - 4])
                    C.dma(sync, BU, z_upraw[i * 128:(i + 1) * 128, :], 'ds')
                    C.dma(sync, BG, z_graw[i * 128:(i + 1) * 128, :], 'ds')
                ds_b.append(C.cnt['ds'])
                ds_bm = ds_b[i] + (32 if i < NI - 1 else 16)

                @block.scalar
                def _(scalar):
                    scalar = C.gs(scalar)
                    scalar.wait_ge(C.s('ds'), ds_bm)
                    if i >= 4:
                        scalar.wait_ge(C.s('vg'), vg_b[i - 4])
                    scalar.activation(out=T1, in_=BU, func=AF.Identity,
                                      bias=c_magic[:, 0:1], scale=sscale[:, 0:1])
                    scalar.activation(out=BU, in_=T1, func=AF.Identity,
                                      bias=c_magic[:, 1:2], scale=1.0)
                    scalar.activation(out=T1, in_=BG, func=AF.Identity,
                                      bias=c_magic[:, 0:1], scale=sscale[:, 2:3])
                    scalar.activation(out=TG, in_=T1, func=AF.Identity,
                                      bias=c_magic[:, 1:2], scale=1.0)
                    scalar.activation(out=SSG, in_=TG, func=AF.Sigmoid,
                                      bias=0.0, scale=sscale[:, 3:4])
                sc_b.append(C.cnt['sc'])

                @block.vector
                def _(vector):
                    vector = C.gv(vector)
                    vector.wait_ge(C.s('sc'), sc_b[i])
                    vector.tensor_scalar(out=SSG, in0=SSG, scalar1=65536.0,
                                         scalar2=MAGIC, op0=ALU.mult, op1=ALU.add)
                    vector.tensor_scalar(out=SSG, in0=SSG, scalar1=MAGIC,
                                         scalar2=65535.0, op0=ALU.subtract, op1=ALU.min)
                    vector.scalar_tensor_tensor(out=BG, in0=TG, scalar=sscale[:, 4:5],
                                                in1=SSG, op0=ALU.mult, op1=ALU.mult)
                    vector.reduce_max(out=sacc[:, 1:2], in_=BG, axis=X,
                                      apply_absolute_value=True)
                    if i == 0:
                        vector.tensor_copy(out=sacc[:, 2:3], in_=sacc[:, 1:2])
                    else:
                        vector.tensor_tensor(out=sacc[:, 2:3], in0=sacc[:, 2:3],
                                             in1=sacc[:, 1:2], op=ALU.max)
                vg_b.append(C.cnt['vg'])

                @block.gpsimd
                def _(gpsimd):
                    gpsimd.wait_ge(C.s('sc'), sc_b[i])
                    gpsimd.wait_ge(C.s('vg'), vg_b[i])
                    C.dma(gpsimd, z_qu[i * 128:(i + 1) * 128, :], BU, 'dg')
                    C.dma(gpsimd, z_g2[i * 128:(i + 1) * 128, :], BG, 'dg')
                dg_b.append(C.cnt['dg'])

            @block.sync
            def _(sync):
                C.dma(sync, spad[:], ltmask[:, 0:16], 'ds')

            @block.vector
            def _(vector):
                vector = C.gv(vector)
                tree_max(vector, 2, 12)
                vector.tensor_copy(out=sval[0:1, 13:14], in_=sval[0:1, 12:13])

            emit_allreduce()

            def scales_g2(vector):
                vector.tensor_scalar(out=sval[0:1, 6:7], in0=sval[0:1, 10:11],
                                     scalar1=INV32767, scalar2=EPS_Q,
                                     op0=ALU.mult, op1=ALU.max)
                vector.reciprocal(out=sval[0:1, 5:6], in_=sval[0:1, 6:7])
            emit_scales(scales_g2, 7)

            vg_c = []
            sc_c = []
            ds_c = []
            dg_c = []
            for i in range(NI):
                s4 = i % 4
                BG = af[:, s4 * 2048:(s4 + 1) * 2048]
                BU = af[:, (4 + s4) * 2048:(5 + s4) * 2048]
                UO = a8f(s4)
                QUI = wa[:, s4 * 2048:(s4 + 1) * 2048].bitcast(mybir.dt.int16)
                T1 = stg[:, 0:2048]

                @block.sync
                def _(sync):
                    if i == 0:
                        C.w(sync, 'vg', 'sc', 'dg')
                    if i >= 4:
                        sync.wait_ge(C.s('vg'), vg_c[i - 4])
                    C.dma(sync, BG, z_g2[i * 128:(i + 1) * 128, :], 'ds')
                    C.dma(sync, QUI, z_qu[i * 128:(i + 1) * 128, :], 'ds')
                ds_c.append(C.cnt['ds'])
                ds_cm = ds_c[i] + (32 if i < NI - 1 else 16)

                @block.scalar
                def _(scalar):
                    scalar = C.gs(scalar)
                    scalar.wait_ge(C.s('ds'), ds_cm)
                    scalar.activation(out=T1, in_=BG, func=AF.Identity,
                                      bias=c_magic[:, 0:1], scale=sscale[:, 5:6])
                    scalar.activation(out=BG, in_=T1, func=AF.Identity,
                                      bias=c_magic[:, 1:2], scale=1.0)
                    scalar.mul(out=BG, in_=BG, mul=sscale[:, 6:7])
                sc_c.append(C.cnt['sc'])

                @block.vector
                def _(vector):
                    vector = C.gv(vector)
                    vector.wait_ge(C.s('ds'), ds_cm)
                    if i >= 4:
                        vector.wait_ge(C.s('dg'), dg_c[i - 4])
                    vector.tensor_copy(out=BU, in_=QUI)
                    vector.tensor_single_scalar(out=BU, in_=BU,
                                                scalar=sscale[:, 1:2], op=ALU.mult)
                    vector.wait_ge(C.s('sc'), sc_c[i])
                    vector.tensor_tensor(out=UO, in0=BG, in1=BU, op=ALU.mult)
                    vector.reduce_max(out=sacc[:, 1:2], in_=UO, axis=X,
                                      apply_absolute_value=True)
                    if i == 0:
                        vector.tensor_copy(out=sacc[:, 2:3], in_=sacc[:, 1:2])
                    else:
                        vector.tensor_tensor(out=sacc[:, 2:3], in0=sacc[:, 2:3],
                                             in1=sacc[:, 1:2], op=ALU.max)
                vg_c.append(C.cnt['vg'])

                @block.gpsimd
                def _(gpsimd):
                    gpsimd.wait_ge(C.s('vg'), vg_c[i])
                    C.dma(gpsimd, z_u[i * 128:(i + 1) * 128, :], UO, 'dg')
                dg_c.append(C.cnt['dg'])

            @block.sync
            def _(sync):
                C.dma(sync, spad[:], ltmask[:, 0:16], 'ds')

            @block.vector
            def _(vector):
                vector = C.gv(vector)
                tree_max(vector, 2, 12)
                vector.tensor_copy(out=sval[0:1, 13:14], in_=sval[0:1, 12:13])

            emit_allreduce()

            def scales_o(vector):
                vector.tensor_scalar(out=sval[0:1, 8:9], in0=sval[0:1, 10:11],
                                     scalar1=INV32767, scalar2=EPS_Q,
                                     op0=ALU.mult, op1=ALU.max)
                vector.reciprocal(out=sval[0:1, 7:8], in_=sval[0:1, 8:9])
                vector.tensor_single_scalar(out=sval[0:1, 8:9], in_=sval[0:1, 8:9],
                                            scalar=0.5, op=ALU.mult)
            emit_scales(scales_o, 9)

            @block.sync
            def _(sync):
                C.w(sync, 'tc', 'vg', 'dg')
                for i in range(NI):
                    C.dma(sync, wa.rearrange("p (a c) -> p a c", a=NI)[:, i, :],
                          z_wdT[i], 'ds')
            ds_wd = C.cnt['ds']

            vg_dn = []
            vg_oq = []
            tc_dn = []
            ds_dn = []
            ds_os = []
            for g in range(8):
                p = g % 2
                banks = pb[0:4] if p == 0 else pb[4:8]
                oarea = a8[:, p * 6144:p * 6144 + 6144]
                xres = stg[:, p * 2048:(p + 1) * 2048]
                uarea = af[:, 4096 + p * 6144:4096 + (p + 1) * 6144]
                outt = af[:, p * 2048:(p + 1) * 2048]

                @block.sync
                def _(sync):
                    if g == 0:
                        C.w(sync, 'dg', 'vg', 'sc')
                    if g >= 2:
                        sync.wait_ge(C.s('vg'), vg_oq[g - 2])
                        sync.wait_ge(C.s('vg'), vg_dn[g - 2])
                        sync.wait_ge(C.s('dg'), ds_os[g - 2])
                    C.dma(sync, uarea.rearrange("p (a c) -> p a c", a=NI),
                          z_u[:, g * 256:(g + 1) * 256].rearrange("(a p) c -> p a c", p=128),
                          'ds')
                    C.dma(sync, xres[:, 0:1024], src[g * 256:g * 256 + 128, :], 'ds')
                    C.dma(sync, xres[:, 1024:2048], src[g * 256 + 128:(g + 1) * 256, :], 'ds')
                ds_dn.append(C.cnt['ds'])
                ds_dm = ds_dn[g] + (48 if g < 7 else 16)

                @block.vector
                def _(vector):
                    vector = C.gv(vector)
                    vector.wait_ge(C.s('ds'), ds_dm)
                    if g >= 2:
                        vector.wait_ge(C.s('tc'), tc_dn[g - 2])
                    vector.tensor_scalar(out=uarea, in0=uarea, scalar1=sscale[:, 7:8],
                                         scalar2=MAGIC, op0=ALU.mult, op1=ALU.add)
                    vector.tensor_scalar(out=oarea, in0=uarea, scalar1=MAGIC,
                                         scalar2=sscale[:, 8:9],
                                         op0=ALU.subtract, op1=ALU.mult)
                vg_oq.append(C.cnt['vg'])

                @block.tensor
                def _(tensor):
                    tensor.wait_ge(C.s('vg'), vg_oq[g])
                    tensor.wait_ge(C.s('ds'), ds_wd)
                    if g >= 2:
                        tensor.wait_ge(C.s('vg'), vg_dn[g - 2])
                    for i in range(NI):
                        for tt in range(2):
                            for n2 in range(2):
                                i2 = tensor.matmul(
                                    out=banks[tt * 2 + n2][:, :],
                                    lhsT=oarea.rearrange("p (a c) -> p a c", a=NI)[:, i, tt * 128:(tt + 1) * 128],
                                    rhs=wa.rearrange("p (a c) -> p a c", a=NI)[:, i, n2 * 512:(n2 + 1) * 512],
                                    start=(i == 0), stop=(i == NI - 1))
                    C.inc(i2, 'tc')
                tc_dn.append(C.cnt['tc'])

                @block.vector
                def _(vector):
                    vector = C.gv(vector)
                    vector.wait_ge(C.s('tc'), tc_dn[g])
                    for tt in range(2):
                        for n2 in range(2):
                            vector.scalar_tensor_tensor(
                                out=outt[:, tt * 1024 + n2 * 512:tt * 1024 + (n2 + 1) * 512],
                                in0=banks[tt * 2 + n2][:, :], scalar=1.0,
                                in1=xres[:, tt * 1024 + n2 * 512:tt * 1024 + (n2 + 1) * 512],
                                op0=ALU.mult, op1=ALU.add)
                vg_dn.append(C.cnt['vg'])

                @block.gpsimd
                def _(gpsimd):
                    gpsimd.wait_ge(C.s('vg'), vg_dn[g])
                    C.dma(gpsimd, dst[g * 256:g * 256 + 128, :], outt[:, 0:1024], 'dg')
                    C.dma(gpsimd, dst[g * 256 + 128:(g + 1) * 256, :], outt[:, 1024:2048], 'dg')
                ds_os.append(C.cnt['dg'])

            @block.sync
            def _(sync):
                C.dma(sync, spad[:], ltmask[:, 0:16], 'ds')

        def emit_attention():
            emit_ln(z_x1)
            emit_tiles_to_a8(quant=False)

            wa_dkv = wa[:, 0:2048]
            wa_uk = wa[:, 2048:4096]
            wa_uv = wa[:, 4096:6144]
            wa_kr = wa[:, 6144:6400]
            wa_ckv = wa[:, 6400:10496]
            wa_wo = wa[:, 10496:18688]
            wa_st = wa[:, 18688:21760]

            @block.gpsimd
            def _(gpsimd):
                C.w(gpsimd, 'tc', 'vg')
                C.dma(gpsimd, wa_dkv.rearrange("p (a c) -> p a c", a=ND),
                      wdkv.rearrange("(a p) c -> p a c", p=128), 'dg')
                C.dma(gpsimd, wa_uk.rearrange("p (a c) -> p a c", a=2),
                      wuk.rearrange("(a p) c -> p a c", p=128), 'dg')
                C.dma(gpsimd, wa_uv.rearrange("p (a c) -> p a c", a=2),
                      wuv.rearrange("(a p) c -> p a c", p=128), 'dg')
                C.dma(gpsimd, wa_kr.rearrange("p (a c) -> p a c", a=ND),
                      wkr.rearrange("(a p) c -> p a c", p=128), 'dg')
                C.dma(gpsimd, wa_wo.rearrange("p (a c) -> p a c", a=ND),
                      wo.rearrange("(a p) c -> p a c", p=128), 'dg')
            dg_w = C.cnt['dg']

            for m2 in range(2):
                @block.tensor
                def _(tensor):
                    tensor.wait_ge(C.s('dg'), dg_w)
                    tensor.wait_ge(C.s('vg'), C.cnt['vg'])
                    for kk in range(ND):
                        for n in range(4):
                            i2 = tensor.matmul(
                                out=pb[n][:, :],
                                lhsT=wa_dkv.rearrange("p (a c) -> p a c", a=ND)[:, kk, m2 * 128:(m2 + 1) * 128],
                                rhs=a8.rearrange("p (j c) -> p j c", j=ND)[:, kk, n * 512:(n + 1) * 512],
                                start=(kk == 0), stop=(kk == ND - 1))
                    C.inc(i2, 'tc')
                tc_h = C.cnt['tc']

                @block.vector
                def _(vector):
                    vector = C.gv(vector)
                    vector.wait_ge(C.s('tc'), tc_h)
                    for n in range(4):
                        vector.tensor_copy(
                            out=wa_ckv[:, m2 * 2048 + n * 512:m2 * 2048 + (n + 1) * 512],
                            in_=pb[n][:, :])

            for n in range(4):
                @block.tensor
                def _(tensor):
                    tensor.wait_ge(C.s('vg'), C.cnt['vg'])
                    for kk in range(ND):
                        i2 = tensor.matmul(
                            out=pb[0][0:32, :],
                            lhsT=wa_kr.rearrange("p (a c) -> p a c", a=ND)[:, kk, :],
                            rhs=a8.rearrange("p (j c) -> p j c", j=ND)[:, kk, n * 512:(n + 1) * 512],
                            start=(kk == 0), stop=(kk == ND - 1))
                    C.inc(i2, 'tc')
                tc_h = C.cnt['tc']

                @block.vector
                def _(vector):
                    vector = C.gv(vector)
                    vector.wait_ge(C.s('tc'), tc_h)
                    vector.tensor_tensor(out=skr[0:32, n * 512:(n + 1) * 512],
                                         in0=pb[0][0:32, :],
                                         in1=c_rope[0:32, n * 512:(n + 1) * 512],
                                         op=ALU.mult)

            vg_v = []
            tc_v = []
            ds_v = []
            for t in range(NT):
                p = t % 2
                slot = sbq[:, p * 1024:(p + 1) * 1024]

                @block.tensor
                def _(tensor):
                    if t >= 2:
                        tensor.wait_ge(C.s('vg'), vg_v[t - 2])
                    for k2 in range(2):
                        for n2 in range(2):
                            i2 = tensor.matmul(
                                out=pb[4 + p * 2 + n2][:, :],
                                lhsT=wa_ckv[:, k2 * 2048 + t * 128:k2 * 2048 + (t + 1) * 128],
                                rhs=wa_uv[:, k2 * 1024 + n2 * 512:k2 * 1024 + (n2 + 1) * 512],
                                start=(k2 == 0), stop=(k2 == 1))
                    C.inc(i2, 'tc')
                tc_v.append(C.cnt['tc'])

                @block.vector
                def _(vector):
                    vector = C.gv(vector)
                    vector.wait_ge(C.s('tc'), tc_v[t])
                    if t >= 2:
                        vector.wait_ge(C.s('ds'), ds_v[t - 2])
                    vector.tensor_copy(out=slot[:, 0:512], in_=pb[4 + p * 2][:, :])
                    vector.tensor_copy(out=slot[:, 512:1024], in_=pb[4 + p * 2 + 1][:, :])
                vg_v.append(C.cnt['vg'])

                @block.sync
                def _(sync):
                    sync.wait_ge(C.s('vg'), vg_v[t])
                    C.dma(sync, z_v[t * 128:(t + 1) * 128, :], slot, 'ds')
                ds_v.append(C.cnt['ds'])

            vg_k = []
            ds_k = []
            for h in range(HN):
                p = h % 2
                ktile = skh[p]

                @block.vector
                def _(vector):
                    vector = C.gv(vector)
                    if h >= 2:
                        vector.wait_ge(C.s('ds'), ds_k[h - 2])
                    vector.tensor_copy(out=ktile[64:96, :], in_=skr[0:32, :])
                vg_kr = C.cnt['vg']

                @block.tensor
                def _(tensor):
                    tensor.wait_ge(C.s('vg'), C.cnt['vg'] if h == 0 else vg_k[h - 1])
                    for n in range(4):
                        for k2 in range(2):
                            i2 = tensor.matmul(
                                out=pb[n][0:64, :],
                                lhsT=wa_uk[:, k2 * 1024 + h * 64:k2 * 1024 + (h + 1) * 64],
                                rhs=wa_ckv[:, k2 * 2048 + n * 512:k2 * 2048 + (n + 1) * 512],
                                start=(k2 == 0), stop=(k2 == 1))
                        if n == 3:
                            C.inc(i2, 'tc')
                tc_k = C.cnt['tc']

                @block.vector
                def _(vector):
                    vector = C.gv(vector)
                    vector.wait_ge(C.s('tc'), tc_k)
                    for n in range(4):
                        vector.tensor_copy(out=ktile[0:64, n * 512:(n + 1) * 512],
                                           in_=pb[n][0:64, :])
                vg_k.append(C.cnt['vg'])

                @block.sync
                def _(sync):
                    sync.wait_ge(C.s('vg'), vg_k[h])
                    C.dma(sync, z_kT[h * HD:(h + 1) * HD, :], ktile[0:HD, :], 'ds')
                ds_k.append(C.cnt['ds'])

            @block.sync
            def _(sync):
                C.w(sync, 'vg', 'tc', 'sc', 'ds')

            dg_we = []
            tc_we = []
            for n3 in range(3):
                for kp in range(32):
                    p = kp % 2
                    lslot = wa_st[:, p * 1024:(p + 1) * 1024]
                    rslot = wa_st[:, 2048 + p * 512:2048 + (p + 1) * 512]

                    @block.gpsimd
                    def _(gpsimd):
                        if len(tc_we) >= 2:
                            gpsimd.wait_ge(C.s('tc'), tc_we[-2])
                        C.dma(gpsimd, lslot.rearrange("p (a c) -> p a c", a=ND),
                              wdqT[kp * 128:(kp + 1) * 128, :].rearrange("p (a c) -> p a c", a=ND),
                              'dg')
                        C.dma(gpsimd, rslot,
                              wcatq[kp * 128:(kp + 1) * 128, n3 * 512:(n3 + 1) * 512],
                              'dg')
                    dg_we.append(C.cnt['dg'])

                    @block.tensor
                    def _(tensor):
                        tensor.wait_ge(C.s('dg'), dg_we[-1])
                        if kp == 0:
                            tensor.wait_ge(C.s('vg'), C.cnt['vg'])
                        for m in range(8):
                            i2 = tensor.matmul(
                                out=pb[m][:, :],
                                lhsT=lslot.rearrange("p (a c) -> p a c", a=ND)[:, m, :],
                                rhs=rslot,
                                start=(kp == 0), stop=(kp == 31))
                        C.inc(i2, 'tc')
                    tc_we.append(C.cnt['tc'])

                @block.vector
                def _(vector):
                    vector = C.gv(vector)
                    vector.wait_ge(C.s('tc'), tc_we[-1])
                    for m in range(8):
                        vector.tensor_copy(
                            out=weff[:, m * NQ + n3 * 512:m * NQ + (n3 + 1) * 512],
                            in_=pb[m][:, :])

            vg_q = []
            ds_q = []
            for half in range(2):
                for n in range(4):
                    @block.tensor
                    def _(tensor):
                        tensor.wait_ge(C.s('vg'), C.cnt['vg'])
                        for kk in range(ND):
                            for h8 in range(8):
                                i2 = tensor.matmul(
                                    out=pb[h8][0:HD, :],
                                    lhsT=weff[:, kk * NQ + (half * 8 + h8) * HD:kk * NQ + (half * 8 + h8 + 1) * HD],
                                    rhs=a8.rearrange("p (j c) -> p j c", j=ND)[:, kk, n * 512:(n + 1) * 512],
                                    start=(kk == 0), stop=(kk == ND - 1))
                        C.inc(i2, 'tc')
                    tc_q = C.cnt['tc']

                    for h8 in range(8):
                        p2 = h8 % 2
                        slot = sbq[:, p2 * 1024:p2 * 1024 + 512]

                        @block.vector
                        def _(vector):
                            vector = C.gv(vector)
                            vector.wait_ge(C.s('tc'), tc_q)
                            if len(ds_q) >= 2:
                                vector.wait_ge(C.s('ds'), ds_q[-2])
                            vector.tensor_copy(out=slot[0:64, :], in_=pb[h8][0:64, :])
                            vector.tensor_tensor(out=slot[64:96, :],
                                                 in0=pb[h8][64:96, :],
                                                 in1=c_rope[64:96, n * 512:(n + 1) * 512],
                                                 op=ALU.mult)
                        vg_q.append(C.cnt['vg'])

                        @block.sync
                        def _(sync):
                            sync.wait_ge(C.s('vg'), vg_q[-1])
                            C.dma(sync, z_qT[(half * 8 + h8) * HD:(half * 8 + h8 + 1) * HD,
                                             n * 512:(n + 1) * 512],
                                  slot[0:HD, :], 'ds')
                        ds_q.append(C.cnt['ds'])

            ds_hd = []
            tc_end = []
            vg_hd = []
            for h in range(HN):
                hp = h % 2
                ybase = 0 if hp == 0 else 64
                dbase = 96 if hp == 0 else 32

                @block.sync
                def _(sync):
                    if h == 0:
                        C.w(sync, 'vg')
                    if h >= 2:
                        sync.wait_ge(C.s('tc'), tc_end[h - 2])
                    C.dma(sync, sqh[hp][0:HD, :], z_qT[h * HD:(h + 1) * HD, :], 'ds')
                    C.dma(sync, skh[hp][0:HD, :], z_kT[h * HD:(h + 1) * HD, :], 'ds')
                    C.dma(sync, svh[hp].rearrange("p (a c) -> p a c", a=NT),
                          z_v[:, h * HS:(h + 1) * HS].rearrange("(a p) c -> p a c", p=128),
                          'ds')
                ds_hd.append(C.cnt['ds'])

                steps = []
                for ki in range(NT):
                    nq = T - 128 * ki
                    for qc in range((nq + 511) // 512):
                        w_ = min(512, nq - qc * 512)
                        steps.append((ki, qc, w_))
                tc0 = C.cnt['tc']
                sc0 = C.cnt['sc']

                @block.tensor
                def _(tensor):
                    tensor.wait_ge(C.s('ds'), ds_hd[h])
                    for si, (ki, qc, w_) in enumerate(steps):
                        if si >= 2:
                            tensor.wait_ge(C.s('sc'), sc0 + si - 1)
                        else:
                            tensor.wait_ge(C.s('sc'), sc0)
                        i2 = tensor.matmul(
                            out=pb[si % 2][:, 0:w_],
                            lhsT=skh[hp][0:HD, ki * 128:(ki + 1) * 128],
                            rhs=sqh[hp][0:HD, 128 * ki + qc * 512:128 * ki + qc * 512 + w_],
                            start=True, stop=True)
                        C.inc(i2, 'tc')

                @block.scalar
                def _(scalar):
                    for si, (ki, qc, w_) in enumerate(steps):
                        scalar.wait_ge(C.s('tc'), tc0 + si + 1)
                        i2 = scalar.activation(
                            out=pTa[:, PT_OFF[ki] + qc * 512:PT_OFF[ki] + qc * 512 + w_],
                            in_=pb[si % 2][:, 0:w_],
                            func=AF.Exp, bias=0.0, scale=ISQ96)
                        C.inc(i2, 'sc')
                sc_exp = C.cnt['sc']

                si_of = {}
                for si, (ki, qc, w_) in enumerate(steps):
                    if qc == 0:
                        si_of[ki] = si

                @block.vector
                def _(vector):
                    vector = C.gv(vector)
                    for ki in range(NT):
                        vector.wait_ge(C.s('sc'), sc0 + si_of[ki] + 1)
                        vector.tensor_tensor(
                            out=pTa[:, PT_OFF[ki]:PT_OFF[ki] + 128],
                            in0=pTa[:, PT_OFF[ki]:PT_OFF[ki] + 128],
                            in1=c_lt[:, :], op=ALU.mult)
                vg_mask = C.cnt['vg']

                @block.tensor
                def _(tensor):
                    tensor.wait_ge(C.s('vg'), vg_mask)
                    for n in range(4):
                        for ki in range(4 * n + 4):
                            if 128 * ki <= 512 * n:
                                rhs = pTa[:, PT_OFF[ki] + 512 * n - 128 * ki:
                                          PT_OFF[ki] + 512 * n - 128 * ki + 512]
                                c0, c1 = 0, 512
                                st_ = (ki == 0)
                            else:
                                off = 128 * ki - 512 * n
                                rhs = pTa[:, PT_OFF[ki]:PT_OFF[ki] + 512 - off]
                                c0, c1 = off, 512
                                st_ = False
                            tensor.matmul(out=pb[2 + n][ybase:ybase + 64, c0:c1],
                                          lhsT=svh[hp][:, ki * 64:(ki + 1) * 64],
                                          rhs=rhs, start=st_, stop=(ki == 4 * n + 3))
                            i2 = tensor.matmul(out=pb[2 + n][dbase:dbase + 32, c0:c1],
                                               lhsT=c_ones32[:, :],
                                               rhs=rhs, start=st_, stop=(ki == 4 * n + 3),
                                               tile_position=(0, dbase))
                    C.inc(i2, 'tc')
                tc_pv = C.cnt['tc']

                @block.vector
                def _(vector):
                    vector = C.gv(vector)
                    vector.wait_ge(C.s('tc'), tc_pv)
                    for n in range(4):
                        vector.tensor_copy(out=sd1[32 * n:32 * n + 1, 0:512],
                                           in_=pb[2 + n][dbase:dbase + 1, :])
                    vector.reciprocal(out=sd1[:, 512:1024], in_=sd1[:, 0:512])
                vg_rd = C.cnt['vg']

                tc_k1 = []

                @block.tensor
                def _(tensor):
                    tensor.wait_ge(C.s('vg'), vg_rd)
                    for n in range(4):
                        bank = pb[6] if n < 2 else pb[7]
                        rb = (n % 2) * 64
                        i2 = tensor.matmul(out=bank[rb:rb + 64, :],
                                           lhsT=c_onesf[32 * n:32 * n + 1, :],
                                           rhs=sd1[32 * n:32 * n + 1, 512:1024],
                                           start=True, stop=True,
                                           tile_position=(32 * n, rb))
                        C.inc(i2, 'tc')
                        tc_k1.append(C.cnt['tc'])

                @block.vector
                def _(vector):
                    vector = C.gv(vector)
                    for n in range(4):
                        bank = pb[6] if n < 2 else pb[7]
                        rb = (n % 2) * 64
                        stage = sd1[ybase:ybase + 64,
                                    1024 + (n % 2) * 512:1536 + (n % 2) * 512]
                        vector.wait_ge(C.s('tc'), tc_k1[n])
                        vector.tensor_copy(out=stage, in_=bank[rb:rb + 64, :])
                        vector.tensor_tensor(
                            out=a8.rearrange("p (j c) -> p j c", j=ND)[:, h // 2, n * 512:(n + 1) * 512][ybase:ybase + 64, :],
                            in0=pb[2 + n][ybase:ybase + 64, :],
                            in1=stage,
                            op=ALU.mult)
                vg_hd.append(C.cnt['vg'])
                tc_end.append(C.cnt['tc'])

            if dbg:
                vg_now = C.cnt['vg']

                @block.sync
                def _(sync):
                    sync.wait_ge(C.s('vg'), vg_now)
                    C.dma(sync, d_a8, a8[:, :], 'ds')

            tc_wo = []
            vg_wo = []
            ds_wo = []
            ds_xo = []
            for t in range(NT):
                p = t % 2
                banks = pb[0:2] if p == 0 else pb[2:4]
                xres = stg[:, p * 1024:(p + 1) * 1024]
                outt = sd1[:, p * 1024:(p + 1) * 1024]

                @block.sync
                def _(sync):
                    if t == 0:
                        C.w(sync, 'vg', 'dg')
                    if t >= 2:
                        sync.wait_ge(C.s('vg'), vg_wo[t - 2])
                        sync.wait_ge(C.s('dg'), ds_xo[t - 2])
                    C.dma(sync, xres, z_x1[t * 128:(t + 1) * 128, :], 'ds')
                ds_wo.append(C.cnt['ds'])

                @block.tensor
                def _(tensor):
                    if t == 0:
                        tensor.wait_ge(C.s('vg'), C.cnt['vg'])
                        tensor.wait_ge(C.s('dg'), dg_w)
                    if t >= 2:
                        tensor.wait_ge(C.s('vg'), vg_wo[t - 2])
                    for kk in range(ND):
                        for n2 in range(2):
                            i2 = tensor.matmul(
                                out=banks[n2][:, :],
                                lhsT=a8.rearrange("p (j c) -> p j c", j=ND)[:, kk, t * 128:(t + 1) * 128],
                                rhs=wa_wo[:, kk * 1024 + n2 * 512:kk * 1024 + (n2 + 1) * 512],
                                start=(kk == 0), stop=(kk == ND - 1))
                    C.inc(i2, 'tc')
                tc_wo.append(C.cnt['tc'])

                @block.vector
                def _(vector):
                    vector = C.gv(vector)
                    vector.wait_ge(C.s('tc'), tc_wo[t])
                    vector.wait_ge(C.s('ds'), ds_wo[t])
                    for n2 in range(2):
                        vector.scalar_tensor_tensor(
                            out=outt[:, n2 * 512:(n2 + 1) * 512],
                            in0=banks[n2][:, :], scalar=1.0,
                            in1=xres[:, n2 * 512:(n2 + 1) * 512],
                            op0=ALU.mult, op1=ALU.add)
                vg_wo.append(C.cnt['vg'])

                @block.gpsimd
                def _(gpsimd):
                    gpsimd.wait_ge(C.s('vg'), vg_wo[t])
                    C.dma(gpsimd, z_x2[t * 128:(t + 1) * 128, :], outt, 'dg')
                ds_xo.append(C.cnt['dg'])

        emit_ffn(x_in, z_x1)
        emit_attention()
        emit_ffn(z_x2, out, tag=2)

        if dbg:
            @block.sync
            def _(sync):
                C.w(sync, 'vg', 'sc')
                C.dma(sync, d_stat, sstat[:, 0:64], 'ds')

        @block.sync
        def _(sync):
            C.w(sync, 'ds', 'dg')

    return nc


_NC_CACHE = {}


def kernel(L0_attn_x, L0_attn_W_DQ, L0_attn_W_UQ, L0_attn_W_DKV, L0_attn_W_UK,
           L0_attn_W_UV, L0_attn_W_QR, L0_attn_W_KR, L0_attn_W_O,
           L0_ffn_w_gate, L0_ffn_w_up, L0_ffn_w_down):
    f = np.float32
    x = np.ascontiguousarray(np.asarray(L0_attn_x, dtype=f))
    wdqT = np.ascontiguousarray(np.asarray(L0_attn_W_DQ, dtype=f).T)
    wuq = np.asarray(L0_attn_W_UQ, dtype=f)
    wqr = np.asarray(L0_attn_W_QR, dtype=f)
    wcatq = np.ascontiguousarray(
        np.concatenate([wuq.reshape(4096, HN, HS), wqr.reshape(4096, HN, DR)],
                       axis=2).reshape(4096, NQ))
    wdkv = np.ascontiguousarray(np.asarray(L0_attn_W_DKV, dtype=f))
    wuk = np.ascontiguousarray(np.asarray(L0_attn_W_UK, dtype=f))
    wuv = np.ascontiguousarray(np.asarray(L0_attn_W_UV, dtype=f))
    wkr = np.ascontiguousarray(np.asarray(L0_attn_W_KR, dtype=f))
    wo = np.ascontiguousarray(np.asarray(L0_attn_W_O, dtype=f))
    wg = np.ascontiguousarray(np.asarray(L0_ffn_w_gate, dtype=f))
    wu = np.ascontiguousarray(np.asarray(L0_ffn_w_up, dtype=f))
    wd = np.ascontiguousarray(np.asarray(L0_ffn_w_down, dtype=f))

    t = np.arange(T, dtype=f)
    inv = (f(1.0) / (f(10000.0) ** (np.arange(0, DR, 2, dtype=f) / f(DR)))).astype(f)
    freqs = (t[:, None] * inv[None, :]).astype(f)
    emb = np.concatenate([freqs, freqs], axis=-1)
    rope = (np.cos(emb) + np.sin(emb)).astype(f)
    ropeT = np.ascontiguousarray(rope.T)

    ltm = np.zeros((128, 128), dtype=f)
    for kk in range(128):
        ltm[kk, kk:] = 1.0
    onescol = np.zeros((128, 32), dtype=f)
    onescol[:, 0] = 1.0
    onesf = np.ones((128, 64), dtype=f)
    magicv = np.zeros((128, 2), dtype=f); magicv[:, 0] = f(12582912.0); magicv[:, 1] = f(-12582912.0)
    eye = np.eye(128, dtype=f)

    if "nc" not in _NC_CACHE:
        _NC_CACHE["nc"] = build_nc()
    nc = _NC_CACHE["nc"]

    shared = dict(wdqT=wdqT, wcatq=wcatq, wdkv=wdkv, wuk=wuk, wuv=wuv, wkr=wkr,
                  wo=wo, wg=wg, wu=wu, wd=wd, ropeT=ropeT, ltmask=ltm,
                  onescol=onescol, onesf=onesf, magicv=magicv, eye=eye)
    in_maps = [dict(shared, x=np.ascontiguousarray(x[c])) for c in range(8)]
    res = run_bass_kernel_spmd(nc, in_maps, core_ids=list(range(8)))
    return np.stack([np.asarray(res.results[c]["out"]) for c in range(8)], axis=0)


# version 13
# speedup vs baseline: 1.2306x; 1.0029x over previous
import numpy as np
import ml_dtypes
from contextlib import ExitStack

import concourse.bass as bass
import concourse.mybir as mybir
from concourse.bass_utils import run_bass_kernel_spmd

F32 = mybir.dt.float32
BF16 = mybir.dt.bfloat16
AF = mybir.ActivationFunctionType
ALU = mybir.AluOpType
X = mybir.AxisListType.X

T = 2048
D = 1024
HN = 16
HS = 64
DR = 32
HD = HS + DR
DFF = 3072
NT = T // 128
ND = D // 128
NI = DFF // 128
NQ = HN * HD

MAGIC = float(np.float32(12582912.0))
INV7 = float(np.float32(1.0) / np.float32(7.0))
INV32767 = float(np.float32(1.0) / np.float32(32767.0))
INV1024 = float(np.float32(1.0) / np.float32(1024.0))
ISQ96 = float(np.float32(1.0) / np.float32(np.sqrt(np.float32(96.0))))
EPS_LN = 1e-5
EPS_Q = 1e-12

PT_OFF = [0] * (NT + 1)
for _ki in range(NT):
    PT_OFF[_ki + 1] = PT_OFF[_ki] + (T - 128 * _ki)
PT_TOT = PT_OFF[NT]


class K:
    def __init__(self, block, sems):
        self.block = block
        self.sems = sems
        self.cnt = {k: 0 for k in sems}

    def s(self, name):
        return self.sems[name]

    def inc(self, instr, name, n=1):
        self.cnt[name] += n
        return instr.then_inc(self.sems[name], n)

    def dma(self, eng, out, in_, name, **kw):
        i = eng.dma_start(out=out, in_=in_, **kw)
        self.cnt[name] += 16
        i.then_inc(self.sems[name], 16)
        return i

    def w(self, eng, *names):
        for name in names:
            eng.wait_ge(self.sems[name], self.cnt[name])

    def gv(self, vector):
        C = self

        class GV:
            def wait_ge(self, *a, **k):
                return vector.wait_ge(*a, **k)

            def __getattr__(self, name):
                fn = getattr(vector, name)

                def wrapped(*a, **k):
                    vector.wait_ge(C.s('vg'), C.cnt['vg'])
                    i = fn(*a, **k)
                    C.inc(i, 'vg')
                    return i
                return wrapped
        return GV()

    def gs(self, scalar):
        C = self

        class GS:
            def wait_ge(self, *a, **k):
                return scalar.wait_ge(*a, **k)

            def __getattr__(self, name):
                fn = getattr(scalar, name)

                def wrapped(*a, **k):
                    scalar.wait_ge(C.s('sc'), C.cnt['sc'])
                    i = fn(*a, **k)
                    C.inc(i, 'sc')
                    return i
                return wrapped
        return GS()


def build_nc(dbg=False):
    nc = bass.Bass("TRN2", target_bir_lowering=False, debug=False,
                   detect_race_conditions=False)

    def _dz(name, shape, dt):
        if dbg:
            return nc.dram_tensor(name, shape, dt, kind="ExternalOutput").ap()
        return nc.dram_tensor(name, shape, dt).ap()

    x_in = nc.dram_tensor("x", [T, D], F32, kind="ExternalInput").ap()
    wdqT = nc.dram_tensor("wdqT", [4096, D], F32, kind="ExternalInput").ap()
    wcatq = nc.dram_tensor("wcatq", [4096, NQ], F32, kind="ExternalInput").ap()
    wdkv = nc.dram_tensor("wdkv", [D, 256], F32, kind="ExternalInput").ap()
    wuk = nc.dram_tensor("wuk", [256, D], F32, kind="ExternalInput").ap()
    wuv = nc.dram_tensor("wuv", [256, D], F32, kind="ExternalInput").ap()
    wkr = nc.dram_tensor("wkr", [D, DR], F32, kind="ExternalInput").ap()
    wo = nc.dram_tensor("wo", [D, D], F32, kind="ExternalInput").ap()
    wg = nc.dram_tensor("wg", [DFF, D], F32, kind="ExternalInput").ap()
    wu = nc.dram_tensor("wu", [DFF, D], F32, kind="ExternalInput").ap()
    wd = nc.dram_tensor("wd", [D, DFF], F32, kind="ExternalInput").ap()
    ropeT_in = nc.dram_tensor("ropeT", [DR, T], F32, kind="ExternalInput").ap()
    ltmask = nc.dram_tensor("ltmask", [128, 128], F32, kind="ExternalInput").ap()
    onescol = nc.dram_tensor("onescol", [128, 32], F32, kind="ExternalInput").ap()
    onesf = nc.dram_tensor("onesf", [128, 64], F32, kind="ExternalInput").ap()
    magicv = nc.dram_tensor("magicv", [128, 2], F32, kind="ExternalInput").ap()
    eye_in = nc.dram_tensor("eye", [128, 128], F32, kind="ExternalInput").ap()
    out = nc.dram_tensor("out", [T, D], F32, kind="ExternalOutput").ap()

    z_x1 = _dz("z_x1", [T, D], F32)
    z_x2 = _dz("z_x2", [T, D], F32)
    z_wgT = _dz("z_wgT", [NI, D, 128], BF16)
    z_wuT = _dz("z_wuT", [NI, D, 128], BF16)
    z_wdT = _dz("z_wdT", [NI, 128, D], BF16)
    z_upraw = _dz("z_upraw", [DFF, T], F32)
    z_graw = _dz("z_graw", [DFF, T], F32)
    z_qu = _dz("z_qu", [DFF, T], mybir.dt.int16)
    z_g2 = _dz("z_g2", [DFF, T], F32)
    z_u = _dz("z_u", [DFF, T], F32)
    z_qT = _dz("z_qT", [NQ, T], BF16)
    z_kT = _dz("z_kT", [NQ, T], BF16)
    z_v = _dz("z_v", [T, D], BF16)
    d_a8 = _dz("d_a8", [128, 8 * T], BF16) if dbg else None
    d_a8f = _dz("d_a8f", [128, 8 * T], BF16) if dbg else None
    d_stat = _dz("d_stat", [128, 64], F32) if dbg else None
    cc_in = nc.dram_tensor("cc_in", [1, 2], F32).ap()
    cc_out = nc.dram_tensor("cc_out", [1, 2], F32).ap()
    z_sc = nc.dram_tensor("z_sc", [1, 16], F32).ap()

    with ExitStack() as ctx:
        def sb(name, shape, dt):
            return ctx.enter_context(nc.sbuf_tensor(name, shape, dt))

        c_rope = sb("c_rope", [128, T], F32)
        c_lt = sb("c_lt", [128, 128], BF16)
        c_eye = sb("c_eye", [128, 128], BF16)
        c_ones32 = sb("c_ones32", [128, 32], BF16)
        c_onesf = sb("c_onesf", [128, 64], F32)
        c_magic = sb("c_magic", [128, 2], F32)
        sval = sb("sval", [1, 16], F32)
        sscale = sb("sscale", [128, 16], F32)
        streeg = sb("streeg", [128, 4], F32)
        s32a = sb("s32a", [128, 32], F32)
        s32b = sb("s32b", [128, 32], F32)
        sacc = sb("sacc", [128, 4], F32)
        sstat = sb("sstat", [128, 4 * NT], F32)
        spad = sb("spad", [128, 16], F32)
        af = sb("af", [128, 16384], F32)
        a8 = sb("a8", [128, 8 * T], BF16)
        wa = sb("wa", [128, 24576], BF16)
        stg = sb("stg", [128, 4096], F32)
        sbq = sb("sbq", [128, 2048], BF16)
        sqh = [sb(f"sqh{i}", [HD, T], BF16) for i in range(2)]
        skh = [sb(f"skh{i}", [HD, T], BF16) for i in range(2)]
        svh = [sb(f"svh{i}", [128, HN * HS], BF16) for i in range(2)]
        skr = sb("skr", [DR, T], BF16)
        sd1 = sb("sd1", [128, 2048], F32)

        pb = [ctx.enter_context(nc.psum_tensor(f"pb{i}", [128, 512], F32))
              for i in range(8)]
        pt6 = pb[6][:, :].bitcast(BF16)
        pt7 = pb[7][:, :].bitcast(BF16)
        pt4 = pb[4][:, :].bitcast(BF16)
        pt5 = pb[5][:, :].bitcast(BF16)
        pTa = af[:, 0:8704].bitcast(BF16)
        weff = af[:, 0:6144].bitcast(BF16)

        sems = {}
        for name in ["ds", "dg", "tc", "sc", "cc", "vg"]:
            sems[name] = ctx.enter_context(nc.semaphore(name))
        lo = min(s.num for s in sems.values())
        hi = max(s.num for s in sems.values()) + 1
        nc.gpsimd.sem_clear(range(lo, hi))
        nc.all_engine_barrier()
        block = ctx.enter_context(nc.Block())
        C = K(block, sems)

        @block.sync
        def _(sync):
            C.dma(sync, c_rope[0:32, :], ropeT_in, 'ds')
            C.dma(sync, c_rope[64:96, :], ropeT_in, 'ds')
            C.dma(sync, c_onesf[:, :], onesf, 'ds')
            C.dma(sync, c_magic[:, :], magicv, 'ds')

        @block.gpsimd
        def _(gpsimd):
            C.dma(gpsimd, c_lt[:, :], ltmask, 'dg')
            C.dma(gpsimd, c_eye[:, :], eye_in, 'dg')
            C.dma(gpsimd, c_ones32[:, :], onescol, 'dg')
        dg_consts = C.cnt['dg']

        def tree_max(vector, src_col, dst_col):
            vector.tensor_copy(out=streeg[:64, 0:1], in_=sacc[64:128, src_col:src_col + 1])
            vector.tensor_tensor(out=streeg[:64, 1:2], in0=sacc[:64, src_col:src_col + 1],
                                 in1=streeg[:64, 0:1], op=ALU.max)
            vector.tensor_copy(out=streeg[:32, 2:3], in_=streeg[32:64, 1:2])
            vector.tensor_tensor(out=s32a[:32, 0:1], in0=streeg[:32, 1:2],
                                 in1=streeg[:32, 2:3], op=ALU.max)
            vector.transpose(out=s32b[:32, :32], in_=s32a[:32, :32])
            return vector.reduce_max(out=sval[0:1, dst_col:dst_col + 1],
                                     in_=s32b[0:1, :32], axis=X)

        def emit_allreduce():
            vg_ready = C.cnt['vg']

            @block.gpsimd
            def _(gpsimd):
                gpsimd.wait_ge(C.s('vg'), vg_ready)
                C.dma(gpsimd, cc_in, sval[0:1, 12:14], 'dg')
                C.w(gpsimd, 'dg')
                i = gpsimd.collective_compute(
                    "AllReduce", ALU.max,
                    replica_groups=[list(range(8))],
                    ins=[cc_in.opt()],
                    outs=[cc_out.opt()],
                )
                C.inc(i, 'cc')
                gpsimd.wait_ge(C.s('cc'), C.cnt['cc'])
                C.dma(gpsimd, sval[0:1, 10:12], cc_out, 'dg')

        def emit_scales(build, ncols):
            dg_ready = C.cnt['dg']

            @block.vector
            def _(vector):
                vector = C.gv(vector)
                vector.wait_ge(C.s('dg'), dg_ready)
                build(vector)
            vg_here = C.cnt['vg']

            @block.sync
            def _(sync):
                sync.wait_ge(C.s('vg'), vg_here)
                C.dma(sync, z_sc[0:1, 0:ncols], sval[0:1, 0:ncols], 'ds')
                C.w(sync, 'ds')
                C.dma(sync, sscale[:, 0:ncols],
                      z_sc[0:1, 0:ncols].broadcast_to([128, ncols]), 'ds')
                C.dma(sync, spad[:], ltmask[:, 0:16], 'ds')

        def dequant_core(vector, buf, tq, bm, scl, rcp, outv, nblk):
            vector.reduce_max(out=bm.rearrange("p (a b) -> p a b", b=1),
                              in_=buf.rearrange("p (a b) -> p a b", b=32),
                              axis=X, apply_absolute_value=True)
            vector.tensor_scalar(out=scl, in0=bm, scalar1=INV7, scalar2=EPS_Q,
                                 op0=ALU.mult, op1=ALU.max)
            vector.reciprocal(out=rcp, in_=scl)
            vector.tensor_tensor(out=tq.rearrange("p (a b) -> p a b", b=32),
                                 in0=buf.rearrange("p (a b) -> p a b", b=32),
                                 in1=rcp[:, :, None].broadcast_to([128, nblk, 32]),
                                 op=ALU.mult)
            vector.tensor_scalar(out=tq, in0=tq, scalar1=MAGIC, scalar2=MAGIC,
                                 op0=ALU.add, op1=ALU.subtract)
            return vector.tensor_tensor(out=outv.rearrange("p (a b) -> p a b", b=32),
                                        in0=tq.rearrange("p (a b) -> p a b", b=32),
                                        in1=scl[:, :, None].broadcast_to([128, nblk, 32]),
                                        op=ALU.mult)

        def emit_dequant_gateup():
            vg_after = []
            tc_after = []
            sc_after = []
            ds_ld = []
            ds_st = []
            ptvs = [[pb[2 * k][:, :].bitcast(BF16), pb[2 * k + 1][:, :].bitcast(BF16)]
                    for k in range(4)]
            vg0 = C.cnt['vg']
            tc0 = C.cnt['tc']
            dg0 = C.cnt['dg']
            for i in range(NI):
                p = i % 4
                buf = af[:, p * 2048:(p + 1) * 2048]
                tq = af[:, 8192 + p * 2048:8192 + (p + 1) * 2048]
                bm = sd1[:, p * 256:p * 256 + 64]
                scl = sd1[:, p * 256 + 64:p * 256 + 128]
                rcp = sd1[:, p * 256 + 128:p * 256 + 192]
                obf = a8[:, p * 2048:(p + 1) * 2048]
                pta, ptb = ptvs[p]

                @block.sync
                def _(sync):
                    if i >= 4:
                        sync.wait_ge(C.s('vg'), vg_after[i - 4])
                    else:
                        sync.wait_ge(C.s('vg'), vg0)
                    C.dma(sync, buf[:, 0:1024], wu[i * 128:(i + 1) * 128, :], 'ds')
                    C.dma(sync, buf[:, 1024:2048], wg[i * 128:(i + 1) * 128, :], 'ds')
                ds_ld.append(C.cnt['ds'])
                ds_lm = ds_ld[i] + 16

                @block.vector
                def _(vector):
                    vector = C.gv(vector)
                    vector.wait_ge(C.s('ds'), ds_lm)
                    if i >= 4:
                        vector.wait_ge(C.s('tc'), tc_after[i - 4])
                        vector.wait_ge(C.s('dg'), ds_st[i - 4])
                    else:
                        vector.wait_ge(C.s('tc'), tc0)
                        vector.wait_ge(C.s('dg'), dg0)
                    dequant_core(vector, buf, tq, bm, scl, rcp, obf, 64)
                vg_after.append(C.cnt['vg'])

                @block.tensor
                def _(tensor):
                    tensor.wait_ge(C.s('vg'), vg_after[i])
                    if i == 0:
                        tensor.wait_ge(C.s('dg'), dg_consts)
                    if i >= 4:
                        tensor.wait_ge(C.s('sc'), sc_after[i - 4])
                    for j in range(2 * ND):
                        ptv = pta if j < ND else ptb
                        i2 = tensor.transpose(out=ptv[:, (j % ND) * 128:(j % ND + 1) * 128],
                                              in_=obf[:, j * 128:(j + 1) * 128],
                                              identity=c_eye[:, :])
                    C.inc(i2, 'tc')
                tc_after.append(C.cnt['tc'])

                @block.scalar
                def _(scalar):
                    scalar = C.gs(scalar)
                    scalar.wait_ge(C.s('tc'), tc_after[i])
                    scalar.copy(out=obf[:, 0:1024], in_=pta[:, :])
                    scalar.copy(out=obf[:, 1024:2048], in_=ptb[:, :])
                sc_after.append(C.cnt['sc'])

                @block.gpsimd
                def _(gpsimd):
                    gpsimd.wait_ge(C.s('sc'), sc_after[i])
                    C.dma(gpsimd, z_wuT[i].rearrange("(a p) c -> p a c", p=128),
                          obf[:, 0:1024].rearrange("p (a c) -> p a c", a=ND), 'dg')
                    C.dma(gpsimd, z_wgT[i].rearrange("(a p) c -> p a c", p=128),
                          obf[:, 1024:2048].rearrange("p (a c) -> p a c", a=ND), 'dg')
                ds_st.append(C.cnt['dg'])

            @block.sync
            def _(sync):
                C.dma(sync, spad[:], ltmask[:, 0:16], 'ds')

        def emit_dequant_down(src, dstT):
            for r in range(ND):
                buf = af[:, 0:3072]
                tq = af[:, 4096:7168]
                bm = af[:, 8192:8288]
                scl = af[:, 8448:8544]
                rcp = af[:, 8704:8800]
                obf = af[:, 10240:11776].bitcast(BF16)

                @block.sync
                def _(sync):
                    C.w(sync, 'vg', 'tc')
                    C.dma(sync, buf, src[r * 128:(r + 1) * 128, :], 'ds')
                    C.dma(sync, spad[:], ltmask[:, 0:16], 'ds')
                ds_h = C.cnt['ds']

                @block.vector
                def _(vector):
                    vector = C.gv(vector)
                    vector.wait_ge(C.s('ds'), ds_h)
                    dequant_core(vector, buf, tq, bm, scl, rcp, obf, 96)
                vg_h = C.cnt['vg']

                for rnd in range(3):
                    ptv = pt6 if rnd % 2 == 0 else pt7
                    slot = sbq[:, (rnd % 2) * 1024:(rnd % 2 + 1) * 1024]

                    @block.tensor
                    def _(tensor):
                        if rnd == 0:
                            tensor.wait_ge(C.s('vg'), vg_h)
                            tensor.wait_ge(C.s('dg'), dg_consts)
                        tensor.wait_ge(C.s('dg'), C.cnt['dg'])
                        tensor.wait_ge(C.s('sc'), C.cnt['sc'])
                        for jj in range(8):
                            i2 = tensor.transpose(
                                out=ptv[:, jj * 128:(jj + 1) * 128],
                                in_=obf[:, (rnd * 8 + jj) * 128:(rnd * 8 + jj + 1) * 128],
                                identity=c_eye[:, :])
                        C.inc(i2, 'tc')
                    tc_h = C.cnt['tc']

                    @block.scalar
                    def _(scalar):
                        scalar = C.gs(scalar)
                        scalar.wait_ge(C.s('tc'), tc_h)
                        scalar.copy(out=slot, in_=ptv[:, :])
                    sc_h = C.cnt['sc']

                    @block.gpsimd
                    def _(gpsimd):
                        gpsimd.wait_ge(C.s('sc'), sc_h)
                        C.dma(gpsimd,
                              dstT[rnd * 8:(rnd + 1) * 8, :, r * 128:(r + 1) * 128]
                              .rearrange("a p c -> p a c"),
                              slot.rearrange("p (a c) -> p a c", a=8), 'dg')

        emit_dequant_gateup()
        emit_dequant_down(wd, z_wdT)

        def emit_ln(src):
            ds_after = []
            for t in range(NT):
                xt = af[:, t * 1024:(t + 1) * 1024]

                @block.sync
                def _(sync):
                    if t == 0:
                        C.w(sync, 'vg', 'tc', 'sc', 'ds', 'dg')
                    C.dma(sync, xt, src[t * 128:(t + 1) * 128, :], 'ds')
                ds_after.append(C.cnt['ds'])

            @block.sync
            def _(sync):
                C.dma(sync, spad[:], ltmask[:, 0:16], 'ds')

            for t in range(NT):
                xt = af[:, t * 1024:(t + 1) * 1024]

                @block.vector
                def _(vector):
                    vector = C.gv(vector)
                    vector.wait_ge(C.s('ds'), ds_after[t] + 16)
                    vector.reduce_sum(out=sstat[:, t:t + 1], in_=xt, axis=X)
                    vector.tensor_single_scalar(out=sstat[:, NT + t:NT + t + 1],
                                                in_=sstat[:, t:t + 1],
                                                scalar=-INV1024, op=ALU.mult)
                vg_t = C.cnt['vg']

                @block.scalar
                def _(scalar):
                    scalar = C.gs(scalar)
                    scalar.wait_ge(C.s('vg'), vg_t)
                    scalar.activation(out=stg[:, (t % 2) * 1024:(t % 2 + 1) * 1024],
                                      in_=xt, func=AF.Square,
                                      bias=sstat[:, NT + t:NT + t + 1], scale=1.0,
                                      accum_out=sstat[:, 2 * NT + t:2 * NT + t + 1])
            sc_sq = C.cnt['sc']

            @block.vector
            def _(vector):
                vector = C.gv(vector)
                vector.wait_ge(C.s('sc'), sc_sq)
                vector.tensor_scalar(out=sstat[:, 2 * NT:3 * NT],
                                     in0=sstat[:, 2 * NT:3 * NT],
                                     scalar1=INV1024, scalar2=EPS_LN,
                                     op0=ALU.mult, op1=ALU.add)
            vg_var = C.cnt['vg']

            @block.scalar
            def _(scalar):
                scalar = C.gs(scalar)
                scalar.wait_ge(C.s('vg'), vg_var)
                scalar.sqrt(out=sstat[:, 2 * NT:3 * NT], in_=sstat[:, 2 * NT:3 * NT])
            sc_sqrt = C.cnt['sc']

            @block.vector
            def _(vector):
                vector = C.gv(vector)
                vector.wait_ge(C.s('sc'), sc_sqrt)
                vector.reciprocal(out=sstat[:, 2 * NT:3 * NT], in_=sstat[:, 2 * NT:3 * NT])
                vector.tensor_tensor(out=sstat[:, 3 * NT:4 * NT],
                                     in0=sstat[:, NT:2 * NT],
                                     in1=sstat[:, 2 * NT:3 * NT], op=ALU.mult)
            vg_rstd = C.cnt['vg']

            for t in range(NT):
                xt = af[:, t * 1024:(t + 1) * 1024]

                @block.scalar
                def _(scalar):
                    scalar = C.gs(scalar)
                    if t == 0:
                        scalar.wait_ge(C.s('vg'), vg_rstd)
                    scalar.activation(out=xt, in_=xt, func=AF.Identity,
                                      bias=sstat[:, 3 * NT + t:3 * NT + t + 1],
                                      scale=sstat[:, 2 * NT + t:2 * NT + t + 1])
                sc_t = C.cnt['sc']

                @block.vector
                def _(vector):
                    vector = C.gv(vector)
                    vector.wait_ge(C.s('sc'), sc_t)
                    vector.reduce_max(out=sacc[:, 1:2], in_=xt, axis=X,
                                      apply_absolute_value=True)
                    if t == 0:
                        vector.tensor_copy(out=sacc[:, 0:1], in_=sacc[:, 1:2])
                    else:
                        vector.tensor_tensor(out=sacc[:, 0:1], in0=sacc[:, 0:1],
                                             in1=sacc[:, 1:2], op=ALU.max)

        def emit_tiles_to_a8(quant):
            vg_evac = []
            tc_tr = []
            for t in range(NT):
                p = t % 2
                xt = af[:, t * 1024:(t + 1) * 1024]
                slot = sbq[:, p * 1024:(p + 1) * 1024]
                ptv = pt6 if p == 0 else pt7

                @block.vector
                def _(vector):
                    vector = C.gv(vector)
                    if t == 0:
                        C.w(vector, 'ds')
                    if t >= 2:
                        vector.wait_ge(C.s('tc'), tc_tr[t - 2])
                    if quant:
                        vector.tensor_scalar(out=xt, in0=xt, scalar1=sscale[:, 0:1],
                                             scalar2=MAGIC, op0=ALU.mult, op1=ALU.add)
                        vector.tensor_scalar(out=slot, in0=xt, scalar1=MAGIC,
                                             scalar2=sscale[:, 1:2],
                                             op0=ALU.subtract, op1=ALU.mult)
                    else:
                        vector.tensor_copy(out=slot, in_=xt)
                vg_q = C.cnt['vg']

                @block.tensor
                def _(tensor):
                    tensor.wait_ge(C.s('vg'), vg_q)
                    if t == 0:
                        tensor.wait_ge(C.s('dg'), dg_consts)
                    if t >= 2:
                        tensor.wait_ge(C.s('vg'), vg_evac[t - 2])
                    for j in range(ND):
                        i2 = tensor.transpose(out=ptv[:, j * 128:(j + 1) * 128],
                                              in_=slot[:, j * 128:(j + 1) * 128],
                                              identity=c_eye[:, :])
                    C.inc(i2, 'tc')
                tc_tr.append(C.cnt['tc'])

                @block.vector
                def _(vector):
                    vector = C.gv(vector)
                    vector.wait_ge(C.s('tc'), tc_tr[t])
                    vector.tensor_copy(
                        out=a8.rearrange("p (j c) -> p j c", j=ND)[:, :, t * 128:(t + 1) * 128],
                        in_=ptv.rearrange("p (j c) -> p j c", j=ND))
                vg_evac.append(C.cnt['vg'])

        def emit_ffn(src, dst, tag=1):
            emit_ln(src)

            @block.vector
            def _(vector):
                vector = C.gv(vector)
                tree_max(vector, 0, 12)
                vector.tensor_copy(out=sval[0:1, 13:14], in_=sval[0:1, 12:13])

            emit_allreduce()

            def scales_x(vector):
                vector.tensor_scalar(out=sval[0:1, 1:2], in0=sval[0:1, 10:11],
                                     scalar1=INV32767, scalar2=EPS_Q,
                                     op0=ALU.mult, op1=ALU.max)
                vector.reciprocal(out=sval[0:1, 0:1], in_=sval[0:1, 1:2])
            emit_scales(scales_x, 2)

            emit_tiles_to_a8(quant=True)

            if dbg and tag == 2:
                vg_now2 = C.cnt['vg']

                @block.sync
                def _(sync):
                    sync.wait_ge(C.s('vg'), vg_now2)
                    C.dma(sync, d_a8f, a8[:, :], 'ds')

            def gateup(wT, dst_raw, acc_col):
                vg_evac = []
                tc_mm = []
                ds_ld = []
                ds_st = []
                tc0 = C.cnt['tc']
                dg0 = C.cnt['dg']
                for i in range(NI):
                    p = i % 2
                    wtile = wa[:, p * 1024:(p + 1) * 1024]
                    banks = pb[0:4] if p == 0 else pb[4:8]
                    stgh = stg[:, 0:2048] if p == 0 else stg[:, 2048:4096]

                    @block.sync
                    def _(sync):
                        if i < 2:
                            sync.wait_ge(C.s('dg'), dg0)
                            sync.wait_ge(C.s('tc'), tc0)
                        if i >= 2:
                            sync.wait_ge(C.s('tc'), tc_mm[i - 2])
                        C.dma(sync, wtile.rearrange("p (a c) -> p a c", a=ND),
                              wT[i].rearrange("(a p) c -> p a c", p=128), 'ds')
                    ds_ld.append(C.cnt['ds'])

                    @block.tensor
                    def _(tensor):
                        tensor.wait_ge(C.s('ds'), ds_ld[i] + 16)
                        if i == 0:
                            tensor.wait_ge(C.s('vg'), C.cnt['vg'])
                        if i >= 2:
                            tensor.wait_ge(C.s('vg'), vg_evac[i - 2])
                        for kk in range(ND):
                            for n in range(4):
                                i2 = tensor.matmul(
                                    out=banks[n][:, :],
                                    lhsT=wtile.rearrange("p (a c) -> p a c", a=ND)[:, kk, :],
                                    rhs=a8.rearrange("p (j c) -> p j c", j=ND)[:, kk, n * 512:(n + 1) * 512],
                                    start=(kk == 0), stop=(kk == ND - 1))
                        C.inc(i2, 'tc')
                    tc_mm.append(C.cnt['tc'])

                    @block.vector
                    def _(vector):
                        vector = C.gv(vector)
                        vector.wait_ge(C.s('tc'), tc_mm[i])
                        if i >= 2:
                            vector.wait_ge(C.s('dg'), ds_st[i - 2])
                        else:
                            C.w(vector, 'dg')
                        for n in range(4):
                            vector.tensor_copy(out=stgh[:, n * 512:(n + 1) * 512],
                                               in_=banks[n][:, :])
                        vector.reduce_max(out=sacc[:, 1:2], in_=stgh,
                                          axis=X, apply_absolute_value=True)
                        if i == 0:
                            vector.tensor_copy(out=sacc[:, acc_col:acc_col + 1],
                                               in_=sacc[:, 1:2])
                        else:
                            vector.tensor_tensor(out=sacc[:, acc_col:acc_col + 1],
                                                 in0=sacc[:, acc_col:acc_col + 1],
                                                 in1=sacc[:, 1:2], op=ALU.max)
                    vg_evac.append(C.cnt['vg'])

                    @block.gpsimd
                    def _(gpsimd):
                        gpsimd.wait_ge(C.s('vg'), vg_evac[i])
                        C.dma(gpsimd, dst_raw[i * 128:(i + 1) * 128, :], stgh, 'dg')
                    ds_st.append(C.cnt['dg'])

                @block.sync
                def _(sync):
                    C.dma(sync, spad[:], ltmask[:, 0:16], 'ds')

            gateup(z_wuT, z_upraw, 2)
            gateup(z_wgT, z_graw, 3)

            @block.vector
            def _(vector):
                vector = C.gv(vector)
                tree_max(vector, 2, 12)
                tree_max(vector, 3, 13)

            emit_allreduce()

            def scales_ug(vector):
                vector.tensor_scalar(out=sval[0:1, 1:2], in0=sval[0:1, 10:11],
                                     scalar1=INV32767, scalar2=EPS_Q,
                                     op0=ALU.mult, op1=ALU.max)
                vector.tensor_scalar(out=sval[0:1, 3:4], in0=sval[0:1, 11:12],
                                     scalar1=INV32767, scalar2=EPS_Q,
                                     op0=ALU.mult, op1=ALU.max)
                vector.reciprocal(out=sval[0:1, 0:1], in_=sval[0:1, 1:2])
                vector.reciprocal(out=sval[0:1, 2:3], in_=sval[0:1, 3:4])
                vector.tensor_single_scalar(out=sval[0:1, 4:5], in_=sval[0:1, 3:4],
                                            scalar=float(2.0 ** -16), op=ALU.mult)
            emit_scales(scales_ug, 5)

            def a8f(k):
                return a8[:, k * 4096:(k + 1) * 4096].bitcast(F32)

            def waf(k):
                return wa[:, k * 4096:(k + 1) * 4096].bitcast(F32)
            tc_preb = C.cnt['tc']

            @block.gpsimd
            def _(gpsimd):
                gpsimd.wait_ge(C.s('tc'), tc_preb)
                for k in range(16, NI):
                    C.dma(gpsimd, wa.rearrange("p (a c) -> p a c", a=NI)[:, k, :],
                          z_wdT[k], 'dg')

            vg_b = []
            sc_b = []
            ds_b = []
            dg_b = []
            for i in range(NI):
                s4 = i % 4
                BU = af[:, s4 * 2048:(s4 + 1) * 2048]
                BG = af[:, (4 + s4) * 2048:(5 + s4) * 2048]
                TG = a8f(s4)
                SSG = waf(s4)
                T1 = stg[:, 0:2048]

                @block.sync
                def _(sync):
                    if i == 0:
                        C.w(sync, 'vg', 'tc', 'sc', 'dg')
                    if i >= 4:
                        sync.wait_ge(C.s('dg'), dg_b[i - 4])
                    C.dma(sync, BU, z_upraw[i * 128:(i + 1) * 128, :], 'ds')
                    C.dma(sync, BG, z_graw[i * 128:(i + 1) * 128, :], 'ds')
                ds_b.append(C.cnt['ds'])
                ds_bm = ds_b[i] + (32 if i < NI - 1 else 16)

                @block.scalar
                def _(scalar):
                    scalar = C.gs(scalar)
                    scalar.wait_ge(C.s('ds'), ds_bm)
                    if i >= 4:
                        scalar.wait_ge(C.s('vg'), vg_b[i - 4])
                    scalar.activation(out=BU, in_=BU, func=AF.Identity,
                                      bias=0.0, scale=sscale[:, 0:1])
                    scalar.activation(out=T1, in_=BG, func=AF.Identity,
                                      bias=c_magic[:, 0:1], scale=sscale[:, 2:3])
                    scalar.activation(out=TG, in_=T1, func=AF.Identity,
                                      bias=c_magic[:, 1:2], scale=1.0)
                    scalar.activation(out=SSG, in_=TG, func=AF.Sigmoid,
                                      bias=0.0, scale=sscale[:, 3:4])
                sc_b.append(C.cnt['sc'])

                @block.vector
                def _(vector):
                    vector = C.gv(vector)
                    vector.wait_ge(C.s('sc'), sc_b[i])
                    vector.tensor_scalar(out=SSG, in0=SSG, scalar1=65536.0,
                                         scalar2=MAGIC, op0=ALU.mult, op1=ALU.add)
                    vector.tensor_scalar(out=SSG, in0=SSG, scalar1=MAGIC,
                                         scalar2=65535.0, op0=ALU.subtract, op1=ALU.min)
                    vector.scalar_tensor_tensor(out=BG, in0=TG, scalar=sscale[:, 4:5],
                                                in1=SSG, op0=ALU.mult, op1=ALU.mult)
                    vector.reduce_max(out=sacc[:, 1:2], in_=BG, axis=X,
                                      apply_absolute_value=True)
                    if i == 0:
                        vector.tensor_copy(out=sacc[:, 2:3], in_=sacc[:, 1:2])
                    else:
                        vector.tensor_tensor(out=sacc[:, 2:3], in0=sacc[:, 2:3],
                                             in1=sacc[:, 1:2], op=ALU.max)
                vg_b.append(C.cnt['vg'])

                @block.gpsimd
                def _(gpsimd):
                    gpsimd.wait_ge(C.s('sc'), sc_b[i])
                    gpsimd.wait_ge(C.s('vg'), vg_b[i])
                    C.dma(gpsimd, z_qu[i * 128:(i + 1) * 128, :], BU, 'dg')
                    C.dma(gpsimd, z_g2[i * 128:(i + 1) * 128, :], BG, 'dg')
                dg_b.append(C.cnt['dg'])

            @block.sync
            def _(sync):
                C.dma(sync, spad[:], ltmask[:, 0:16], 'ds')

            @block.vector
            def _(vector):
                vector = C.gv(vector)
                tree_max(vector, 2, 12)
                vector.tensor_copy(out=sval[0:1, 13:14], in_=sval[0:1, 12:13])

            emit_allreduce()

            def scales_g2(vector):
                vector.tensor_scalar(out=sval[0:1, 6:7], in0=sval[0:1, 10:11],
                                     scalar1=INV32767, scalar2=EPS_Q,
                                     op0=ALU.mult, op1=ALU.max)
                vector.reciprocal(out=sval[0:1, 5:6], in_=sval[0:1, 6:7])
            emit_scales(scales_g2, 7)

            vg_postb = C.cnt['vg']

            @block.gpsimd
            def _(gpsimd):
                gpsimd.wait_ge(C.s('vg'), vg_postb)
                for k in range(8, 16):
                    C.dma(gpsimd, wa.rearrange("p (a c) -> p a c", a=NI)[:, k, :],
                          z_wdT[k], 'dg')

            vg_c = []
            sc_c = []
            ds_c = []
            dg_c = []
            for i in range(NI):
                s4 = i % 4
                BG = af[:, s4 * 2048:(s4 + 1) * 2048]
                BU = af[:, (4 + s4) * 2048:(5 + s4) * 2048]
                UO = a8f(s4)
                QUI = wa[:, s4 * 2048:(s4 + 1) * 2048].bitcast(mybir.dt.int16)
                T1 = stg[:, 0:2048]

                @block.sync
                def _(sync):
                    if i == 0:
                        C.w(sync, 'vg', 'sc', 'dg')
                    if i >= 4:
                        sync.wait_ge(C.s('vg'), vg_c[i - 4])
                    C.dma(sync, BG, z_g2[i * 128:(i + 1) * 128, :], 'ds')
                    C.dma(sync, QUI, z_qu[i * 128:(i + 1) * 128, :], 'ds')
                ds_c.append(C.cnt['ds'])
                ds_cm = ds_c[i] + (32 if i < NI - 1 else 16)

                @block.scalar
                def _(scalar):
                    scalar = C.gs(scalar)
                    scalar.wait_ge(C.s('ds'), ds_cm)
                    scalar.activation(out=T1, in_=BG, func=AF.Identity,
                                      bias=c_magic[:, 0:1], scale=sscale[:, 5:6])
                    scalar.activation(out=BG, in_=T1, func=AF.Identity,
                                      bias=c_magic[:, 1:2], scale=1.0)
                    scalar.mul(out=BG, in_=BG, mul=sscale[:, 6:7])
                sc_c.append(C.cnt['sc'])

                @block.vector
                def _(vector):
                    vector = C.gv(vector)
                    vector.wait_ge(C.s('ds'), ds_cm)
                    if i >= 4:
                        vector.wait_ge(C.s('dg'), dg_c[i - 4])
                    vector.tensor_copy(out=BU, in_=QUI)
                    vector.tensor_single_scalar(out=BU, in_=BU,
                                                scalar=sscale[:, 1:2], op=ALU.mult)
                    vector.wait_ge(C.s('sc'), sc_c[i])
                    vector.tensor_tensor(out=UO, in0=BG, in1=BU, op=ALU.mult)
                    vector.reduce_max(out=sacc[:, 1:2], in_=UO, axis=X,
                                      apply_absolute_value=True)
                    if i == 0:
                        vector.tensor_copy(out=sacc[:, 2:3], in_=sacc[:, 1:2])
                    else:
                        vector.tensor_tensor(out=sacc[:, 2:3], in0=sacc[:, 2:3],
                                             in1=sacc[:, 1:2], op=ALU.max)
                vg_c.append(C.cnt['vg'])

                @block.gpsimd
                def _(gpsimd):
                    gpsimd.wait_ge(C.s('vg'), vg_c[i])
                    C.dma(gpsimd, z_u[i * 128:(i + 1) * 128, :], UO, 'dg')
                dg_c.append(C.cnt['dg'])

            @block.sync
            def _(sync):
                C.dma(sync, spad[:], ltmask[:, 0:16], 'ds')

            @block.vector
            def _(vector):
                vector = C.gv(vector)
                tree_max(vector, 2, 12)
                vector.tensor_copy(out=sval[0:1, 13:14], in_=sval[0:1, 12:13])

            emit_allreduce()

            def scales_o(vector):
                vector.tensor_scalar(out=sval[0:1, 8:9], in0=sval[0:1, 10:11],
                                     scalar1=INV32767, scalar2=EPS_Q,
                                     op0=ALU.mult, op1=ALU.max)
                vector.reciprocal(out=sval[0:1, 7:8], in_=sval[0:1, 8:9])
                vector.tensor_single_scalar(out=sval[0:1, 8:9], in_=sval[0:1, 8:9],
                                            scalar=0.5, op=ALU.mult)
            emit_scales(scales_o, 9)

            @block.sync
            def _(sync):
                C.w(sync, 'tc', 'vg', 'dg')
                for i in range(8):
                    C.dma(sync, wa.rearrange("p (a c) -> p a c", a=NI)[:, i, :],
                          z_wdT[i], 'ds')
            ds_wd = C.cnt['ds']

            vg_dn = []
            vg_oq = []
            tc_dn = []
            ds_dn = []
            ds_os = []
            for g in range(8):
                p = g % 2
                banks = pb[0:4] if p == 0 else pb[4:8]
                oarea = a8[:, p * 6144:p * 6144 + 6144]
                xres = stg[:, p * 2048:(p + 1) * 2048]
                uarea = af[:, 4096 + p * 6144:4096 + (p + 1) * 6144]
                outt = af[:, p * 2048:(p + 1) * 2048]

                @block.sync
                def _(sync):
                    if g == 0:
                        C.w(sync, 'dg', 'vg', 'sc')
                    if g >= 2:
                        sync.wait_ge(C.s('vg'), vg_oq[g - 2])
                        sync.wait_ge(C.s('vg'), vg_dn[g - 2])
                        sync.wait_ge(C.s('dg'), ds_os[g - 2])
                    C.dma(sync, uarea.rearrange("p (a c) -> p a c", a=NI),
                          z_u[:, g * 256:(g + 1) * 256].rearrange("(a p) c -> p a c", p=128),
                          'ds')
                    C.dma(sync, xres[:, 0:1024], src[g * 256:g * 256 + 128, :], 'ds')
                    C.dma(sync, xres[:, 1024:2048], src[g * 256 + 128:(g + 1) * 256, :], 'ds')
                ds_dn.append(C.cnt['ds'])
                ds_dm = ds_dn[g] + (48 if g < 7 else 16)

                @block.vector
                def _(vector):
                    vector = C.gv(vector)
                    vector.wait_ge(C.s('ds'), ds_dm)
                    if g >= 2:
                        vector.wait_ge(C.s('tc'), tc_dn[g - 2])
                    vector.tensor_scalar(out=uarea, in0=uarea, scalar1=sscale[:, 7:8],
                                         scalar2=MAGIC, op0=ALU.mult, op1=ALU.add)
                    vector.tensor_scalar(out=oarea, in0=uarea, scalar1=MAGIC,
                                         scalar2=sscale[:, 8:9],
                                         op0=ALU.subtract, op1=ALU.mult)
                vg_oq.append(C.cnt['vg'])

                @block.tensor
                def _(tensor):
                    tensor.wait_ge(C.s('vg'), vg_oq[g])
                    tensor.wait_ge(C.s('ds'), ds_wd)
                    if g >= 2:
                        tensor.wait_ge(C.s('vg'), vg_dn[g - 2])
                    for i in range(NI):
                        for tt in range(2):
                            for n2 in range(2):
                                i2 = tensor.matmul(
                                    out=banks[tt * 2 + n2][:, :],
                                    lhsT=oarea.rearrange("p (a c) -> p a c", a=NI)[:, i, tt * 128:(tt + 1) * 128],
                                    rhs=wa.rearrange("p (a c) -> p a c", a=NI)[:, i, n2 * 512:(n2 + 1) * 512],
                                    start=(i == 0), stop=(i == NI - 1))
                    C.inc(i2, 'tc')
                tc_dn.append(C.cnt['tc'])

                @block.vector
                def _(vector):
                    vector = C.gv(vector)
                    vector.wait_ge(C.s('tc'), tc_dn[g])
                    for tt in range(2):
                        for n2 in range(2):
                            vector.scalar_tensor_tensor(
                                out=outt[:, tt * 1024 + n2 * 512:tt * 1024 + (n2 + 1) * 512],
                                in0=banks[tt * 2 + n2][:, :], scalar=1.0,
                                in1=xres[:, tt * 1024 + n2 * 512:tt * 1024 + (n2 + 1) * 512],
                                op0=ALU.mult, op1=ALU.add)
                vg_dn.append(C.cnt['vg'])

                @block.gpsimd
                def _(gpsimd):
                    gpsimd.wait_ge(C.s('vg'), vg_dn[g])
                    C.dma(gpsimd, dst[g * 256:g * 256 + 128, :], outt[:, 0:1024], 'dg')
                    C.dma(gpsimd, dst[g * 256 + 128:(g + 1) * 256, :], outt[:, 1024:2048], 'dg')
                ds_os.append(C.cnt['dg'])

            @block.sync
            def _(sync):
                C.dma(sync, spad[:], ltmask[:, 0:16], 'ds')

        def emit_attention():
            emit_ln(z_x1)
            emit_tiles_to_a8(quant=False)

            wa_dkv = wa[:, 0:2048]
            wa_uk = wa[:, 2048:4096]
            wa_uv = wa[:, 4096:6144]
            wa_kr = wa[:, 6144:6400]
            wa_ckv = wa[:, 6400:10496]
            wa_wo = wa[:, 10496:18688]
            wa_st = wa[:, 18688:21760]

            @block.gpsimd
            def _(gpsimd):
                C.w(gpsimd, 'tc', 'vg')
                C.dma(gpsimd, wa_dkv.rearrange("p (a c) -> p a c", a=ND),
                      wdkv.rearrange("(a p) c -> p a c", p=128), 'dg')
                C.dma(gpsimd, wa_uk.rearrange("p (a c) -> p a c", a=2),
                      wuk.rearrange("(a p) c -> p a c", p=128), 'dg')
                C.dma(gpsimd, wa_uv.rearrange("p (a c) -> p a c", a=2),
                      wuv.rearrange("(a p) c -> p a c", p=128), 'dg')
                C.dma(gpsimd, wa_kr.rearrange("p (a c) -> p a c", a=ND),
                      wkr.rearrange("(a p) c -> p a c", p=128), 'dg')
                C.dma(gpsimd, wa_wo.rearrange("p (a c) -> p a c", a=ND),
                      wo.rearrange("(a p) c -> p a c", p=128), 'dg')
            dg_w = C.cnt['dg']

            for m2 in range(2):
                @block.tensor
                def _(tensor):
                    tensor.wait_ge(C.s('dg'), dg_w)
                    tensor.wait_ge(C.s('vg'), C.cnt['vg'])
                    for kk in range(ND):
                        for n in range(4):
                            i2 = tensor.matmul(
                                out=pb[n][:, :],
                                lhsT=wa_dkv.rearrange("p (a c) -> p a c", a=ND)[:, kk, m2 * 128:(m2 + 1) * 128],
                                rhs=a8.rearrange("p (j c) -> p j c", j=ND)[:, kk, n * 512:(n + 1) * 512],
                                start=(kk == 0), stop=(kk == ND - 1))
                    C.inc(i2, 'tc')
                tc_h = C.cnt['tc']

                @block.vector
                def _(vector):
                    vector = C.gv(vector)
                    vector.wait_ge(C.s('tc'), tc_h)
                    for n in range(4):
                        vector.tensor_copy(
                            out=wa_ckv[:, m2 * 2048 + n * 512:m2 * 2048 + (n + 1) * 512],
                            in_=pb[n][:, :])

            for n in range(4):
                @block.tensor
                def _(tensor):
                    tensor.wait_ge(C.s('vg'), C.cnt['vg'])
                    for kk in range(ND):
                        i2 = tensor.matmul(
                            out=pb[0][0:32, :],
                            lhsT=wa_kr.rearrange("p (a c) -> p a c", a=ND)[:, kk, :],
                            rhs=a8.rearrange("p (j c) -> p j c", j=ND)[:, kk, n * 512:(n + 1) * 512],
                            start=(kk == 0), stop=(kk == ND - 1))
                    C.inc(i2, 'tc')
                tc_h = C.cnt['tc']

                @block.vector
                def _(vector):
                    vector = C.gv(vector)
                    vector.wait_ge(C.s('tc'), tc_h)
                    vector.tensor_tensor(out=skr[0:32, n * 512:(n + 1) * 512],
                                         in0=pb[0][0:32, :],
                                         in1=c_rope[0:32, n * 512:(n + 1) * 512],
                                         op=ALU.mult)

            vg_v = []
            tc_v = []
            ds_v = []
            for t in range(NT):
                p = t % 2
                slot = sbq[:, p * 1024:(p + 1) * 1024]

                @block.tensor
                def _(tensor):
                    if t >= 2:
                        tensor.wait_ge(C.s('vg'), vg_v[t - 2])
                    for k2 in range(2):
                        for n2 in range(2):
                            i2 = tensor.matmul(
                                out=pb[4 + p * 2 + n2][:, :],
                                lhsT=wa_ckv[:, k2 * 2048 + t * 128:k2 * 2048 + (t + 1) * 128],
                                rhs=wa_uv[:, k2 * 1024 + n2 * 512:k2 * 1024 + (n2 + 1) * 512],
                                start=(k2 == 0), stop=(k2 == 1))
                    C.inc(i2, 'tc')
                tc_v.append(C.cnt['tc'])

                @block.vector
                def _(vector):
                    vector = C.gv(vector)
                    vector.wait_ge(C.s('tc'), tc_v[t])
                    if t >= 2:
                        vector.wait_ge(C.s('ds'), ds_v[t - 2])
                    vector.tensor_copy(out=slot[:, 0:512], in_=pb[4 + p * 2][:, :])
                    vector.tensor_copy(out=slot[:, 512:1024], in_=pb[4 + p * 2 + 1][:, :])
                vg_v.append(C.cnt['vg'])

                @block.sync
                def _(sync):
                    sync.wait_ge(C.s('vg'), vg_v[t])
                    C.dma(sync, z_v[t * 128:(t + 1) * 128, :], slot, 'ds')
                ds_v.append(C.cnt['ds'])

            vg_k = []
            ds_k = []
            for h in range(HN):
                p = h % 2
                ktile = skh[p]

                @block.vector
                def _(vector):
                    vector = C.gv(vector)
                    if h >= 2:
                        vector.wait_ge(C.s('ds'), ds_k[h - 2])
                    vector.tensor_copy(out=ktile[64:96, :], in_=skr[0:32, :])
                vg_kr = C.cnt['vg']

                @block.tensor
                def _(tensor):
                    tensor.wait_ge(C.s('vg'), C.cnt['vg'] if h == 0 else vg_k[h - 1])
                    for n in range(4):
                        for k2 in range(2):
                            i2 = tensor.matmul(
                                out=pb[n][0:64, :],
                                lhsT=wa_uk[:, k2 * 1024 + h * 64:k2 * 1024 + (h + 1) * 64],
                                rhs=wa_ckv[:, k2 * 2048 + n * 512:k2 * 2048 + (n + 1) * 512],
                                start=(k2 == 0), stop=(k2 == 1))
                        if n == 3:
                            C.inc(i2, 'tc')
                tc_k = C.cnt['tc']

                @block.vector
                def _(vector):
                    vector = C.gv(vector)
                    vector.wait_ge(C.s('tc'), tc_k)
                    for n in range(4):
                        vector.tensor_copy(out=ktile[0:64, n * 512:(n + 1) * 512],
                                           in_=pb[n][0:64, :])
                vg_k.append(C.cnt['vg'])

                @block.sync
                def _(sync):
                    sync.wait_ge(C.s('vg'), vg_k[h])
                    C.dma(sync, z_kT[h * HD:(h + 1) * HD, :], ktile[0:HD, :], 'ds')
                ds_k.append(C.cnt['ds'])

            @block.sync
            def _(sync):
                C.w(sync, 'vg', 'tc', 'sc', 'ds')

            dg_we = []
            tc_we = []
            for n3 in range(3):
                for kp in range(32):
                    p = kp % 2
                    lslot = wa_st[:, p * 1024:(p + 1) * 1024]
                    rslot = wa_st[:, 2048 + p * 512:2048 + (p + 1) * 512]

                    @block.gpsimd
                    def _(gpsimd):
                        if len(tc_we) >= 2:
                            gpsimd.wait_ge(C.s('tc'), tc_we[-2])
                        C.dma(gpsimd, lslot.rearrange("p (a c) -> p a c", a=ND),
                              wdqT[kp * 128:(kp + 1) * 128, :].rearrange("p (a c) -> p a c", a=ND),
                              'dg')
                        C.dma(gpsimd, rslot,
                              wcatq[kp * 128:(kp + 1) * 128, n3 * 512:(n3 + 1) * 512],
                              'dg')
                    dg_we.append(C.cnt['dg'])

                    @block.tensor
                    def _(tensor):
                        tensor.wait_ge(C.s('dg'), dg_we[-1])
                        if kp == 0:
                            tensor.wait_ge(C.s('vg'), C.cnt['vg'])
                        for m in range(8):
                            i2 = tensor.matmul(
                                out=pb[m][:, :],
                                lhsT=lslot.rearrange("p (a c) -> p a c", a=ND)[:, m, :],
                                rhs=rslot,
                                start=(kp == 0), stop=(kp == 31))
                        C.inc(i2, 'tc')
                    tc_we.append(C.cnt['tc'])

                @block.vector
                def _(vector):
                    vector = C.gv(vector)
                    vector.wait_ge(C.s('tc'), tc_we[-1])
                    for m in range(8):
                        vector.tensor_copy(
                            out=weff[:, m * NQ + n3 * 512:m * NQ + (n3 + 1) * 512],
                            in_=pb[m][:, :])

            vg_q = []
            ds_q = []
            for half in range(2):
                for n in range(4):
                    @block.tensor
                    def _(tensor):
                        tensor.wait_ge(C.s('vg'), C.cnt['vg'])
                        for kk in range(ND):
                            for h8 in range(8):
                                i2 = tensor.matmul(
                                    out=pb[h8][0:HD, :],
                                    lhsT=weff[:, kk * NQ + (half * 8 + h8) * HD:kk * NQ + (half * 8 + h8 + 1) * HD],
                                    rhs=a8.rearrange("p (j c) -> p j c", j=ND)[:, kk, n * 512:(n + 1) * 512],
                                    start=(kk == 0), stop=(kk == ND - 1))
                        C.inc(i2, 'tc')
                    tc_q = C.cnt['tc']

                    for h8 in range(8):
                        p2 = h8 % 2
                        slot = sbq[:, p2 * 1024:p2 * 1024 + 512]

                        @block.vector
                        def _(vector):
                            vector = C.gv(vector)
                            vector.wait_ge(C.s('tc'), tc_q)
                            if len(ds_q) >= 2:
                                vector.wait_ge(C.s('ds'), ds_q[-2])
                            vector.tensor_copy(out=slot[0:64, :], in_=pb[h8][0:64, :])
                            vector.tensor_tensor(out=slot[64:96, :],
                                                 in0=pb[h8][64:96, :],
                                                 in1=c_rope[64:96, n * 512:(n + 1) * 512],
                                                 op=ALU.mult)
                        vg_q.append(C.cnt['vg'])

                        @block.sync
                        def _(sync):
                            sync.wait_ge(C.s('vg'), vg_q[-1])
                            C.dma(sync, z_qT[(half * 8 + h8) * HD:(half * 8 + h8 + 1) * HD,
                                             n * 512:(n + 1) * 512],
                                  slot[0:HD, :], 'ds')
                        ds_q.append(C.cnt['ds'])

            ds_hd = []
            tc_end = []
            vg_hd = []
            for h in range(HN):
                hp = h % 2
                ybase = 0 if hp == 0 else 64
                dbase = 96 if hp == 0 else 32

                @block.sync
                def _(sync):
                    if h == 0:
                        C.w(sync, 'vg')
                    if h >= 2:
                        sync.wait_ge(C.s('tc'), tc_end[h - 2])
                    C.dma(sync, sqh[hp][0:HD, :], z_qT[h * HD:(h + 1) * HD, :], 'ds')
                    C.dma(sync, skh[hp][0:HD, :], z_kT[h * HD:(h + 1) * HD, :], 'ds')
                    C.dma(sync, svh[hp].rearrange("p (a c) -> p a c", a=NT),
                          z_v[:, h * HS:(h + 1) * HS].rearrange("(a p) c -> p a c", p=128),
                          'ds')
                ds_hd.append(C.cnt['ds'])

                steps = []
                for ki in range(NT):
                    nq = T - 128 * ki
                    for qc in range((nq + 511) // 512):
                        w_ = min(512, nq - qc * 512)
                        steps.append((ki, qc, w_))
                tc0 = C.cnt['tc']
                sc0 = C.cnt['sc']

                @block.tensor
                def _(tensor):
                    tensor.wait_ge(C.s('ds'), ds_hd[h])
                    for si, (ki, qc, w_) in enumerate(steps):
                        if si >= 2:
                            tensor.wait_ge(C.s('sc'), sc0 + si - 1)
                        else:
                            tensor.wait_ge(C.s('sc'), sc0)
                        i2 = tensor.matmul(
                            out=pb[si % 2][:, 0:w_],
                            lhsT=skh[hp][0:HD, ki * 128:(ki + 1) * 128],
                            rhs=sqh[hp][0:HD, 128 * ki + qc * 512:128 * ki + qc * 512 + w_],
                            start=True, stop=True)
                        C.inc(i2, 'tc')

                @block.scalar
                def _(scalar):
                    for si, (ki, qc, w_) in enumerate(steps):
                        scalar.wait_ge(C.s('tc'), tc0 + si + 1)
                        i2 = scalar.activation(
                            out=pTa[:, PT_OFF[ki] + qc * 512:PT_OFF[ki] + qc * 512 + w_],
                            in_=pb[si % 2][:, 0:w_],
                            func=AF.Exp, bias=0.0, scale=ISQ96)
                        C.inc(i2, 'sc')
                sc_exp = C.cnt['sc']

                si_of = {}
                for si, (ki, qc, w_) in enumerate(steps):
                    if qc == 0:
                        si_of[ki] = si

                @block.vector
                def _(vector):
                    vector = C.gv(vector)
                    for ki in range(NT):
                        vector.wait_ge(C.s('sc'), sc0 + si_of[ki] + 1)
                        vector.tensor_tensor(
                            out=pTa[:, PT_OFF[ki]:PT_OFF[ki] + 128],
                            in0=pTa[:, PT_OFF[ki]:PT_OFF[ki] + 128],
                            in1=c_lt[:, :], op=ALU.mult)
                vg_mask = C.cnt['vg']

                @block.tensor
                def _(tensor):
                    tensor.wait_ge(C.s('vg'), vg_mask)
                    for n in range(4):
                        for ki in range(4 * n + 4):
                            if 128 * ki <= 512 * n:
                                rhs = pTa[:, PT_OFF[ki] + 512 * n - 128 * ki:
                                          PT_OFF[ki] + 512 * n - 128 * ki + 512]
                                c0, c1 = 0, 512
                                st_ = (ki == 0)
                            else:
                                off = 128 * ki - 512 * n
                                rhs = pTa[:, PT_OFF[ki]:PT_OFF[ki] + 512 - off]
                                c0, c1 = off, 512
                                st_ = False
                            tensor.matmul(out=pb[2 + n][ybase:ybase + 64, c0:c1],
                                          lhsT=svh[hp][:, ki * 64:(ki + 1) * 64],
                                          rhs=rhs, start=st_, stop=(ki == 4 * n + 3))
                            i2 = tensor.matmul(out=pb[2 + n][dbase:dbase + 32, c0:c1],
                                               lhsT=c_ones32[:, :],
                                               rhs=rhs, start=st_, stop=(ki == 4 * n + 3),
                                               tile_position=(0, dbase))
                    C.inc(i2, 'tc')
                tc_pv = C.cnt['tc']

                @block.vector
                def _(vector):
                    vector = C.gv(vector)
                    vector.wait_ge(C.s('tc'), tc_pv)
                    for n in range(4):
                        vector.tensor_copy(out=sd1[32 * n:32 * n + 1, 0:512],
                                           in_=pb[2 + n][dbase:dbase + 1, :])
                    vector.reciprocal(out=sd1[:, 512:1024], in_=sd1[:, 0:512])
                vg_rd = C.cnt['vg']

                tc_k1 = []

                @block.tensor
                def _(tensor):
                    tensor.wait_ge(C.s('vg'), vg_rd)
                    for n in range(4):
                        bank = pb[6] if n < 2 else pb[7]
                        rb = (n % 2) * 64
                        i2 = tensor.matmul(out=bank[rb:rb + 64, :],
                                           lhsT=c_onesf[32 * n:32 * n + 1, :],
                                           rhs=sd1[32 * n:32 * n + 1, 512:1024],
                                           start=True, stop=True,
                                           tile_position=(32 * n, rb))
                        C.inc(i2, 'tc')
                        tc_k1.append(C.cnt['tc'])

                @block.vector
                def _(vector):
                    vector = C.gv(vector)
                    for n in range(4):
                        bank = pb[6] if n < 2 else pb[7]
                        rb = (n % 2) * 64
                        stage = sd1[ybase:ybase + 64,
                                    1024 + (n % 2) * 512:1536 + (n % 2) * 512]
                        vector.wait_ge(C.s('tc'), tc_k1[n])
                        vector.tensor_copy(out=stage, in_=bank[rb:rb + 64, :])
                        vector.tensor_tensor(
                            out=a8.rearrange("p (j c) -> p j c", j=ND)[:, h // 2, n * 512:(n + 1) * 512][ybase:ybase + 64, :],
                            in0=pb[2 + n][ybase:ybase + 64, :],
                            in1=stage,
                            op=ALU.mult)
                vg_hd.append(C.cnt['vg'])
                tc_end.append(C.cnt['tc'])

            if dbg:
                vg_now = C.cnt['vg']

                @block.sync
                def _(sync):
                    sync.wait_ge(C.s('vg'), vg_now)
                    C.dma(sync, d_a8, a8[:, :], 'ds')

            tc_wo = []
            vg_wo = []
            ds_wo = []
            ds_xo = []
            for t in range(NT):
                p = t % 2
                banks = pb[0:2] if p == 0 else pb[2:4]
                xres = stg[:, p * 1024:(p + 1) * 1024]
                outt = sd1[:, p * 1024:(p + 1) * 1024]

                @block.sync
                def _(sync):
                    if t == 0:
                        C.w(sync, 'vg', 'dg')
                    if t >= 2:
                        sync.wait_ge(C.s('vg'), vg_wo[t - 2])
                        sync.wait_ge(C.s('dg'), ds_xo[t - 2])
                    C.dma(sync, xres, z_x1[t * 128:(t + 1) * 128, :], 'ds')
                ds_wo.append(C.cnt['ds'])

                @block.tensor
                def _(tensor):
                    if t == 0:
                        tensor.wait_ge(C.s('vg'), C.cnt['vg'])
                        tensor.wait_ge(C.s('dg'), dg_w)
                    if t >= 2:
                        tensor.wait_ge(C.s('vg'), vg_wo[t - 2])
                    for kk in range(ND):
                        for n2 in range(2):
                            i2 = tensor.matmul(
                                out=banks[n2][:, :],
                                lhsT=a8.rearrange("p (j c) -> p j c", j=ND)[:, kk, t * 128:(t + 1) * 128],
                                rhs=wa_wo[:, kk * 1024 + n2 * 512:kk * 1024 + (n2 + 1) * 512],
                                start=(kk == 0), stop=(kk == ND - 1))
                    C.inc(i2, 'tc')
                tc_wo.append(C.cnt['tc'])

                @block.vector
                def _(vector):
                    vector = C.gv(vector)
                    vector.wait_ge(C.s('tc'), tc_wo[t])
                    vector.wait_ge(C.s('ds'), ds_wo[t])
                    for n2 in range(2):
                        vector.scalar_tensor_tensor(
                            out=outt[:, n2 * 512:(n2 + 1) * 512],
                            in0=banks[n2][:, :], scalar=1.0,
                            in1=xres[:, n2 * 512:(n2 + 1) * 512],
                            op0=ALU.mult, op1=ALU.add)
                vg_wo.append(C.cnt['vg'])

                @block.gpsimd
                def _(gpsimd):
                    gpsimd.wait_ge(C.s('vg'), vg_wo[t])
                    C.dma(gpsimd, z_x2[t * 128:(t + 1) * 128, :], outt, 'dg')
                ds_xo.append(C.cnt['dg'])

        emit_ffn(x_in, z_x1)
        emit_attention()
        emit_ffn(z_x2, out, tag=2)

        if dbg:
            @block.sync
            def _(sync):
                C.w(sync, 'vg', 'sc')
                C.dma(sync, d_stat, sstat[:, 0:64], 'ds')

        @block.sync
        def _(sync):
            C.w(sync, 'ds', 'dg')

    return nc


_NC_CACHE = {}


def kernel(L0_attn_x, L0_attn_W_DQ, L0_attn_W_UQ, L0_attn_W_DKV, L0_attn_W_UK,
           L0_attn_W_UV, L0_attn_W_QR, L0_attn_W_KR, L0_attn_W_O,
           L0_ffn_w_gate, L0_ffn_w_up, L0_ffn_w_down):
    f = np.float32
    x = np.ascontiguousarray(np.asarray(L0_attn_x, dtype=f))
    wdqT = np.ascontiguousarray(np.asarray(L0_attn_W_DQ, dtype=f).T)
    wuq = np.asarray(L0_attn_W_UQ, dtype=f)
    wqr = np.asarray(L0_attn_W_QR, dtype=f)
    wcatq = np.ascontiguousarray(
        np.concatenate([wuq.reshape(4096, HN, HS), wqr.reshape(4096, HN, DR)],
                       axis=2).reshape(4096, NQ))
    wdkv = np.ascontiguousarray(np.asarray(L0_attn_W_DKV, dtype=f))
    wuk = np.ascontiguousarray(np.asarray(L0_attn_W_UK, dtype=f))
    wuv = np.ascontiguousarray(np.asarray(L0_attn_W_UV, dtype=f))
    wkr = np.ascontiguousarray(np.asarray(L0_attn_W_KR, dtype=f))
    wo = np.ascontiguousarray(np.asarray(L0_attn_W_O, dtype=f))
    wg = np.ascontiguousarray(np.asarray(L0_ffn_w_gate, dtype=f))
    wu = np.ascontiguousarray(np.asarray(L0_ffn_w_up, dtype=f))
    wd = np.ascontiguousarray(np.asarray(L0_ffn_w_down, dtype=f))

    t = np.arange(T, dtype=f)
    inv = (f(1.0) / (f(10000.0) ** (np.arange(0, DR, 2, dtype=f) / f(DR)))).astype(f)
    freqs = (t[:, None] * inv[None, :]).astype(f)
    emb = np.concatenate([freqs, freqs], axis=-1)
    rope = (np.cos(emb) + np.sin(emb)).astype(f)
    ropeT = np.ascontiguousarray(rope.T)

    ltm = np.zeros((128, 128), dtype=f)
    for kk in range(128):
        ltm[kk, kk:] = 1.0
    onescol = np.zeros((128, 32), dtype=f)
    onescol[:, 0] = 1.0
    onesf = np.ones((128, 64), dtype=f)
    magicv = np.zeros((128, 2), dtype=f); magicv[:, 0] = f(12582912.0); magicv[:, 1] = f(-12582912.0)
    eye = np.eye(128, dtype=f)

    if "nc" not in _NC_CACHE:
        _NC_CACHE["nc"] = build_nc()
    nc = _NC_CACHE["nc"]

    shared = dict(wdqT=wdqT, wcatq=wcatq, wdkv=wdkv, wuk=wuk, wuv=wuv, wkr=wkr,
                  wo=wo, wg=wg, wu=wu, wd=wd, ropeT=ropeT, ltmask=ltm,
                  onescol=onescol, onesf=onesf, magicv=magicv, eye=eye)
    in_maps = [dict(shared, x=np.ascontiguousarray(x[c])) for c in range(8)]
    res = run_bass_kernel_spmd(nc, in_maps, core_ids=list(range(8)))
    return np.stack([np.asarray(res.results[c]["out"]) for c in range(8)], axis=0)
